# version 13
# baseline (speedup 1.0000x reference)
import contextlib
import numpy as np
import concourse.bass as bass
import concourse.mybir as mybir
from concourse.bass_utils import run_bass_kernel_spmd

F32 = mybir.dt.float32
BF16 = mybir.dt.bfloat16
AF = mybir.ActivationFunctionType
ALU = mybir.AluOpType
AX = mybir.AxisListType

T = 2048
D = 1024
KC = 8
NB = 16
NT = 4
NHP = 8
E = 8
HCH = 22
HHALF = 11
EPS = 1e-5
SCALE = 0.125

DO_ATTN = True
DO_FFN = True

ENGS = ("sync", "gpsimd", "tensor", "vector", "scalar")
ENG_SEM = {"tensor": "tPE", "vector": "tVE", "scalar": "tSC",
           "sync": "tSY", "gpsimd": "tGP"}

AO = 0
QHS = 16384
KHS = 18432
VH = (20480, 22528)
PRB = 24576
ATT = 26624
WQB, WQSB, WKB, WKSB = 28672, 29696, 30720, 31744
WVB = 32768
WOB = (32768, 32768)
HIDB = 0
W1B = (28672, 29696)
W2B = (30720, 31744)
W3B = (24576, 25984)
POOLB_N = 35840
PRB_B = 33792

XT = (0, 1024)
XC = 2048
SQ = 3072
XNF = 4096
DIAG = 2048
RT1, RT2 = 0, 512
WBC = (0, 2048)
SIL = (4096, 4608)
H2S = (5120, 5632)
OT = (0, 1024)
ARF_N = 6144

GRAN = {"pp": 512, "stat": 8}


def BK(b, off=0, w=512):
    return (b * 512 + off, b * 512 + off + w)


class Builder:

    def __init__(self):
        self.ops = {e: [] for e in ENGS}
        import collections
        self.cnt = collections.defaultdict(int)
        self.wr = {}
        self.rd = {}
        self.waited = {e: {} for e in ENGS}

    def _slots(self, rng):
        name, lo, hi = rng
        g = GRAN.get(name, 128)
        return [(name, s) for s in range(lo // g, (hi + g - 1) // g)]

    def emit(self, eng, fns, reads=(), writes=(), dma=None, extra=()):
        if not isinstance(fns, (list, tuple)):
            fns = [fns]
        waits = {}

        def need(cap):
            if cap is None:
                return
            s, v = cap
            if v <= 0:
                return
            waits[s] = max(waits.get(s, 0), v)

        rslots = []
        for rng in reads:
            for sl in self._slots(rng):
                rslots.append(sl)
                need(self.wr.get(sl))
                if sl[0] == "pp":
                    for cap, eng2 in self.rd.get(sl, ()):
                        if eng2 != eng:
                            need(cap)
        wslots = []
        for rng in writes:
            for sl in self._slots(rng):
                wslots.append(sl)
                need(self.wr.get(sl))
                for cap, eng2 in self.rd.get(sl, ()):
                    need(cap)
        for cap in extra:
            need(cap)

        wd = self.waited[eng]
        final = []
        for s, v in sorted(waits.items()):
            if wd.get(s, 0) >= v:
                continue
            wd[s] = v
            final.append((s, v))

        sem = dma if dma is not None else ENG_SEM[eng]
        inc_k = 16 if dma is not None else 1
        self.cnt[sem] += inc_k
        cap = (sem, self.cnt[sem])

        for sl in rslots:
            self.rd.setdefault(sl, []).append((cap, eng))
        for sl in wslots:
            self.wr[sl] = cap
            self.rd[sl] = []

        self.ops[eng].append((fns, tuple(final), sem, inc_k))
        return cap


def build_nc():
    nc = bass.Bass("TRN2", target_bir_lowering=False, debug=False)
    dt = nc.dram_tensor
    x_in = dt("x", [T, D], F32, kind="ExternalInput").ap()
    wq_l = dt("wq_l", [NHP, 128, 1024], F32, kind="ExternalInput").ap()
    wqs_l = dt("wqs_l", [NHP, 128, 1024], F32, kind="ExternalInput").ap()
    wk_l = dt("wk_l", [NHP, 128, 1024], F32, kind="ExternalInput").ap()
    wks_l = dt("wks_l", [NHP, 128, 1024], F32, kind="ExternalInput").ap()
    wv_r = dt("wv_r", [NHP, 128, 1024], F32, kind="ExternalInput").ap()
    wo_l = dt("wo_l", [KC, 128, 1024], F32, kind="ExternalInput").ap()
    wg_r = dt("wg_r", [128, 64], F32, kind="ExternalInput").ap()
    w1_l = dt("w1_l", [E, HCH, 128, 1024], F32, kind="ExternalInput").ap()
    w2_l = dt("w2_l", [E, HCH, 128, 1024], F32, kind="ExternalInput").ap()
    w3_l = dt("w3_l", [E, KC, 128, HCH * 128], F32, kind="ExternalInput").ap()
    ct_in = dt("ct", [128, T], F32, kind="ExternalInput").ap()
    st_in = dt("st", [128, T], F32, kind="ExternalInput").ap()
    i128_in = dt("i128", [128, 128], F32, kind="ExternalInput").ap()
    trineg_in = dt("trineg", [128, 128], F32, kind="ExternalInput").ap()
    sel8_in = dt("sel8", [8, 1024], F32, kind="ExternalInput").ap()
    out = dt("out", [T, D], F32, kind="ExternalOutput").ap()

    with contextlib.ExitStack() as ctx:
        en = ctx.enter_context
        acc = en(nc.sbuf_tensor("acc", [128, KC * T], F32))
        xn = en(nc.sbuf_tensor("xn", [128, KC * T], BF16))
        pb = en(nc.sbuf_tensor("poolb", [128, POOLB_N], BF16))
        arf = en(nc.sbuf_tensor("arf", [128, ARF_N], F32))
        csb = en(nc.sbuf_tensor("csb", [128, 2 * T], BF16))
        wgate = en(nc.sbuf_tensor("wgate", [128, 128], F32))
        wrow = en(nc.sbuf_tensor("wrow", [8, T], BF16))
        i128f = en(nc.sbuf_tensor("i128f", [128, 128], F32))
        i128b = en(nc.sbuf_tensor("i128b", [128, 128], BF16))
        trineg = en(nc.sbuf_tensor("trineg_s", [128, 128], F32))
        sel8 = en(nc.sbuf_tensor("sel8_s", [8, 1024], BF16))
        wgr = en(nc.sbuf_tensor("wgr", [128, 64], F32))
        dgb = en(nc.sbuf_tensor("dgb", [128, 256], BF16))
        stat = en(nc.sbuf_tensor("stat", [128, 64], F32))
        pp = en(nc.psum_tensor("pp", [128, 4096], F32))
        sem_names = ("tPE", "tVE", "tSC", "tSY", "tGP",
                     "cI128F", "cTRI", "cWGR", "cCT", "cST", "cI128B", "cSEL",
                     "dx0", "dx1", "wqS", "wqsS", "wkS", "wksS", "wvS", "woS",
                     "w10", "w11", "w20", "w21", "w30", "w31", "do0", "do1")
        sems = {n: en(nc.semaphore(n)) for n in sem_names}
        block = en(nc.Block())

        B = Builder()
        em = B.emit

        def arfs(off, w):
            return arf[:, off:off + w]

        def pbs(off, w):
            return pb[:, off:off + w]

        def pps(lo_hi):
            lo, hi = lo_hi
            return pp[:, lo:hi]

        em("sync", lambda e: e.dma_start(i128f[:], i128_in),
           writes=[("i128f", 0, 128)], dma="cI128F")
        em("sync", lambda e: e.dma_start(trineg[:], trineg_in),
           writes=[("trineg", 0, 128)], dma="cTRI")
        em("sync", lambda e: e.dma_start(wgr[:], wg_r),
           writes=[("wgr", 0, 64)], dma="cWGR")
        em("gpsimd", lambda e: e.dma_start(csb[:, 0:T], ct_in),
           writes=[("csb", 0, T)], dma="cCT")
        em("gpsimd", lambda e: e.dma_start(csb[:, T:2 * T], st_in),
           writes=[("csb", T, 2 * T)], dma="cST")
        em("gpsimd", lambda e: e.dma_start(i128b[:], i128_in),
           writes=[("i128b", 0, 128)], dma="cI128B")
        em("gpsimd", lambda e: e.dma_start(sel8[:], sel8_in),
           writes=[("sel8", 0, 1024)], dma="cSEL")

        def emit_ln_block(b):
            bf = b % 2
            xt0 = XT[bf]
            sb = bf * 32
            em("sync", (lambda bb, o: lambda e: e.dma_start(
                arfs(o, 1024), x_in[bb * 128:(bb + 1) * 128, :]))(b, xt0),
               writes=[("arf", xt0, xt0 + 1024)], dma="dx%d" % bf)
            em("vector", (lambda o, ss: lambda e: e.reduce_sum(
                stat[:, ss:ss + 1], arfs(o, 1024), axis=AX.X))(xt0, sb),
               reads=[("arf", xt0, xt0 + 1024)], writes=[("stat", sb, sb + 1)])
            em("vector", (lambda ss: lambda e: e.tensor_scalar(
                stat[:, ss + 1:ss + 2], stat[:, ss:ss + 1], -1.0 / D, None, ALU.mult))(sb),
               reads=[("stat", sb, sb + 1)], writes=[("stat", sb + 1, sb + 2)])
            em("vector", (lambda o, ss: lambda e: e.tensor_scalar(
                arfs(XC, 1024), arfs(o, 1024), stat[:, ss + 1:ss + 2], None, ALU.add))(xt0, sb),
               reads=[("arf", xt0, xt0 + 1024), ("stat", sb + 1, sb + 2)],
               writes=[("arf", XC, XC + 1024)])
            em("scalar", (lambda ss: lambda e: e.activation(
                arfs(SQ, 1024), arfs(XC, 1024), AF.Square,
                accum_out=stat[:, ss + 2:ss + 3]))(sb),
               reads=[("arf", XC, XC + 1024)],
               writes=[("arf", SQ, SQ + 1024), ("stat", sb + 2, sb + 3)])
            em("vector", (lambda ss: lambda e: e.tensor_scalar(
                stat[:, ss + 3:ss + 4], stat[:, ss + 2:ss + 3], 1.0 / D, EPS,
                ALU.mult, ALU.add))(sb),
               reads=[("stat", sb + 2, sb + 3)], writes=[("stat", sb + 3, sb + 4)])
            em("scalar", (lambda ss: lambda e: e.activation(
                stat[:, ss + 4:ss + 5], stat[:, ss + 3:ss + 4], AF.Sqrt))(sb),
               reads=[("stat", sb + 3, sb + 4)], writes=[("stat", sb + 4, sb + 5)])
            em("vector", (lambda ss: lambda e: e.reciprocal(
                stat[:, ss + 5:ss + 6], stat[:, ss + 4:ss + 5]))(sb),
               reads=[("stat", sb + 4, sb + 5)], writes=[("stat", sb + 5, sb + 6)])
            em("vector", (lambda ss: lambda e: e.tensor_scalar(
                arfs(XC, 1024), arfs(XC, 1024), stat[:, ss + 5:ss + 6], None, ALU.mult))(sb),
               reads=[("arf", XC, XC + 1024), ("stat", sb + 5, sb + 6)],
               writes=[("arf", XC, XC + 1024)])
            em("tensor", [(lambda kkc: lambda e: e.transpose(
                pp[:, 2048 + kkc * 128:2048 + (kkc + 1) * 128],
                arf[:, XC + kkc * 128:XC + (kkc + 1) * 128], i128f[:]))(kc)
                for kc in range(KC)],
               reads=[("arf", XC, XC + 1024), ("i128f", 0, 128)],
               writes=[("pp", 2048, 3072)])
            em("scalar", (lambda bb: lambda e: e.copy(
                xn[:].rearrange("p (k t) -> p k t", k=KC)[:, :, bb * 128:(bb + 1) * 128],
                pp[:, 2048:3072].rearrange("p (k t) -> p k t", k=KC)))(b),
               reads=[("pp", 2048, 3072)],
               writes=[("xn", kc * T + b * 128, kc * T + (b + 1) * 128) for kc in range(KC)])
            em("vector", lambda e: e.tensor_copy(arfs(XNF, 1024), pp[:, 2048:3072]),
               reads=[("pp", 2048, 3072)], writes=[("arf", XNF, XNF + 1024)])
            em("tensor", [(lambda kkc, fi, la: lambda e: e.matmul(
                pp[:, 2048:2056], arf[:, XNF + kkc * 128:XNF + (kkc + 1) * 128],
                wgr[:, kkc * 8:(kkc + 1) * 8], start=fi, stop=la))(kc, kc == 0, kc == KC - 1)
                for kc in range(KC)],
               reads=[("arf", XNF, XNF + 1024), ("wgr", 0, 64)],
               writes=[("pp", 2048, 2560)])
            em("tensor", [(lambda o, kkc: lambda e: e.transpose(
                pp[:, 3072 + kkc * 128:3072 + (kkc + 1) * 128],
                arf[:, o + kkc * 128:o + (kkc + 1) * 128], i128f[:]))(xt0, kc)
                for kc in range(KC)],
               reads=[("arf", xt0, xt0 + 1024), ("i128f", 0, 128)],
               writes=[("pp", 3072, 4096)])
            em("vector", (lambda bb: lambda e: e.tensor_copy(
                acc[:].rearrange("p (k t) -> p k t", k=KC)[:, :, bb * 128:(bb + 1) * 128],
                pp[:, 3072:4096].rearrange("p (k t) -> p k t", k=KC)))(b),
               reads=[("pp", 3072, 4096)],
               writes=[("acc", kc * T + b * 128, kc * T + (b + 1) * 128) for kc in range(KC)])
            gread = [("pp", 2048, 2560)]
            em("vector", (lambda ss: lambda e: e.reduce_max(
                stat[:, ss + 6:ss + 7], pp[:, 2048:2056], axis=AX.X))(sb),
               reads=gread, writes=[("stat", sb + 6, sb + 7)])
            em("vector", (lambda ss: lambda e: e.tensor_scalar(
                stat[:, ss + 7:ss + 8], stat[:, ss + 6:ss + 7], -1.0, None, ALU.mult))(sb),
               reads=[("stat", sb + 6, sb + 7)], writes=[("stat", sb + 7, sb + 8)])
            em("vector", (lambda ss: lambda e: e.tensor_scalar(
                stat[:, ss + 8:ss + 16], pp[:, 2048:2056], stat[:, ss + 6:ss + 7], None,
                ALU.is_ge))(sb),
               reads=gread + [("stat", sb + 6, sb + 7)], writes=[("stat", sb + 8, sb + 16)])
            em("vector", (lambda ss: lambda e: e.scalar_tensor_tensor(
                stat[:, ss + 16:ss + 24], stat[:, ss + 8:ss + 16], -1e30, pp[:, 2048:2056],
                ALU.mult, ALU.add))(sb),
               reads=gread + [("stat", sb + 8, sb + 16)], writes=[("stat", sb + 16, sb + 24)])
            em("vector", (lambda ss: lambda e: e.reduce_max(
                stat[:, ss + 24:ss + 25], stat[:, ss + 16:ss + 24], axis=AX.X))(sb),
               reads=[("stat", sb + 16, sb + 24)], writes=[("stat", sb + 24, sb + 25)])
            em("scalar", (lambda ss: lambda e: e.activation(
                stat[:, ss + 25:ss + 26], stat[:, ss + 24:ss + 25], AF.Exp,
                bias=stat[:, ss + 7:ss + 8]))(sb),
               reads=[("stat", sb + 24, sb + 25), ("stat", sb + 7, sb + 8)],
               writes=[("stat", sb + 25, sb + 26)])
            em("vector", (lambda ss: lambda e: e.tensor_scalar(
                stat[:, ss + 26:ss + 27], stat[:, ss + 25:ss + 26], 1.0, None, ALU.add))(sb),
               reads=[("stat", sb + 25, sb + 26)], writes=[("stat", sb + 26, sb + 27)])
            em("vector", (lambda ss: lambda e: e.reciprocal(
                stat[:, ss + 27:ss + 28], stat[:, ss + 26:ss + 27]))(sb),
               reads=[("stat", sb + 26, sb + 27)], writes=[("stat", sb + 27, sb + 28)])
            em("scalar", (lambda ss: lambda e: e.activation(
                stat[:, ss + 16:ss + 24], pp[:, 2048:2056], AF.Exp,
                bias=stat[:, ss + 7:ss + 8]))(sb),
               reads=gread + [("stat", sb + 7, sb + 8)], writes=[("stat", sb + 16, sb + 24)])
            em("vector", (lambda ss: lambda e: e.tensor_scalar(
                stat[:, ss + 8:ss + 16], pp[:, 2048:2056], stat[:, ss + 24:ss + 25], None,
                ALU.is_ge))(sb),
               reads=gread + [("stat", sb + 24, sb + 25)], writes=[("stat", sb + 8, sb + 16)])
            em("vector", (lambda ss: lambda e: e.tensor_tensor(
                stat[:, ss + 8:ss + 16], stat[:, ss + 16:ss + 24],
                stat[:, ss + 8:ss + 16], ALU.mult))(sb),
               reads=[("stat", sb + 8, sb + 24)], writes=[("stat", sb + 8, sb + 16)])
            em("vector", (lambda bb, ss: lambda e: e.tensor_scalar(
                wgate[:, bb * 8:(bb + 1) * 8], stat[:, ss + 8:ss + 16],
                stat[:, ss + 27:ss + 28], None, ALU.mult))(b, sb),
               reads=[("stat", sb + 8, sb + 16), ("stat", sb + 27, sb + 28)],
               writes=[("wgate", b * 8, (b + 1) * 8)])

        def emit_hp_loads(hp):
            for buf, warr, wsem in ((WQB, wq_l, "wqS"), (WQSB, wqs_l, "wqsS"),
                                    (WKB, wk_l, "wkS"), (WKSB, wks_l, "wksS"),
                                    (WVB, wv_r, "wvS")):
                em("gpsimd", (lambda bb, ss, h: lambda e: e.dma_start(
                    pbs(bb, 1024), ss[h]))(buf, warr, hp),
                   writes=[("pb", buf, buf + 1024)], dma=wsem)


        def emit_hp_proj(hp, tq_only=None):
            for tq in (range(NT) if tq_only is None else [tq_only]):
                c0, c1 = tq * 512, (tq + 1) * 512
                for bank, wbuf in ((0, WQB), (1, WQSB), (2, WKB), (3, WKSB)):
                    em("tensor", [(lambda bk, wb, kkc, fi, la, a0, a1: lambda e: e.matmul(
                        pps(BK(bk)), pb[:, wb + kkc * 128:wb + (kkc + 1) * 128],
                        xn[:, kkc * T + a0:kkc * T + a1],
                        start=fi, stop=la))(bank, wbuf, kc, kc == 0, kc == KC - 1, c0, c1)
                        for kc in range(KC)],
                       reads=[("pb", wbuf, wbuf + 1024)] +
                             [("xn", kc * T + c0, kc * T + c1) for kc in range(KC)],
                       writes=[("pp",) + BK(bank)])
                em("vector", (lambda a0, a1: lambda e: e.tensor_tensor(
                    arfs(RT1, 512), pps(BK(0)), csb[:, a0:a1], ALU.mult))(c0, c1),
                   reads=[("pp",) + BK(0), ("csb", c0, c1)],
                   writes=[("arf", RT1, RT1 + 512)])
                em("vector", (lambda a0, a1: lambda e: e.tensor_tensor(
                    arfs(RT2, 512), pps(BK(1)), csb[:, T + a0:T + a1], ALU.mult))(c0, c1),
                   reads=[("pp",) + BK(1), ("csb", T + c0, T + c1)],
                   writes=[("arf", RT2, RT2 + 512)])
                em("vector", (lambda a0, a1: lambda e: e.tensor_tensor(
                    pb[:, QHS + a0:QHS + a1], arfs(RT1, 512), arfs(RT2, 512),
                    ALU.add))(c0, c1),
                   reads=[("arf", RT1, RT1 + 512), ("arf", RT2, RT2 + 512)],
                   writes=[("pb", QHS + c0, QHS + c1)])
                em("vector", (lambda a0, a1: lambda e: e.tensor_tensor(
                    arfs(RT1, 512), pps(BK(2)), csb[:, a0:a1], ALU.mult))(c0, c1),
                   reads=[("pp",) + BK(2), ("csb", c0, c1)],
                   writes=[("arf", RT1, RT1 + 512)])
                em("vector", (lambda a0, a1: lambda e: e.tensor_tensor(
                    arfs(RT2, 512), pps(BK(3)), csb[:, T + a0:T + a1], ALU.mult))(c0, c1),
                   reads=[("pp",) + BK(3), ("csb", T + c0, T + c1)],
                   writes=[("arf", RT2, RT2 + 512)])
                em("vector", (lambda a0, a1: lambda e: e.tensor_tensor(
                    pb[:, KHS + a0:KHS + a1], arfs(RT1, 512), arfs(RT2, 512),
                    ALU.add))(c0, c1),
                   reads=[("arf", RT1, RT1 + 512), ("arf", RT2, RT2 + 512)],
                   writes=[("pb", KHS + c0, KHS + c1)])


        def emit_hp_vprod(hp):
            vb = VH[hp % 2]
            for g4 in range(4):
                bank = 4 + (g4 % 2)
                fns = []
                for j in range(4):
                    blk = g4 * 4 + j
                    for kc in range(KC):
                        fns.append((lambda bk, jj, bb, kkc, fi, la: lambda e: e.matmul(
                            pp[:, BK(bk)[0] + jj * 128:BK(bk)[0] + (jj + 1) * 128],
                            xn[:, kkc * T + bb * 128:kkc * T + (bb + 1) * 128],
                            pb[:, WVB + kkc * 128:WVB + (kkc + 1) * 128],
                            start=fi, stop=la))(bank, j, blk, kc, kc == 0, kc == KC - 1))
                em("tensor", fns,
                   reads=[("pb", WVB, WVB + 1024)] +
                         [("xn", kc * T + g4 * 512, kc * T + (g4 + 1) * 512) for kc in range(KC)],
                   writes=[("pp",) + BK(bank)])
                em("scalar", (lambda vbb, gg, bk: lambda e: e.copy(
                    pb[:, vbb + gg * 512:vbb + (gg + 1) * 512], pps(BK(bk))))(vb, g4, bank),
                   reads=[("pp",) + BK(bank)],
                   writes=[("pb", vb + g4 * 512, vb + (g4 + 1) * 512)])


        def emit_hp_attn(hp):
            vb = VH[hp % 2]
            def s2_stage(q, a, py):
                prb = PRB if py == 0 else PRB_B
                L = (q + 1) * 128
                ngrp = (q + 4) // 4
                for g in range(ngrp):
                    bank = 4 + (g % 2)
                    kb0 = g * 4
                    kbn = min(4, q + 1 - kb0)
                    em("tensor", [(lambda bk, jj, kk, pr, pyy: lambda e: e.matmul(
                        pp[:, BK(bk)[0] + jj * 128:BK(bk)[0] + (jj + 1) * 128],
                        pb[:, pr + kk * 128:pr + (kk + 1) * 128],
                        dgb[:, pyy * 128:(pyy + 1) * 128],
                        start=True, stop=True))(bank, j, kb0 + j, prb, py)
                        for j in range(kbn)],
                       reads=[("pb", prb + kb0 * 128, prb + (kb0 + kbn) * 128),
                              ("dgb", py * 128, (py + 1) * 128)],
                       writes=[("pp",) + BK(bank)])
                    em("vector", (lambda gg, bk, n: lambda e: e.tensor_copy(
                        pb[:, ATT + gg * 512:ATT + gg * 512 + n * 128],
                        pp[:, BK(bk)[0]:BK(bk)[0] + n * 128]))(g, bank, kbn),
                       reads=[("pp",) + BK(bank)],
                       writes=[("pb", ATT + kb0 * 128, ATT + (kb0 + kbn) * 128)])
                reg = 3072 + (q % 2) * 128
                fns = []
                for kb in range(q + 1):
                    tp = (0, 64) if a == 1 else None
                    fns.append((lambda ax, kk, rr, fi, la, tpp, vbb: lambda e: e.matmul(
                        pp[64 * ax:64 * ax + 64, rr:rr + 128],
                        pb[:, vbb + kk * 128 + 64 * ax:vbb + kk * 128 + 64 * ax + 64],
                        pb[:, ATT + kk * 128:ATT + (kk + 1) * 128],
                        start=fi, stop=la, tile_position=tpp))(
                            a, kb, reg, kb == 0, kb == q, tp, vb))
                em("tensor", fns,
                   reads=[("pb", vb, vb + (q + 1) * 128),
                          ("pb", ATT, ATT + L)],
                   writes=[("pp", reg, reg + 128)])
                if a == 1:
                    em("scalar", (lambda hh, qq, rr: lambda e: e.copy(
                        pb[:, AO + hh * T + qq * 128:AO + hh * T + (qq + 1) * 128],
                        pp[:, rr:rr + 128]))(hp, q, reg),
                       reads=[("pp", reg, reg + 128)],
                       writes=[("pb", AO + hp * T + q * 128, AO + hp * T + (q + 1) * 128)])

            pipe = []
            for qb in range(NB):
                for a in range(2):
                    u = qb * 2 + a
                    py = u % 2
                    st0 = 16 * py
                    prb = PRB if py == 0 else PRB_B
                    L = (qb + 1) * 128
                    nch = (L + 511) // 512
                    sb0 = 1024 if (a == 1 and qb < 8) else 0
                    for c in range(nch):
                        w = min(512, L - c * 512)
                        em("tensor", (lambda aa, qq, cc, ww, b0: lambda e: e.matmul(
                            pp[:, b0 + cc * 512:b0 + cc * 512 + ww],
                            pb[64 * aa:64 * aa + 64, QHS + qq * 128:QHS + (qq + 1) * 128],
                            pb[64 * aa:64 * aa + 64, KHS + cc * 512:KHS + cc * 512 + ww],
                            start=True, stop=True))(a, qb, c, w, sb0),
                           reads=[("pb", QHS + qb * 128, QHS + (qb + 1) * 128),
                                  ("pb", KHS + c * 512, KHS + c * 512 + w)],
                           writes=[("pp", sb0 + c * 512, sb0 + c * 512 + w)])
                    em("vector", (lambda LL, b0, s0: lambda e: e.reduce_max(
                        stat[:, s0:s0 + 1], pp[:, b0:b0 + LL], axis=AX.X,
                        negate=True))(L, sb0, st0),
                       reads=[("pp", sb0, sb0 + L)], writes=[("stat", st0, st0 + 1)])
                    em("vector", (lambda s0: lambda e: e.tensor_scalar(
                        stat[:, s0 + 1:s0 + 2], stat[:, s0:s0 + 1], SCALE, None, ALU.mult))(st0),
                       reads=[("stat", st0, st0 + 1)], writes=[("stat", st0 + 1, st0 + 2)])
                    em("vector", (lambda qq, b0, pyy: lambda e: e.tensor_tensor(
                        arfs(DIAG + pyy * 128, 128), pp[:, b0 + qq * 128:b0 + (qq + 1) * 128],
                        trineg[:], ALU.add))(qb, sb0, py),
                       reads=[("pp", sb0 + qb * 128, sb0 + (qb + 1) * 128), ("trineg", 0, 128)],
                       writes=[("arf", DIAG + py * 128, DIAG + py * 128 + 128)])
                    if len(pipe) >= 2:
                        s2_stage(*pipe.pop(0))
                    pipe.append((qb, a, py))
                    if qb > 0:
                        em("scalar", (lambda qq, b0, s0, pr: lambda e: e.activation(
                            pb[:, pr:pr + qq * 128], pp[:, b0:b0 + qq * 128], AF.Exp,
                            bias=stat[:, s0 + 1:s0 + 2], scale=SCALE,
                            accum_out=stat[:, s0 + 2:s0 + 3]))(qb, sb0, st0, prb),
                           reads=[("pp", sb0, sb0 + qb * 128), ("stat", st0 + 1, st0 + 2)],
                           writes=[("pb", prb, prb + qb * 128), ("stat", st0 + 2, st0 + 3)])
                    em("scalar", (lambda qq, s0, pyy, pr: lambda e: e.activation(
                        pb[:, pr + qq * 128:pr + (qq + 1) * 128],
                        arfs(DIAG + pyy * 128, 128), AF.Exp,
                        bias=stat[:, s0 + 1:s0 + 2], scale=SCALE,
                        accum_out=stat[:, s0 + 3:s0 + 4]))(qb, st0, py, prb),
                       reads=[("arf", DIAG + py * 128, DIAG + py * 128 + 128),
                              ("stat", st0 + 1, st0 + 2)],
                       writes=[("pb", prb + qb * 128, prb + (qb + 1) * 128),
                               ("stat", st0 + 3, st0 + 4)])
                    if qb > 0:
                        em("vector", (lambda s0: lambda e: e.tensor_tensor(
                            stat[:, s0 + 4:s0 + 5], stat[:, s0 + 2:s0 + 3],
                            stat[:, s0 + 3:s0 + 4], ALU.add))(st0),
                           reads=[("stat", st0 + 2, st0 + 4)], writes=[("stat", st0 + 4, st0 + 5)])
                    else:
                        em("vector", (lambda s0: lambda e: e.tensor_copy(
                            stat[:, s0 + 4:s0 + 5], stat[:, s0 + 3:s0 + 4]))(st0),
                           reads=[("stat", st0 + 3, st0 + 4)], writes=[("stat", st0 + 4, st0 + 5)])
                    em("vector", (lambda s0: lambda e: e.reciprocal(
                        stat[:, s0 + 5:s0 + 6], stat[:, s0 + 4:s0 + 5]))(st0),
                       reads=[("stat", st0 + 4, st0 + 5)], writes=[("stat", st0 + 5, st0 + 6)])
                    em("vector", (lambda s0, pyy: lambda e: e.tensor_scalar(
                        dgb[:, pyy * 128:(pyy + 1) * 128], i128b[:],
                        stat[:, s0 + 5:s0 + 6], None, ALU.mult))(st0, py),
                       reads=[("i128b", 0, 128), ("stat", st0 + 5, st0 + 6)],
                       writes=[("dgb", py * 128, (py + 1) * 128)])
            for st_ in pipe:
                s2_stage(*st_)
            pipe = []


        emit_hp_loads(0)
        for tq4 in range(NT):
            for b in range(tq4 * 4, tq4 * 4 + 4):
                emit_ln_block(b)
            emit_hp_proj(0, tq4)
        for g in range(4):
            em("tensor", [(lambda bb, jj: lambda e: e.transpose(
                pp[0:8, 3584 + jj * 128:3584 + (jj + 1) * 128],
                wgate[:, bb * 8:(bb + 1) * 8], i128f[:]))(g * 4 + j, j)
                for j in range(4)],
               reads=[("wgate", g * 32, (g + 1) * 32), ("i128f", 0, 128)],
               writes=[("pp", 3584, 4096)])
            em("vector", (lambda gg: lambda e: e.tensor_copy(
                wrow[0:8, gg * 512:(gg + 1) * 512], pp[0:8, 3584:4096]))(g),
               reads=[("pp", 3584, 4096)], writes=[("wrow", g * 512, (g + 1) * 512)])
        emit_hp_vprod(0)
        for hp in range(NHP if DO_ATTN else 0):
            emit_hp_attn(hp)
            if hp + 1 < NHP:
                emit_hp_loads(hp + 1)
                emit_hp_proj(hp + 1)
                emit_hp_vprod(hp + 1)

        for mc in range(KC if DO_ATTN else 0):
            em("gpsimd", (lambda m: lambda e: e.dma_start(
                pbs(WOB[m % 2], 1024), wo_l[m]))(mc),
               writes=[("pb", WOB[mc % 2], WOB[mc % 2] + 1024)], dma="woS")
            for t in range(NT):
                bank = 6 + ((mc * NT + t) % 2)
                em("tensor", [(lambda m, tt, bk, kkc, fi, la: lambda e: e.matmul(
                    pps(BK(bk)),
                    pb[:, WOB[m % 2] + kkc * 128:WOB[m % 2] + (kkc + 1) * 128],
                    pb[:, AO + kkc * T + tt * 512:AO + kkc * T + (tt + 1) * 512],
                    start=fi, stop=la))(mc, t, bank, kc, kc == 0, kc == KC - 1)
                    for kc in range(KC)],
                   reads=[("pb", WOB[mc % 2], WOB[mc % 2] + 1024)] +
                         [("pb", AO + kc * T + t * 512, AO + kc * T + (t + 1) * 512)
                          for kc in range(KC)],
                   writes=[("pp",) + BK(bank)])
                em("vector", (lambda m, tt, bk: lambda e: e.tensor_tensor(
                    acc[:, m * T + tt * 512:m * T + (tt + 1) * 512],
                    acc[:, m * T + tt * 512:m * T + (tt + 1) * 512],
                    pps(BK(bk)), ALU.add))(mc, t, bank),
                   reads=[("acc", mc * T + t * 512, mc * T + (t + 1) * 512),
                          ("pp",) + BK(bank)],
                   writes=[("acc", mc * T + t * 512, mc * T + (t + 1) * 512)])

        final_caps = []

        def emit_o_block(b):
            base = (b % 2) * 1024
            ot = OT[b % 2]
            em("tensor", [(lambda bb, kkc, bs: lambda e: e.transpose(
                pp[:, bs + kkc * 128:bs + (kkc + 1) * 128],
                acc[:, kkc * T + bb * 128:kkc * T + (bb + 1) * 128], i128f[:]))(b, kc, base)
                for kc in range(KC)],
               reads=[("acc", kc * T + b * 128, kc * T + (b + 1) * 128) for kc in range(KC)] +
                     [("i128f", 0, 128)],
               writes=[("pp", base, base + 1024)])
            em("vector", (lambda bs, o: lambda e: e.tensor_copy(
                arfs(o, 1024), pp[:, bs:bs + 1024]))(base, ot),
               reads=[("pp", base, base + 1024)],
               writes=[("arf", ot, ot + 1024)])
            final_caps.append(em("sync", (lambda bb, o: lambda e: e.dma_start(
                out[bb * 128:(bb + 1) * 128, :], arfs(o, 1024)))(b, ot),
               reads=[("arf", ot, ot + 1024)], dma="do%d" % (b % 2)))

        o_done = set()
        for e_i in range(E if DO_FFN else 0):
            eb = e_i % 2
            for t in range(NT):
                em("tensor", (lambda ee, tt: lambda e: e.matmul(
                    pps(BK(6)), sel8[:, ee * 128:(ee + 1) * 128],
                    wrow[0:8, tt * 512:(tt + 1) * 512],
                    start=True, stop=True))(e_i, t),
                   reads=[("sel8", e_i * 128, (e_i + 1) * 128),
                          ("wrow", t * 512, (t + 1) * 512)],
                   writes=[("pp",) + BK(6)])
                em("scalar", (lambda ebb, tt: lambda e: e.copy(
                    arfs(WBC[ebb] + tt * 512, 512), pps(BK(6))))(eb, t),
                   reads=[("pp",) + BK(6)],
                   writes=[("arf", WBC[eb] + t * 512, WBC[eb] + (t + 1) * 512)])

            for half in range(2):
                for c in range(half * HHALF, (half + 1) * HHALF):
                    cl = c - half * HHALF
                    cb = c % 2
                    b1 = 0 if cb == 0 else 2
                    b2 = 1 if cb == 0 else 3
                    em("gpsimd", (lambda ee, cc: lambda e: e.dma_start(
                        pbs(W1B[cc % 2], 1024), w1_l[ee, cc]))(e_i, c),
                       writes=[("pb", W1B[cb], W1B[cb] + 1024)], dma="w1%d" % cb)
                    em("gpsimd", (lambda ee, cc: lambda e: e.dma_start(
                        pbs(W2B[cc % 2], 1024), w2_l[ee, cc]))(e_i, c),
                       writes=[("pb", W2B[cb], W2B[cb] + 1024)], dma="w2%d" % cb)
                    for t in range(NT):
                        tb = t % 2
                        em("tensor", [(lambda cc, tt, bk, kkc, fi, la: lambda e: e.matmul(
                            pps(BK(bk)),
                            pb[:, W1B[cc % 2] + kkc * 128:W1B[cc % 2] + (kkc + 1) * 128],
                            xn[:, kkc * T + tt * 512:kkc * T + (tt + 1) * 512],
                            start=fi, stop=la))(c, t, b1, kc, kc == 0, kc == KC - 1)
                            for kc in range(KC)],
                           reads=[("pb", W1B[cb], W1B[cb] + 1024)] +
                                 [("xn", kc * T + t * 512, kc * T + (t + 1) * 512)
                                  for kc in range(KC)],
                           writes=[("pp",) + BK(b1)])
                        em("tensor", [(lambda cc, tt, bk, kkc, fi, la: lambda e: e.matmul(
                            pps(BK(bk)),
                            pb[:, W2B[cc % 2] + kkc * 128:W2B[cc % 2] + (kkc + 1) * 128],
                            xn[:, kkc * T + tt * 512:kkc * T + (tt + 1) * 512],
                            start=fi, stop=la))(c, t, b2, kc, kc == 0, kc == KC - 1)
                            for kc in range(KC)],
                           reads=[("pb", W2B[cb], W2B[cb] + 1024)] +
                                 [("xn", kc * T + t * 512, kc * T + (t + 1) * 512)
                                  for kc in range(KC)],
                           writes=[("pp",) + BK(b2)])
                        em("scalar", (lambda bk, tbb: lambda e: e.activation(
                            arfs(SIL[tbb], 512), pps(BK(bk)), AF.Silu))(b1, tb),
                           reads=[("pp",) + BK(b1)],
                           writes=[("arf", SIL[tb], SIL[tb] + 512)])
                        em("vector", (lambda bk, tbb, ebb, tt: lambda e: e.tensor_tensor(
                            arfs(H2S[tbb], 512), pps(BK(bk)),
                            arfs(WBC[ebb] + tt * 512, 512), ALU.mult))(b2, tb, eb, t),
                           reads=[("pp",) + BK(b2),
                                  ("arf", WBC[eb] + t * 512, WBC[eb] + (t + 1) * 512)],
                           writes=[("arf", H2S[tb], H2S[tb] + 512)])
                        em("vector", (lambda cll, tt, tbb: lambda e: e.tensor_tensor(
                            pb[:, HIDB + cll * T + tt * 512:HIDB + cll * T + (tt + 1) * 512],
                            arfs(SIL[tbb], 512), arfs(H2S[tbb], 512), ALU.mult))(cl, t, tb),
                           reads=[("arf", SIL[tb], SIL[tb] + 512),
                                  ("arf", H2S[tb], H2S[tb] + 512)],
                           writes=[("pb", HIDB + cl * T + t * 512,
                                    HIDB + cl * T + (t + 1) * 512)])
                for m in range(KC):
                    mb = m % 2
                    em("gpsimd", (lambda ee, hf, mm: lambda e: e.dma_start(
                        pbs(W3B[mm % 2], HHALF * 128),
                        w3_l[ee, mm, :, hf * HHALF * 128:(hf + 1) * HHALF * 128]))(e_i, half, m),
                       writes=[("pb", W3B[mb], W3B[mb] + HHALF * 128)], dma="w3%d" % mb)
                    for t in range(NT):
                        bank = 4 + ((m * NT + t) % 2)
                        em("tensor", [(lambda mm, tt, bk, cll, fi, la: lambda e: e.matmul(
                            pps(BK(bk)),
                            pb[:, W3B[mm % 2] + cll * 128:W3B[mm % 2] + (cll + 1) * 128],
                            pb[:, HIDB + cll * T + tt * 512:HIDB + cll * T + (tt + 1) * 512],
                            start=fi, stop=la))(m, t, bank, cl, cl == 0, cl == HHALF - 1)
                            for cl in range(HHALF)],
                           reads=[("pb", W3B[mb], W3B[mb] + HHALF * 128)] +
                                 [("pb", HIDB + cl * T + t * 512,
                                   HIDB + cl * T + (t + 1) * 512) for cl in range(HHALF)],
                           writes=[("pp",) + BK(bank)])
                        em("vector", (lambda mm, tt, bk: lambda e: e.tensor_tensor(
                            acc[:, mm * T + tt * 512:mm * T + (tt + 1) * 512],
                            acc[:, mm * T + tt * 512:mm * T + (tt + 1) * 512],
                            pps(BK(bk)), ALU.add))(m, t, bank),
                           reads=[("acc", m * T + t * 512, m * T + (t + 1) * 512),
                                  ("pp",) + BK(bank)],
                           writes=[("acc", m * T + t * 512, m * T + (t + 1) * 512)])
                        if (e_i == E - 1 and half == 1 and m == KC - 1):
                            for bb in range(t * 4, t * 4 + 4):
                                if bb not in o_done:
                                    o_done.add(bb)
                                    emit_o_block(bb)

        o_rest = [b for b in range(NB) if b not in o_done]
        for b in o_rest:
            base = (b % 2) * 1024
            ot = OT[b % 2]
            em("tensor", [(lambda bb, kkc, bs: lambda e: e.transpose(
                pp[:, bs + kkc * 128:bs + (kkc + 1) * 128],
                acc[:, kkc * T + bb * 128:kkc * T + (bb + 1) * 128], i128f[:]))(b, kc, base)
                for kc in range(KC)],
               reads=[("acc", kc * T + b * 128, kc * T + (b + 1) * 128) for kc in range(KC)] +
                     [("i128f", 0, 128)],
               writes=[("pp", base, base + 1024)])
            em("vector", (lambda bs, o: lambda e: e.tensor_copy(
                arfs(o, 1024), pp[:, bs:bs + 1024]))(base, ot),
               reads=[("pp", base, base + 1024)],
               writes=[("arf", ot, ot + 1024)])
            final_caps.append(em("sync", (lambda bb, o: lambda e: e.dma_start(
                out[bb * 128:(bb + 1) * 128, :], arfs(o, 1024)))(b, ot),
               reads=[("arf", ot, ot + 1024)], dma="do%d" % (b % 2)))

        def play(e, name):
            for (fns, waits, sem, inc_k) in B.ops[name]:
                for (s, v) in waits:
                    e.wait_ge(sems[s], v)
                r = None
                for fn in fns:
                    r = fn(e)
                r.then_inc(sems[sem], inc_k)
            if name == "sync":
                e.wait_ge(sems["do0"], B.cnt["do0"])
                e.wait_ge(sems["do1"], B.cnt["do1"])

        @block.sync
        def _(e):
            play(e, "sync")

        @block.gpsimd
        def _(e):
            play(e, "gpsimd")

        @block.tensor
        def _(e):
            play(e, "tensor")

        @block.vector
        def _(e):
            play(e, "vector")

        @block.scalar
        def _(e):
            play(e, "scalar")

    return nc


def _lhsT_tiles(w):
    return np.ascontiguousarray(
        w.reshape(8, 128, 8, 128).transpose(0, 3, 2, 1).reshape(8, 128, 1024))


def _swap_rows(w):
    return np.ascontiguousarray(w.reshape(-1, 2, w.shape[-1])[:, ::-1, :].reshape(w.shape))


def kernel(L0_attn_x, L0_attn_Wq, L0_attn_Wk, L0_attn_Wv, L0_attn_Wo,
           L0_ffn_Wg, L0_ffn_W1, L0_ffn_W2, L0_ffn_W3):
    x = np.ascontiguousarray(np.asarray(L0_attn_x, dtype=np.float32))
    Wq = np.asarray(L0_attn_Wq, dtype=np.float32)
    Wk = np.asarray(L0_attn_Wk, dtype=np.float32)
    Wv = np.asarray(L0_attn_Wv, dtype=np.float32)
    Wo = np.asarray(L0_attn_Wo, dtype=np.float32)
    Wg = np.asarray(L0_ffn_Wg, dtype=np.float32)
    W1 = np.asarray(L0_ffn_W1, dtype=np.float32)
    W2 = np.asarray(L0_ffn_W2, dtype=np.float32)
    W3 = np.asarray(L0_ffn_W3, dtype=np.float32)

    wq_l = _lhsT_tiles(Wq)
    wqs_l = _lhsT_tiles(_swap_rows(Wq))
    wk_l = _lhsT_tiles(Wk)
    wks_l = _lhsT_tiles(_swap_rows(Wk))
    wv_r = _lhsT_tiles(Wv)
    wo_l = _lhsT_tiles(Wo)
    wg_r = np.ascontiguousarray(Wg.reshape(8, 8, 128).transpose(2, 1, 0).reshape(128, 64))

    HPAD = HCH * 128
    W1p = np.zeros((E, HPAD, D), np.float32)
    W1p[:, :2730] = W1
    W2p = np.zeros((E, HPAD, D), np.float32)
    W2p[:, :2730] = W2
    W3p = np.zeros((E, D, HPAD), np.float32)
    W3p[:, :, :2730] = W3
    w1_l = np.ascontiguousarray(
        W1p.reshape(E, HCH, 128, 8, 128).transpose(0, 1, 4, 3, 2).reshape(E, HCH, 128, 1024))
    w2_l = np.ascontiguousarray(
        W2p.reshape(E, HCH, 128, 8, 128).transpose(0, 1, 4, 3, 2).reshape(E, HCH, 128, 1024))
    w3_l = np.ascontiguousarray(
        W3p.reshape(E, 8, 128, HCH, 128).transpose(0, 1, 4, 3, 2).reshape(E, 8, 128, HCH * 128))

    inv_freq = 1.0 / (10000.0 ** (np.arange(0, 64, 2, dtype=np.float64) / 64.0))
    tt = np.arange(T, dtype=np.float64)
    fr = np.outer(tt, inv_freq)
    cosv = np.cos(fr)
    sinv = np.sin(fr)
    p = np.arange(128)
    fi = (p % 64) // 2
    ct = cosv[:, fi].T.astype(np.float32)
    sgn = np.where(p % 2 == 0, -1.0, 1.0)
    st = (sinv[:, fi].T * sgn[:, None]).astype(np.float32)

    i128 = np.eye(128, dtype=np.float32)
    qi = np.arange(128)[:, None]
    ki = np.arange(128)[None, :]
    trineg = np.where(ki <= qi, 0.0, -1e30).astype(np.float32)
    sel8 = np.zeros((8, 1024), np.float32)
    for e_ in range(8):
        sel8[e_, e_ * 128:(e_ + 1) * 128] = 1.0

    shared = {
        "wq_l": wq_l, "wqs_l": wqs_l, "wk_l": wk_l, "wks_l": wks_l,
        "wv_r": wv_r, "wo_l": wo_l, "wg_r": wg_r,
        "w1_l": w1_l, "w2_l": w2_l, "w3_l": w3_l,
        "ct": ct, "st": st, "i128": i128, "trineg": trineg, "sel8": sel8,
    }
    in_maps = [dict(shared, x=np.ascontiguousarray(x[b])) for b in range(8)]
    nc = build_nc()
    res = run_bass_kernel_spmd(nc, in_maps, core_ids=list(range(8)))
    return np.stack([np.asarray(r["out"]) for r in res.results], axis=0)


# version 14
# speedup vs baseline: 1.0315x; 1.0315x over previous
import contextlib
import numpy as np
import concourse.bass as bass
import concourse.mybir as mybir
from concourse.bass_utils import run_bass_kernel_spmd

F32 = mybir.dt.float32
BF16 = mybir.dt.bfloat16
AF = mybir.ActivationFunctionType
ALU = mybir.AluOpType
AX = mybir.AxisListType

T = 2048
D = 1024
KC = 8
NB = 16
NT = 4
NHP = 8
E = 8
HCH = 22
HHALF = 11
EPS = 1e-5
SCALE = 0.125

DO_ATTN = True
DO_FFN = True

ENGS = ("sync", "gpsimd", "tensor", "vector", "scalar")
ENG_SEM = {"tensor": "tPE", "vector": "tVE", "scalar": "tSC",
           "sync": "tSY", "gpsimd": "tGP"}

AO = 0
QHS = 16384
KHS = 18432
VH = (20480, 22528)
PRB = 24576
ATT = 26624
WQB, WQSB, WKB, WKSB = 28672, 29696, 30720, 31744
WVB = 32768
WOB = (32768, 32768)
HIDB = 0
W1B = (28672, 29696)
W2B = (30720, 31744)
W3B = (24576, 25984)
POOLB_N = 35840
PRB_B = 33792

XT = (0, 1024)
XC = 2048
SQ = 3072
XNF = 4096
DIAG = 2048
RT1, RT2 = 0, 512
WBC = (0, 2048)
SIL = (4096, 4608)
H2S = (5120, 5632)
OT = (0, 1024)
ARF_N = 6144

GRAN = {"pp": 512, "stat": 8}


def BK(b, off=0, w=512):
    return (b * 512 + off, b * 512 + off + w)


class Builder:

    def __init__(self):
        self.ops = {e: [] for e in ENGS}
        import collections
        self.cnt = collections.defaultdict(int)
        self.wr = {}
        self.rd = {}
        self.waited = {e: {} for e in ENGS}

    def _slots(self, rng):
        name, lo, hi = rng
        g = GRAN.get(name, 128)
        return [(name, s) for s in range(lo // g, (hi + g - 1) // g)]

    def emit(self, eng, fns, reads=(), writes=(), dma=None, extra=()):
        if not isinstance(fns, (list, tuple)):
            fns = [fns]
        waits = {}

        def need(cap):
            if cap is None:
                return
            s, v = cap
            if v <= 0:
                return
            waits[s] = max(waits.get(s, 0), v)

        rslots = []
        for rng in reads:
            for sl in self._slots(rng):
                rslots.append(sl)
                need(self.wr.get(sl))
                if sl[0] == "pp":
                    for cap, eng2 in self.rd.get(sl, ()):
                        if eng2 != eng:
                            need(cap)
        wslots = []
        for rng in writes:
            for sl in self._slots(rng):
                wslots.append(sl)
                need(self.wr.get(sl))
                for cap, eng2 in self.rd.get(sl, ()):
                    need(cap)
        for cap in extra:
            need(cap)

        wd = self.waited[eng]
        final = []
        for s, v in sorted(waits.items()):
            if wd.get(s, 0) >= v:
                continue
            wd[s] = v
            final.append((s, v))

        sem = dma if dma is not None else ENG_SEM[eng]
        inc_k = 16 if dma is not None else 1
        self.cnt[sem] += inc_k
        cap = (sem, self.cnt[sem])

        for sl in rslots:
            self.rd.setdefault(sl, []).append((cap, eng))
        for sl in wslots:
            self.wr[sl] = cap
            self.rd[sl] = []

        self.ops[eng].append((fns, tuple(final), sem, inc_k))
        return cap


def build_nc():
    nc = bass.Bass("TRN2", target_bir_lowering=False, debug=False)
    dt = nc.dram_tensor
    x_in = dt("x", [T, D], F32, kind="ExternalInput").ap()
    wq_l = dt("wq_l", [NHP, 128, 1024], F32, kind="ExternalInput").ap()
    wqs_l = dt("wqs_l", [NHP, 128, 1024], F32, kind="ExternalInput").ap()
    wk_l = dt("wk_l", [NHP, 128, 1024], F32, kind="ExternalInput").ap()
    wks_l = dt("wks_l", [NHP, 128, 1024], F32, kind="ExternalInput").ap()
    wv_r = dt("wv_r", [NHP, 128, 1024], F32, kind="ExternalInput").ap()
    wo_l = dt("wo_l", [KC, 128, 1024], F32, kind="ExternalInput").ap()
    wg_r = dt("wg_r", [128, 64], F32, kind="ExternalInput").ap()
    w1_l = dt("w1_l", [E, HCH, 128, 1024], F32, kind="ExternalInput").ap()
    w2_l = dt("w2_l", [E, HCH, 128, 1024], F32, kind="ExternalInput").ap()
    w3_l = dt("w3_l", [E, KC, 128, HCH * 128], F32, kind="ExternalInput").ap()
    ct_in = dt("ct", [128, T], F32, kind="ExternalInput").ap()
    st_in = dt("st", [128, T], F32, kind="ExternalInput").ap()
    i128_in = dt("i128", [128, 128], F32, kind="ExternalInput").ap()
    trineg_in = dt("trineg", [128, 128], F32, kind="ExternalInput").ap()
    sel8_in = dt("sel8", [8, 1024], F32, kind="ExternalInput").ap()
    out = dt("out", [T, D], F32, kind="ExternalOutput").ap()

    with contextlib.ExitStack() as ctx:
        en = ctx.enter_context
        acc = en(nc.sbuf_tensor("acc", [128, KC * T], F32))
        xn = en(nc.sbuf_tensor("xn", [128, KC * T], BF16))
        pb = en(nc.sbuf_tensor("poolb", [128, POOLB_N], BF16))
        arf = en(nc.sbuf_tensor("arf", [128, ARF_N], F32))
        csb = en(nc.sbuf_tensor("csb", [128, 2 * T], BF16))
        wgate = en(nc.sbuf_tensor("wgate", [128, 128], F32))
        wrow = en(nc.sbuf_tensor("wrow", [8, T], BF16))
        i128f = en(nc.sbuf_tensor("i128f", [128, 128], F32))
        i128b = en(nc.sbuf_tensor("i128b", [128, 128], BF16))
        trineg = en(nc.sbuf_tensor("trineg_s", [128, 128], F32))
        sel8 = en(nc.sbuf_tensor("sel8_s", [8, 1024], BF16))
        wgr = en(nc.sbuf_tensor("wgr", [128, 64], F32))
        dgb = en(nc.sbuf_tensor("dgb", [128, 256], BF16))
        stat = en(nc.sbuf_tensor("stat", [128, 64], F32))
        pp = en(nc.psum_tensor("pp", [128, 4096], F32))
        sem_names = ("tPE", "tVE", "tSC", "tSY", "tGP",
                     "cI128F", "cTRI", "cWGR", "cCT", "cST", "cI128B", "cSEL",
                     "dx0", "dx1", "wqS", "wqsS", "wkS", "wksS", "wvS", "woS",
                     "w10", "w11", "w20", "w21", "w30", "w31", "do0", "do1")
        sems = {n: en(nc.semaphore(n)) for n in sem_names}
        block = en(nc.Block())

        B = Builder()
        em = B.emit

        def arfs(off, w):
            return arf[:, off:off + w]

        def pbs(off, w):
            return pb[:, off:off + w]

        def pps(lo_hi):
            lo, hi = lo_hi
            return pp[:, lo:hi]

        em("sync", lambda e: e.dma_start(i128f[:], i128_in),
           writes=[("i128f", 0, 128)], dma="cI128F")
        em("sync", lambda e: e.dma_start(trineg[:], trineg_in),
           writes=[("trineg", 0, 128)], dma="cTRI")
        em("sync", lambda e: e.dma_start(wgr[:], wg_r),
           writes=[("wgr", 0, 64)], dma="cWGR")
        em("gpsimd", lambda e: e.dma_start(csb[:, 0:T], ct_in),
           writes=[("csb", 0, T)], dma="cCT")
        em("gpsimd", lambda e: e.dma_start(csb[:, T:2 * T], st_in),
           writes=[("csb", T, 2 * T)], dma="cST")
        em("gpsimd", lambda e: e.dma_start(i128b[:], i128_in),
           writes=[("i128b", 0, 128)], dma="cI128B")
        em("gpsimd", lambda e: e.dma_start(sel8[:], sel8_in),
           writes=[("sel8", 0, 1024)], dma="cSEL")

        def emit_ln_block(b):
            bf = b % 2
            xt0 = XT[bf]
            sb = bf * 32
            em("sync", (lambda bb, o: lambda e: e.dma_start(
                arfs(o, 1024), x_in[bb * 128:(bb + 1) * 128, :]))(b, xt0),
               writes=[("arf", xt0, xt0 + 1024)], dma="dx%d" % bf)
            em("vector", (lambda o, ss: lambda e: e.reduce_sum(
                stat[:, ss:ss + 1], arfs(o, 1024), axis=AX.X))(xt0, sb),
               reads=[("arf", xt0, xt0 + 1024)], writes=[("stat", sb, sb + 1)])
            em("vector", (lambda ss: lambda e: e.tensor_scalar(
                stat[:, ss + 1:ss + 2], stat[:, ss:ss + 1], -1.0 / D, None, ALU.mult))(sb),
               reads=[("stat", sb, sb + 1)], writes=[("stat", sb + 1, sb + 2)])
            em("vector", (lambda o, ss: lambda e: e.tensor_scalar(
                arfs(XC, 1024), arfs(o, 1024), stat[:, ss + 1:ss + 2], None, ALU.add))(xt0, sb),
               reads=[("arf", xt0, xt0 + 1024), ("stat", sb + 1, sb + 2)],
               writes=[("arf", XC, XC + 1024)])
            em("scalar", (lambda ss: lambda e: e.activation(
                arfs(SQ, 1024), arfs(XC, 1024), AF.Square,
                accum_out=stat[:, ss + 2:ss + 3]))(sb),
               reads=[("arf", XC, XC + 1024)],
               writes=[("arf", SQ, SQ + 1024), ("stat", sb + 2, sb + 3)])
            em("vector", (lambda ss: lambda e: e.tensor_scalar(
                stat[:, ss + 3:ss + 4], stat[:, ss + 2:ss + 3], 1.0 / D, EPS,
                ALU.mult, ALU.add))(sb),
               reads=[("stat", sb + 2, sb + 3)], writes=[("stat", sb + 3, sb + 4)])
            em("scalar", (lambda ss: lambda e: e.activation(
                stat[:, ss + 4:ss + 5], stat[:, ss + 3:ss + 4], AF.Sqrt))(sb),
               reads=[("stat", sb + 3, sb + 4)], writes=[("stat", sb + 4, sb + 5)])
            em("vector", (lambda ss: lambda e: e.reciprocal(
                stat[:, ss + 5:ss + 6], stat[:, ss + 4:ss + 5]))(sb),
               reads=[("stat", sb + 4, sb + 5)], writes=[("stat", sb + 5, sb + 6)])
            em("vector", (lambda ss: lambda e: e.tensor_scalar(
                arfs(XC, 1024), arfs(XC, 1024), stat[:, ss + 5:ss + 6], None, ALU.mult))(sb),
               reads=[("arf", XC, XC + 1024), ("stat", sb + 5, sb + 6)],
               writes=[("arf", XC, XC + 1024)])
            em("tensor", [(lambda kkc: lambda e: e.transpose(
                pp[:, 2048 + kkc * 128:2048 + (kkc + 1) * 128],
                arf[:, XC + kkc * 128:XC + (kkc + 1) * 128], i128f[:]))(kc)
                for kc in range(KC)],
               reads=[("arf", XC, XC + 1024), ("i128f", 0, 128)],
               writes=[("pp", 2048, 3072)])
            em("scalar", (lambda bb: lambda e: e.copy(
                xn[:].rearrange("p (k t) -> p k t", k=KC)[:, :, bb * 128:(bb + 1) * 128],
                pp[:, 2048:3072].rearrange("p (k t) -> p k t", k=KC)))(b),
               reads=[("pp", 2048, 3072)],
               writes=[("xn", kc * T + b * 128, kc * T + (b + 1) * 128) for kc in range(KC)])
            em("vector", lambda e: e.tensor_copy(arfs(XNF, 1024), pp[:, 2048:3072]),
               reads=[("pp", 2048, 3072)], writes=[("arf", XNF, XNF + 1024)])
            em("tensor", [(lambda kkc, fi, la: lambda e: e.matmul(
                pp[:, 2048:2056], arf[:, XNF + kkc * 128:XNF + (kkc + 1) * 128],
                wgr[:, kkc * 8:(kkc + 1) * 8], start=fi, stop=la))(kc, kc == 0, kc == KC - 1)
                for kc in range(KC)],
               reads=[("arf", XNF, XNF + 1024), ("wgr", 0, 64)],
               writes=[("pp", 2048, 2560)])
            em("tensor", [(lambda o, kkc: lambda e: e.transpose(
                pp[:, 3072 + kkc * 128:3072 + (kkc + 1) * 128],
                arf[:, o + kkc * 128:o + (kkc + 1) * 128], i128f[:]))(xt0, kc)
                for kc in range(KC)],
               reads=[("arf", xt0, xt0 + 1024), ("i128f", 0, 128)],
               writes=[("pp", 3072, 4096)])
            em("vector", (lambda bb: lambda e: e.tensor_copy(
                acc[:].rearrange("p (k t) -> p k t", k=KC)[:, :, bb * 128:(bb + 1) * 128],
                pp[:, 3072:4096].rearrange("p (k t) -> p k t", k=KC)))(b),
               reads=[("pp", 3072, 4096)],
               writes=[("acc", kc * T + b * 128, kc * T + (b + 1) * 128) for kc in range(KC)])
            gread = [("pp", 2048, 2560)]
            em("vector", (lambda ss: lambda e: e.reduce_max(
                stat[:, ss + 6:ss + 7], pp[:, 2048:2056], axis=AX.X))(sb),
               reads=gread, writes=[("stat", sb + 6, sb + 7)])
            em("vector", (lambda ss: lambda e: e.tensor_scalar(
                stat[:, ss + 7:ss + 8], stat[:, ss + 6:ss + 7], -1.0, None, ALU.mult))(sb),
               reads=[("stat", sb + 6, sb + 7)], writes=[("stat", sb + 7, sb + 8)])
            em("vector", (lambda ss: lambda e: e.tensor_scalar(
                stat[:, ss + 8:ss + 16], pp[:, 2048:2056], stat[:, ss + 6:ss + 7], None,
                ALU.is_ge))(sb),
               reads=gread + [("stat", sb + 6, sb + 7)], writes=[("stat", sb + 8, sb + 16)])
            em("vector", (lambda ss: lambda e: e.scalar_tensor_tensor(
                stat[:, ss + 16:ss + 24], stat[:, ss + 8:ss + 16], -1e30, pp[:, 2048:2056],
                ALU.mult, ALU.add))(sb),
               reads=gread + [("stat", sb + 8, sb + 16)], writes=[("stat", sb + 16, sb + 24)])
            em("vector", (lambda ss: lambda e: e.reduce_max(
                stat[:, ss + 24:ss + 25], stat[:, ss + 16:ss + 24], axis=AX.X))(sb),
               reads=[("stat", sb + 16, sb + 24)], writes=[("stat", sb + 24, sb + 25)])
            em("scalar", (lambda ss: lambda e: e.activation(
                stat[:, ss + 25:ss + 26], stat[:, ss + 24:ss + 25], AF.Exp,
                bias=stat[:, ss + 7:ss + 8]))(sb),
               reads=[("stat", sb + 24, sb + 25), ("stat", sb + 7, sb + 8)],
               writes=[("stat", sb + 25, sb + 26)])
            em("vector", (lambda ss: lambda e: e.tensor_scalar(
                stat[:, ss + 26:ss + 27], stat[:, ss + 25:ss + 26], 1.0, None, ALU.add))(sb),
               reads=[("stat", sb + 25, sb + 26)], writes=[("stat", sb + 26, sb + 27)])
            em("vector", (lambda ss: lambda e: e.reciprocal(
                stat[:, ss + 27:ss + 28], stat[:, ss + 26:ss + 27]))(sb),
               reads=[("stat", sb + 26, sb + 27)], writes=[("stat", sb + 27, sb + 28)])
            em("scalar", (lambda ss: lambda e: e.activation(
                stat[:, ss + 16:ss + 24], pp[:, 2048:2056], AF.Exp,
                bias=stat[:, ss + 7:ss + 8]))(sb),
               reads=gread + [("stat", sb + 7, sb + 8)], writes=[("stat", sb + 16, sb + 24)])
            em("vector", (lambda ss: lambda e: e.tensor_scalar(
                stat[:, ss + 8:ss + 16], pp[:, 2048:2056], stat[:, ss + 24:ss + 25], None,
                ALU.is_ge))(sb),
               reads=gread + [("stat", sb + 24, sb + 25)], writes=[("stat", sb + 8, sb + 16)])
            em("vector", (lambda ss: lambda e: e.tensor_tensor(
                stat[:, ss + 8:ss + 16], stat[:, ss + 16:ss + 24],
                stat[:, ss + 8:ss + 16], ALU.mult))(sb),
               reads=[("stat", sb + 8, sb + 24)], writes=[("stat", sb + 8, sb + 16)])
            em("vector", (lambda bb, ss: lambda e: e.tensor_scalar(
                wgate[:, bb * 8:(bb + 1) * 8], stat[:, ss + 8:ss + 16],
                stat[:, ss + 27:ss + 28], None, ALU.mult))(b, sb),
               reads=[("stat", sb + 8, sb + 16), ("stat", sb + 27, sb + 28)],
               writes=[("wgate", b * 8, (b + 1) * 8)])

        def emit_hp_loads(hp):
            for buf, warr, wsem in ((WQB, wq_l, "wqS"), (WQSB, wqs_l, "wqsS"),
                                    (WKB, wk_l, "wkS"), (WKSB, wks_l, "wksS"),
                                    (WVB, wv_r, "wvS")):
                em("gpsimd", (lambda bb, ss, h: lambda e: e.dma_start(
                    pbs(bb, 1024), ss[h]))(buf, warr, hp),
                   writes=[("pb", buf, buf + 1024)], dma=wsem)


        def emit_hp_proj(hp, tq_only=None):
            for tq in (range(NT) if tq_only is None else [tq_only]):
                c0, c1 = tq * 512, (tq + 1) * 512
                for bank, wbuf in ((0, WQB), (1, WQSB), (2, WKB), (3, WKSB)):
                    em("tensor", [(lambda bk, wb, kkc, fi, la, a0, a1: lambda e: e.matmul(
                        pps(BK(bk)), pb[:, wb + kkc * 128:wb + (kkc + 1) * 128],
                        xn[:, kkc * T + a0:kkc * T + a1],
                        start=fi, stop=la))(bank, wbuf, kc, kc == 0, kc == KC - 1, c0, c1)
                        for kc in range(KC)],
                       reads=[("pb", wbuf, wbuf + 1024)] +
                             [("xn", kc * T + c0, kc * T + c1) for kc in range(KC)],
                       writes=[("pp",) + BK(bank)])
                em("vector", (lambda a0, a1: lambda e: e.tensor_tensor(
                    arfs(RT1, 512), pps(BK(0)), csb[:, a0:a1], ALU.mult))(c0, c1),
                   reads=[("pp",) + BK(0), ("csb", c0, c1)],
                   writes=[("arf", RT1, RT1 + 512)])
                em("vector", (lambda a0, a1: lambda e: e.tensor_tensor(
                    arfs(RT2, 512), pps(BK(1)), csb[:, T + a0:T + a1], ALU.mult))(c0, c1),
                   reads=[("pp",) + BK(1), ("csb", T + c0, T + c1)],
                   writes=[("arf", RT2, RT2 + 512)])
                em("vector", (lambda a0, a1: lambda e: e.tensor_tensor(
                    pb[:, QHS + a0:QHS + a1], arfs(RT1, 512), arfs(RT2, 512),
                    ALU.add))(c0, c1),
                   reads=[("arf", RT1, RT1 + 512), ("arf", RT2, RT2 + 512)],
                   writes=[("pb", QHS + c0, QHS + c1)])
                em("vector", (lambda a0, a1: lambda e: e.tensor_tensor(
                    arfs(RT1, 512), pps(BK(2)), csb[:, a0:a1], ALU.mult))(c0, c1),
                   reads=[("pp",) + BK(2), ("csb", c0, c1)],
                   writes=[("arf", RT1, RT1 + 512)])
                em("vector", (lambda a0, a1: lambda e: e.tensor_tensor(
                    arfs(RT2, 512), pps(BK(3)), csb[:, T + a0:T + a1], ALU.mult))(c0, c1),
                   reads=[("pp",) + BK(3), ("csb", T + c0, T + c1)],
                   writes=[("arf", RT2, RT2 + 512)])
                em("vector", (lambda a0, a1: lambda e: e.tensor_tensor(
                    pb[:, KHS + a0:KHS + a1], arfs(RT1, 512), arfs(RT2, 512),
                    ALU.add))(c0, c1),
                   reads=[("arf", RT1, RT1 + 512), ("arf", RT2, RT2 + 512)],
                   writes=[("pb", KHS + c0, KHS + c1)])


        def emit_hp_vprod(hp):
            vb = VH[hp % 2]
            for g4 in range(4):
                bank = 4 + (g4 % 2)
                fns = []
                for j in range(4):
                    blk = g4 * 4 + j
                    for kc in range(KC):
                        fns.append((lambda bk, jj, bb, kkc, fi, la: lambda e: e.matmul(
                            pp[:, BK(bk)[0] + jj * 128:BK(bk)[0] + (jj + 1) * 128],
                            xn[:, kkc * T + bb * 128:kkc * T + (bb + 1) * 128],
                            pb[:, WVB + kkc * 128:WVB + (kkc + 1) * 128],
                            start=fi, stop=la))(bank, j, blk, kc, kc == 0, kc == KC - 1))
                em("tensor", fns,
                   reads=[("pb", WVB, WVB + 1024)] +
                         [("xn", kc * T + g4 * 512, kc * T + (g4 + 1) * 512) for kc in range(KC)],
                   writes=[("pp",) + BK(bank)])
                em("scalar", (lambda vbb, gg, bk: lambda e: e.copy(
                    pb[:, vbb + gg * 512:vbb + (gg + 1) * 512], pps(BK(bk))))(vb, g4, bank),
                   reads=[("pp",) + BK(bank)],
                   writes=[("pb", vb + g4 * 512, vb + (g4 + 1) * 512)])


        def emit_hp_attn(hp):
            vb = VH[hp % 2]
            def s2_stage(q, a, py):
                prb = PRB if py == 0 else PRB_B
                L = (q + 1) * 128
                ngrp = (q + 4) // 4
                for g in range(ngrp):
                    bank = 4 + (g % 2)
                    kb0 = g * 4
                    kbn = min(4, q + 1 - kb0)
                    em("tensor", [(lambda bk, jj, kk, pr, pyy: lambda e: e.matmul(
                        pp[:, BK(bk)[0] + jj * 128:BK(bk)[0] + (jj + 1) * 128],
                        pb[:, pr + kk * 128:pr + (kk + 1) * 128],
                        dgb[:, pyy * 128:(pyy + 1) * 128],
                        start=True, stop=True))(bank, j, kb0 + j, prb, py)
                        for j in range(kbn)],
                       reads=[("pb", prb + kb0 * 128, prb + (kb0 + kbn) * 128),
                              ("dgb", py * 128, (py + 1) * 128)],
                       writes=[("pp",) + BK(bank)])
                    em("vector", (lambda gg, bk, n: lambda e: e.tensor_copy(
                        pb[:, ATT + gg * 512:ATT + gg * 512 + n * 128],
                        pp[:, BK(bk)[0]:BK(bk)[0] + n * 128]))(g, bank, kbn),
                       reads=[("pp",) + BK(bank)],
                       writes=[("pb", ATT + kb0 * 128, ATT + (kb0 + kbn) * 128)])
                reg = 3072 + (q % 2) * 128
                fns = []
                for kb in range(q + 1):
                    tp = (0, 64) if a == 1 else None
                    fns.append((lambda ax, kk, rr, fi, la, tpp, vbb: lambda e: e.matmul(
                        pp[64 * ax:64 * ax + 64, rr:rr + 128],
                        pb[:, vbb + kk * 128 + 64 * ax:vbb + kk * 128 + 64 * ax + 64],
                        pb[:, ATT + kk * 128:ATT + (kk + 1) * 128],
                        start=fi, stop=la, tile_position=tpp))(
                            a, kb, reg, kb == 0, kb == q, tp, vb))
                em("tensor", fns,
                   reads=[("pb", vb, vb + (q + 1) * 128),
                          ("pb", ATT, ATT + L)],
                   writes=[("pp", reg, reg + 128)])
                if a == 1:
                    em("scalar", (lambda hh, qq, rr: lambda e: e.copy(
                        pb[:, AO + hh * T + qq * 128:AO + hh * T + (qq + 1) * 128],
                        pp[:, rr:rr + 128]))(hp, q, reg),
                       reads=[("pp", reg, reg + 128)],
                       writes=[("pb", AO + hp * T + q * 128, AO + hp * T + (q + 1) * 128)])

            pipe = []
            for qb in range(NB):
                for a in range(2):
                    u = qb * 2 + a
                    py = u % 2
                    st0 = 16 * py
                    prb = PRB if py == 0 else PRB_B
                    L = (qb + 1) * 128
                    nch = (L + 511) // 512
                    sb0 = 1024 if (a == 1 and qb < 8) else 0
                    for c in range(nch):
                        w = min(512, L - c * 512)
                        em("tensor", (lambda aa, qq, cc, ww, b0: lambda e: e.matmul(
                            pp[:, b0 + cc * 512:b0 + cc * 512 + ww],
                            pb[64 * aa:64 * aa + 64, QHS + qq * 128:QHS + (qq + 1) * 128],
                            pb[64 * aa:64 * aa + 64, KHS + cc * 512:KHS + cc * 512 + ww],
                            start=True, stop=True))(a, qb, c, w, sb0),
                           reads=[("pb", QHS + qb * 128, QHS + (qb + 1) * 128),
                                  ("pb", KHS + c * 512, KHS + c * 512 + w)],
                           writes=[("pp", sb0 + c * 512, sb0 + c * 512 + w)])
                    em("vector", (lambda LL, b0, s0: lambda e: e.reduce_max(
                        stat[:, s0:s0 + 1], pp[:, b0:b0 + LL], axis=AX.X,
                        negate=True))(L, sb0, st0),
                       reads=[("pp", sb0, sb0 + L)], writes=[("stat", st0, st0 + 1)])
                    em("vector", (lambda s0: lambda e: e.tensor_scalar(
                        stat[:, s0 + 1:s0 + 2], stat[:, s0:s0 + 1], SCALE, None, ALU.mult))(st0),
                       reads=[("stat", st0, st0 + 1)], writes=[("stat", st0 + 1, st0 + 2)])
                    em("vector", (lambda qq, b0, pyy: lambda e: e.tensor_tensor(
                        arfs(DIAG + pyy * 128, 128), pp[:, b0 + qq * 128:b0 + (qq + 1) * 128],
                        trineg[:], ALU.add))(qb, sb0, py),
                       reads=[("pp", sb0 + qb * 128, sb0 + (qb + 1) * 128), ("trineg", 0, 128)],
                       writes=[("arf", DIAG + py * 128, DIAG + py * 128 + 128)])
                    if len(pipe) >= 1:
                        s2_stage(*pipe.pop(0))
                    pipe.append((qb, a, py))
                    if qb > 0:
                        em("scalar", (lambda qq, b0, s0, pr: lambda e: e.activation(
                            pb[:, pr:pr + qq * 128], pp[:, b0:b0 + qq * 128], AF.Exp,
                            bias=stat[:, s0 + 1:s0 + 2], scale=SCALE,
                            accum_out=stat[:, s0 + 2:s0 + 3]))(qb, sb0, st0, prb),
                           reads=[("pp", sb0, sb0 + qb * 128), ("stat", st0 + 1, st0 + 2)],
                           writes=[("pb", prb, prb + qb * 128), ("stat", st0 + 2, st0 + 3)])
                    em("scalar", (lambda qq, s0, pyy, pr: lambda e: e.activation(
                        pb[:, pr + qq * 128:pr + (qq + 1) * 128],
                        arfs(DIAG + pyy * 128, 128), AF.Exp,
                        bias=stat[:, s0 + 1:s0 + 2], scale=SCALE,
                        accum_out=stat[:, s0 + 3:s0 + 4]))(qb, st0, py, prb),
                       reads=[("arf", DIAG + py * 128, DIAG + py * 128 + 128),
                              ("stat", st0 + 1, st0 + 2)],
                       writes=[("pb", prb + qb * 128, prb + (qb + 1) * 128),
                               ("stat", st0 + 3, st0 + 4)])
                    if qb > 0:
                        em("vector", (lambda s0: lambda e: e.tensor_tensor(
                            stat[:, s0 + 4:s0 + 5], stat[:, s0 + 2:s0 + 3],
                            stat[:, s0 + 3:s0 + 4], ALU.add))(st0),
                           reads=[("stat", st0 + 2, st0 + 4)], writes=[("stat", st0 + 4, st0 + 5)])
                    else:
                        em("vector", (lambda s0: lambda e: e.tensor_copy(
                            stat[:, s0 + 4:s0 + 5], stat[:, s0 + 3:s0 + 4]))(st0),
                           reads=[("stat", st0 + 3, st0 + 4)], writes=[("stat", st0 + 4, st0 + 5)])
                    em("vector", (lambda s0: lambda e: e.reciprocal(
                        stat[:, s0 + 5:s0 + 6], stat[:, s0 + 4:s0 + 5]))(st0),
                       reads=[("stat", st0 + 4, st0 + 5)], writes=[("stat", st0 + 5, st0 + 6)])
                    em("gpsimd", (lambda s0, pyy: lambda e: e.tensor_scalar(
                        dgb[:, pyy * 128:(pyy + 1) * 128], i128b[:],
                        stat[:, s0 + 5:s0 + 6], None, ALU.mult))(st0, py),
                       reads=[("i128b", 0, 128), ("stat", st0 + 5, st0 + 6)],
                       writes=[("dgb", py * 128, (py + 1) * 128)])
            for st_ in pipe:
                s2_stage(*st_)
            pipe = []


        emit_hp_loads(0)
        for tq4 in range(NT):
            for b in range(tq4 * 4, tq4 * 4 + 4):
                emit_ln_block(b)
            emit_hp_proj(0, tq4)
        for g in range(4):
            em("tensor", [(lambda bb, jj: lambda e: e.transpose(
                pp[0:8, 3584 + jj * 128:3584 + (jj + 1) * 128],
                wgate[:, bb * 8:(bb + 1) * 8], i128f[:]))(g * 4 + j, j)
                for j in range(4)],
               reads=[("wgate", g * 32, (g + 1) * 32), ("i128f", 0, 128)],
               writes=[("pp", 3584, 4096)])
            em("vector", (lambda gg: lambda e: e.tensor_copy(
                wrow[0:8, gg * 512:(gg + 1) * 512], pp[0:8, 3584:4096]))(g),
               reads=[("pp", 3584, 4096)], writes=[("wrow", g * 512, (g + 1) * 512)])
        emit_hp_vprod(0)
        for hp in range(NHP if DO_ATTN else 0):
            emit_hp_attn(hp)
            if hp + 1 < NHP:
                emit_hp_loads(hp + 1)
                emit_hp_proj(hp + 1)
                emit_hp_vprod(hp + 1)

        for mc in range(KC if DO_ATTN else 0):
            em("gpsimd", (lambda m: lambda e: e.dma_start(
                pbs(WOB[m % 2], 1024), wo_l[m]))(mc),
               writes=[("pb", WOB[mc % 2], WOB[mc % 2] + 1024)], dma="woS")
            for t in range(NT):
                bank = 6 + ((mc * NT + t) % 2)
                em("tensor", [(lambda m, tt, bk, kkc, fi, la: lambda e: e.matmul(
                    pps(BK(bk)),
                    pb[:, WOB[m % 2] + kkc * 128:WOB[m % 2] + (kkc + 1) * 128],
                    pb[:, AO + kkc * T + tt * 512:AO + kkc * T + (tt + 1) * 512],
                    start=fi, stop=la))(mc, t, bank, kc, kc == 0, kc == KC - 1)
                    for kc in range(KC)],
                   reads=[("pb", WOB[mc % 2], WOB[mc % 2] + 1024)] +
                         [("pb", AO + kc * T + t * 512, AO + kc * T + (t + 1) * 512)
                          for kc in range(KC)],
                   writes=[("pp",) + BK(bank)])
                em("vector", (lambda m, tt, bk: lambda e: e.tensor_tensor(
                    acc[:, m * T + tt * 512:m * T + (tt + 1) * 512],
                    acc[:, m * T + tt * 512:m * T + (tt + 1) * 512],
                    pps(BK(bk)), ALU.add))(mc, t, bank),
                   reads=[("acc", mc * T + t * 512, mc * T + (t + 1) * 512),
                          ("pp",) + BK(bank)],
                   writes=[("acc", mc * T + t * 512, mc * T + (t + 1) * 512)])

        final_caps = []

        def emit_o_block(b):
            base = (b % 2) * 1024
            ot = OT[b % 2]
            em("tensor", [(lambda bb, kkc, bs: lambda e: e.transpose(
                pp[:, bs + kkc * 128:bs + (kkc + 1) * 128],
                acc[:, kkc * T + bb * 128:kkc * T + (bb + 1) * 128], i128f[:]))(b, kc, base)
                for kc in range(KC)],
               reads=[("acc", kc * T + b * 128, kc * T + (b + 1) * 128) for kc in range(KC)] +
                     [("i128f", 0, 128)],
               writes=[("pp", base, base + 1024)])
            em("vector", (lambda bs, o: lambda e: e.tensor_copy(
                arfs(o, 1024), pp[:, bs:bs + 1024]))(base, ot),
               reads=[("pp", base, base + 1024)],
               writes=[("arf", ot, ot + 1024)])
            final_caps.append(em("sync", (lambda bb, o: lambda e: e.dma_start(
                out[bb * 128:(bb + 1) * 128, :], arfs(o, 1024)))(b, ot),
               reads=[("arf", ot, ot + 1024)], dma="do%d" % (b % 2)))

        o_done = set()
        for e_i in range(E if DO_FFN else 0):
            eb = e_i % 2
            for t in range(NT):
                em("tensor", (lambda ee, tt: lambda e: e.matmul(
                    pps(BK(6)), sel8[:, ee * 128:(ee + 1) * 128],
                    wrow[0:8, tt * 512:(tt + 1) * 512],
                    start=True, stop=True))(e_i, t),
                   reads=[("sel8", e_i * 128, (e_i + 1) * 128),
                          ("wrow", t * 512, (t + 1) * 512)],
                   writes=[("pp",) + BK(6)])
                em("scalar", (lambda ebb, tt: lambda e: e.copy(
                    arfs(WBC[ebb] + tt * 512, 512), pps(BK(6))))(eb, t),
                   reads=[("pp",) + BK(6)],
                   writes=[("arf", WBC[eb] + t * 512, WBC[eb] + (t + 1) * 512)])

            for half in range(2):
                for c in range(half * HHALF, (half + 1) * HHALF):
                    cl = c - half * HHALF
                    cb = c % 2
                    b1 = 0 if cb == 0 else 2
                    b2 = 1 if cb == 0 else 3
                    em("gpsimd", (lambda ee, cc: lambda e: e.dma_start(
                        pbs(W1B[cc % 2], 1024), w1_l[ee, cc]))(e_i, c),
                       writes=[("pb", W1B[cb], W1B[cb] + 1024)], dma="w1%d" % cb)
                    em("gpsimd", (lambda ee, cc: lambda e: e.dma_start(
                        pbs(W2B[cc % 2], 1024), w2_l[ee, cc]))(e_i, c),
                       writes=[("pb", W2B[cb], W2B[cb] + 1024)], dma="w2%d" % cb)
                    for t in range(NT):
                        tb = t % 2
                        em("tensor", [(lambda cc, tt, bk, kkc, fi, la: lambda e: e.matmul(
                            pps(BK(bk)),
                            pb[:, W1B[cc % 2] + kkc * 128:W1B[cc % 2] + (kkc + 1) * 128],
                            xn[:, kkc * T + tt * 512:kkc * T + (tt + 1) * 512],
                            start=fi, stop=la))(c, t, b1, kc, kc == 0, kc == KC - 1)
                            for kc in range(KC)],
                           reads=[("pb", W1B[cb], W1B[cb] + 1024)] +
                                 [("xn", kc * T + t * 512, kc * T + (t + 1) * 512)
                                  for kc in range(KC)],
                           writes=[("pp",) + BK(b1)])
                        em("tensor", [(lambda cc, tt, bk, kkc, fi, la: lambda e: e.matmul(
                            pps(BK(bk)),
                            pb[:, W2B[cc % 2] + kkc * 128:W2B[cc % 2] + (kkc + 1) * 128],
                            xn[:, kkc * T + tt * 512:kkc * T + (tt + 1) * 512],
                            start=fi, stop=la))(c, t, b2, kc, kc == 0, kc == KC - 1)
                            for kc in range(KC)],
                           reads=[("pb", W2B[cb], W2B[cb] + 1024)] +
                                 [("xn", kc * T + t * 512, kc * T + (t + 1) * 512)
                                  for kc in range(KC)],
                           writes=[("pp",) + BK(b2)])
                        em("scalar", (lambda bk, tbb: lambda e: e.activation(
                            arfs(SIL[tbb], 512), pps(BK(bk)), AF.Silu))(b1, tb),
                           reads=[("pp",) + BK(b1)],
                           writes=[("arf", SIL[tb], SIL[tb] + 512)])
                        em("vector", (lambda bk, tbb, ebb, tt: lambda e: e.tensor_tensor(
                            arfs(H2S[tbb], 512), pps(BK(bk)),
                            arfs(WBC[ebb] + tt * 512, 512), ALU.mult))(b2, tb, eb, t),
                           reads=[("pp",) + BK(b2),
                                  ("arf", WBC[eb] + t * 512, WBC[eb] + (t + 1) * 512)],
                           writes=[("arf", H2S[tb], H2S[tb] + 512)])
                        em("vector", (lambda cll, tt, tbb: lambda e: e.tensor_tensor(
                            pb[:, HIDB + cll * T + tt * 512:HIDB + cll * T + (tt + 1) * 512],
                            arfs(SIL[tbb], 512), arfs(H2S[tbb], 512), ALU.mult))(cl, t, tb),
                           reads=[("arf", SIL[tb], SIL[tb] + 512),
                                  ("arf", H2S[tb], H2S[tb] + 512)],
                           writes=[("pb", HIDB + cl * T + t * 512,
                                    HIDB + cl * T + (t + 1) * 512)])
                for m in range(KC):
                    mb = m % 2
                    em("gpsimd", (lambda ee, hf, mm: lambda e: e.dma_start(
                        pbs(W3B[mm % 2], HHALF * 128),
                        w3_l[ee, mm, :, hf * HHALF * 128:(hf + 1) * HHALF * 128]))(e_i, half, m),
                       writes=[("pb", W3B[mb], W3B[mb] + HHALF * 128)], dma="w3%d" % mb)
                    for t in range(NT):
                        bank = 4 + ((m * NT + t) % 2)
                        em("tensor", [(lambda mm, tt, bk, cll, fi, la: lambda e: e.matmul(
                            pps(BK(bk)),
                            pb[:, W3B[mm % 2] + cll * 128:W3B[mm % 2] + (cll + 1) * 128],
                            pb[:, HIDB + cll * T + tt * 512:HIDB + cll * T + (tt + 1) * 512],
                            start=fi, stop=la))(m, t, bank, cl, cl == 0, cl == HHALF - 1)
                            for cl in range(HHALF)],
                           reads=[("pb", W3B[mb], W3B[mb] + HHALF * 128)] +
                                 [("pb", HIDB + cl * T + t * 512,
                                   HIDB + cl * T + (t + 1) * 512) for cl in range(HHALF)],
                           writes=[("pp",) + BK(bank)])
                        em("vector", (lambda mm, tt, bk: lambda e: e.tensor_tensor(
                            acc[:, mm * T + tt * 512:mm * T + (tt + 1) * 512],
                            acc[:, mm * T + tt * 512:mm * T + (tt + 1) * 512],
                            pps(BK(bk)), ALU.add))(m, t, bank),
                           reads=[("acc", m * T + t * 512, m * T + (t + 1) * 512),
                                  ("pp",) + BK(bank)],
                           writes=[("acc", m * T + t * 512, m * T + (t + 1) * 512)])
                        if (e_i == E - 1 and half == 1 and m == KC - 1):
                            for bb in range(t * 4, t * 4 + 4):
                                if bb not in o_done:
                                    o_done.add(bb)
                                    emit_o_block(bb)

        o_rest = [b for b in range(NB) if b not in o_done]
        for b in o_rest:
            base = (b % 2) * 1024
            ot = OT[b % 2]
            em("tensor", [(lambda bb, kkc, bs: lambda e: e.transpose(
                pp[:, bs + kkc * 128:bs + (kkc + 1) * 128],
                acc[:, kkc * T + bb * 128:kkc * T + (bb + 1) * 128], i128f[:]))(b, kc, base)
                for kc in range(KC)],
               reads=[("acc", kc * T + b * 128, kc * T + (b + 1) * 128) for kc in range(KC)] +
                     [("i128f", 0, 128)],
               writes=[("pp", base, base + 1024)])
            em("vector", (lambda bs, o: lambda e: e.tensor_copy(
                arfs(o, 1024), pp[:, bs:bs + 1024]))(base, ot),
               reads=[("pp", base, base + 1024)],
               writes=[("arf", ot, ot + 1024)])
            final_caps.append(em("sync", (lambda bb, o: lambda e: e.dma_start(
                out[bb * 128:(bb + 1) * 128, :], arfs(o, 1024)))(b, ot),
               reads=[("arf", ot, ot + 1024)], dma="do%d" % (b % 2)))

        def play(e, name):
            for (fns, waits, sem, inc_k) in B.ops[name]:
                for (s, v) in waits:
                    e.wait_ge(sems[s], v)
                r = None
                for fn in fns:
                    r = fn(e)
                r.then_inc(sems[sem], inc_k)
            if name == "sync":
                e.wait_ge(sems["do0"], B.cnt["do0"])
                e.wait_ge(sems["do1"], B.cnt["do1"])

        @block.sync
        def _(e):
            play(e, "sync")

        @block.gpsimd
        def _(e):
            play(e, "gpsimd")

        @block.tensor
        def _(e):
            play(e, "tensor")

        @block.vector
        def _(e):
            play(e, "vector")

        @block.scalar
        def _(e):
            play(e, "scalar")

    return nc


def _lhsT_tiles(w):
    return np.ascontiguousarray(
        w.reshape(8, 128, 8, 128).transpose(0, 3, 2, 1).reshape(8, 128, 1024))


def _swap_rows(w):
    return np.ascontiguousarray(w.reshape(-1, 2, w.shape[-1])[:, ::-1, :].reshape(w.shape))


def kernel(L0_attn_x, L0_attn_Wq, L0_attn_Wk, L0_attn_Wv, L0_attn_Wo,
           L0_ffn_Wg, L0_ffn_W1, L0_ffn_W2, L0_ffn_W3):
    x = np.ascontiguousarray(np.asarray(L0_attn_x, dtype=np.float32))
    Wq = np.asarray(L0_attn_Wq, dtype=np.float32)
    Wk = np.asarray(L0_attn_Wk, dtype=np.float32)
    Wv = np.asarray(L0_attn_Wv, dtype=np.float32)
    Wo = np.asarray(L0_attn_Wo, dtype=np.float32)
    Wg = np.asarray(L0_ffn_Wg, dtype=np.float32)
    W1 = np.asarray(L0_ffn_W1, dtype=np.float32)
    W2 = np.asarray(L0_ffn_W2, dtype=np.float32)
    W3 = np.asarray(L0_ffn_W3, dtype=np.float32)

    wq_l = _lhsT_tiles(Wq)
    wqs_l = _lhsT_tiles(_swap_rows(Wq))
    wk_l = _lhsT_tiles(Wk)
    wks_l = _lhsT_tiles(_swap_rows(Wk))
    wv_r = _lhsT_tiles(Wv)
    wo_l = _lhsT_tiles(Wo)
    wg_r = np.ascontiguousarray(Wg.reshape(8, 8, 128).transpose(2, 1, 0).reshape(128, 64))

    HPAD = HCH * 128
    W1p = np.zeros((E, HPAD, D), np.float32)
    W1p[:, :2730] = W1
    W2p = np.zeros((E, HPAD, D), np.float32)
    W2p[:, :2730] = W2
    W3p = np.zeros((E, D, HPAD), np.float32)
    W3p[:, :, :2730] = W3
    w1_l = np.ascontiguousarray(
        W1p.reshape(E, HCH, 128, 8, 128).transpose(0, 1, 4, 3, 2).reshape(E, HCH, 128, 1024))
    w2_l = np.ascontiguousarray(
        W2p.reshape(E, HCH, 128, 8, 128).transpose(0, 1, 4, 3, 2).reshape(E, HCH, 128, 1024))
    w3_l = np.ascontiguousarray(
        W3p.reshape(E, 8, 128, HCH, 128).transpose(0, 1, 4, 3, 2).reshape(E, 8, 128, HCH * 128))

    inv_freq = 1.0 / (10000.0 ** (np.arange(0, 64, 2, dtype=np.float64) / 64.0))
    tt = np.arange(T, dtype=np.float64)
    fr = np.outer(tt, inv_freq)
    cosv = np.cos(fr)
    sinv = np.sin(fr)
    p = np.arange(128)
    fi = (p % 64) // 2
    ct = cosv[:, fi].T.astype(np.float32)
    sgn = np.where(p % 2 == 0, -1.0, 1.0)
    st = (sinv[:, fi].T * sgn[:, None]).astype(np.float32)

    i128 = np.eye(128, dtype=np.float32)
    qi = np.arange(128)[:, None]
    ki = np.arange(128)[None, :]
    trineg = np.where(ki <= qi, 0.0, -1e30).astype(np.float32)
    sel8 = np.zeros((8, 1024), np.float32)
    for e_ in range(8):
        sel8[e_, e_ * 128:(e_ + 1) * 128] = 1.0

    shared = {
        "wq_l": wq_l, "wqs_l": wqs_l, "wk_l": wk_l, "wks_l": wks_l,
        "wv_r": wv_r, "wo_l": wo_l, "wg_r": wg_r,
        "w1_l": w1_l, "w2_l": w2_l, "w3_l": w3_l,
        "ct": ct, "st": st, "i128": i128, "trineg": trineg, "sel8": sel8,
    }
    in_maps = [dict(shared, x=np.ascontiguousarray(x[b])) for b in range(8)]
    nc = build_nc()
    res = run_bass_kernel_spmd(nc, in_maps, core_ids=list(range(8)))
    return np.stack([np.asarray(r["out"]) for r in res.results], axis=0)


# version 16
# speedup vs baseline: 1.0507x; 1.0186x over previous
import contextlib
import numpy as np
import concourse.bass as bass
import concourse.mybir as mybir
from concourse.bass_utils import run_bass_kernel_spmd

F32 = mybir.dt.float32
BF16 = mybir.dt.bfloat16
AF = mybir.ActivationFunctionType
ALU = mybir.AluOpType
AX = mybir.AxisListType

T = 2048
D = 1024
KC = 8
NB = 16
NT = 4
NHP = 8
E = 8
HCH = 22
HHALF = 11
EPS = 1e-5
SCALE = 0.125

DO_ATTN = True
DO_FFN = True

ENGS = ("sync", "gpsimd", "tensor", "vector", "scalar")
ENG_SEM = {"tensor": "tPE", "vector": "tVE", "scalar": "tSC",
           "sync": "tSY", "gpsimd": "tGP"}

AO = 0
QHS = 16384
KHS = 18432
VH = (20480, 22528)
PRB = 24576
ATT = 26624
WQB, WQSB, WKB, WKSB = 28672, 29696, 30720, 31744
WVB = 32768
WOB = (32768, 32768)
HIDB = 0
W1B = (28672, 29696)
W2B = (30720, 31744)
W3B = (24576, 25984)
POOLB_N = 35840
PRB_B = 33792

XT = (0, 1024)
XC = 2048
SQ = 3072
XNF = 4096
DIAG = 2048
RT1, RT2 = 0, 512
WBC = (0, 2048)
SIL = (4096, 4608)
H2S = (5120, 5632)
OT = (0, 1024)
ARF_N = 6144

GRAN = {"pp": 512, "stat": 8}


def BK(b, off=0, w=512):
    return (b * 512 + off, b * 512 + off + w)


class Builder:

    def __init__(self):
        self.ops = {e: [] for e in ENGS}
        import collections
        self.cnt = collections.defaultdict(int)
        self.wr = {}
        self.rd = {}
        self.waited = {e: {} for e in ENGS}

    def _slots(self, rng):
        name, lo, hi = rng
        g = GRAN.get(name, 128)
        return [(name, s) for s in range(lo // g, (hi + g - 1) // g)]

    def emit(self, eng, fns, reads=(), writes=(), dma=None, extra=()):
        if not isinstance(fns, (list, tuple)):
            fns = [fns]
        waits = {}

        def need(cap):
            if cap is None:
                return
            s, v = cap
            if v <= 0:
                return
            waits[s] = max(waits.get(s, 0), v)

        rslots = []
        for rng in reads:
            for sl in self._slots(rng):
                rslots.append(sl)
                need(self.wr.get(sl))
                if sl[0] == "pp":
                    for cap, eng2 in self.rd.get(sl, ()):
                        if eng2 != eng:
                            need(cap)
        wslots = []
        for rng in writes:
            for sl in self._slots(rng):
                wslots.append(sl)
                need(self.wr.get(sl))
                for cap, eng2 in self.rd.get(sl, ()):
                    need(cap)
        for cap in extra:
            need(cap)

        wd = self.waited[eng]
        final = []
        for s, v in sorted(waits.items()):
            if wd.get(s, 0) >= v:
                continue
            wd[s] = v
            final.append((s, v))

        sem = dma if dma is not None else ENG_SEM[eng]
        inc_k = 16 if dma is not None else 1
        self.cnt[sem] += inc_k
        cap = (sem, self.cnt[sem])

        for sl in rslots:
            self.rd.setdefault(sl, []).append((cap, eng))
        for sl in wslots:
            self.wr[sl] = cap
            self.rd[sl] = []

        self.ops[eng].append((fns, tuple(final), sem, inc_k))
        return cap


def build_nc():
    nc = bass.Bass("TRN2", target_bir_lowering=False, debug=False)
    dt = nc.dram_tensor
    x_in = dt("x", [T, D], F32, kind="ExternalInput").ap()
    wq_l = dt("wq_l", [NHP, 128, 1024], F32, kind="ExternalInput").ap()
    wqs_l = dt("wqs_l", [NHP, 128, 1024], F32, kind="ExternalInput").ap()
    wk_l = dt("wk_l", [NHP, 128, 1024], F32, kind="ExternalInput").ap()
    wks_l = dt("wks_l", [NHP, 128, 1024], F32, kind="ExternalInput").ap()
    wv_r = dt("wv_r", [NHP, 128, 1024], F32, kind="ExternalInput").ap()
    wo_l = dt("wo_l", [KC, 128, 1024], F32, kind="ExternalInput").ap()
    wg_r = dt("wg_r", [128, 64], F32, kind="ExternalInput").ap()
    w1_l = dt("w1_l", [E, HCH, 128, 1024], F32, kind="ExternalInput").ap()
    w2_l = dt("w2_l", [E, HCH, 128, 1024], F32, kind="ExternalInput").ap()
    w3_l = dt("w3_l", [E, KC, 128, HCH * 128], F32, kind="ExternalInput").ap()
    ct_in = dt("ct", [128, T], F32, kind="ExternalInput").ap()
    st_in = dt("st", [128, T], F32, kind="ExternalInput").ap()
    i128_in = dt("i128", [128, 128], F32, kind="ExternalInput").ap()
    trineg_in = dt("trineg", [128, 128], F32, kind="ExternalInput").ap()
    sel8_in = dt("sel8", [8, 1024], F32, kind="ExternalInput").ap()
    out = dt("out", [T, D], F32, kind="ExternalOutput").ap()

    with contextlib.ExitStack() as ctx:
        en = ctx.enter_context
        acc = en(nc.sbuf_tensor("acc", [128, KC * T], F32))
        xn = en(nc.sbuf_tensor("xn", [128, KC * T], BF16))
        pb = en(nc.sbuf_tensor("poolb", [128, POOLB_N], BF16))
        arf = en(nc.sbuf_tensor("arf", [128, ARF_N], F32))
        csb = en(nc.sbuf_tensor("csb", [128, 2 * T], BF16))
        wgate = en(nc.sbuf_tensor("wgate", [128, 128], F32))
        wrow = en(nc.sbuf_tensor("wrow", [8, T], BF16))
        i128f = en(nc.sbuf_tensor("i128f", [128, 128], F32))
        i128b = en(nc.sbuf_tensor("i128b", [128, 128], BF16))
        trineg = en(nc.sbuf_tensor("trineg_s", [128, 128], F32))
        sel8 = en(nc.sbuf_tensor("sel8_s", [8, 1024], BF16))
        wgr = en(nc.sbuf_tensor("wgr", [128, 64], F32))
        dgb = en(nc.sbuf_tensor("dgb", [128, 256], BF16))
        stat = en(nc.sbuf_tensor("stat", [128, 64], F32))
        pp = en(nc.psum_tensor("pp", [128, 4096], F32))
        sem_names = ("tPE", "tVE", "tSC", "tSY", "tGP",
                     "cI128F", "cTRI", "cWGR", "cCT", "cST", "cI128B", "cSEL",
                     "dx0", "dx1", "wqS", "wqsS", "wkS", "wksS", "wvS", "woS",
                     "w10", "w11", "w20", "w21", "w30", "w31", "do0", "do1")
        sems = {n: en(nc.semaphore(n)) for n in sem_names}
        block = en(nc.Block())

        B = Builder()
        em = B.emit

        def arfs(off, w):
            return arf[:, off:off + w]

        def pbs(off, w):
            return pb[:, off:off + w]

        def pps(lo_hi):
            lo, hi = lo_hi
            return pp[:, lo:hi]

        em("sync", lambda e: e.dma_start(i128f[:], i128_in),
           writes=[("i128f", 0, 128)], dma="cI128F")
        em("sync", lambda e: e.dma_start(trineg[:], trineg_in),
           writes=[("trineg", 0, 128)], dma="cTRI")
        em("sync", lambda e: e.dma_start(wgr[:], wg_r),
           writes=[("wgr", 0, 64)], dma="cWGR")
        em("gpsimd", lambda e: e.dma_start(csb[:, 0:T], ct_in),
           writes=[("csb", 0, T)], dma="cCT")
        em("gpsimd", lambda e: e.dma_start(csb[:, T:2 * T], st_in),
           writes=[("csb", T, 2 * T)], dma="cST")
        em("gpsimd", lambda e: e.dma_start(i128b[:], i128_in),
           writes=[("i128b", 0, 128)], dma="cI128B")
        em("gpsimd", lambda e: e.dma_start(sel8[:], sel8_in),
           writes=[("sel8", 0, 1024)], dma="cSEL")

        def emit_ln_block(b):
            bf = b % 2
            xt0 = XT[bf]
            sb = bf * 32
            em("sync", (lambda bb, o: lambda e: e.dma_start(
                arfs(o, 1024), x_in[bb * 128:(bb + 1) * 128, :]))(b, xt0),
               writes=[("arf", xt0, xt0 + 1024)], dma="dx%d" % bf)
            em("vector", (lambda o, ss: lambda e: e.reduce_sum(
                stat[:, ss:ss + 1], arfs(o, 1024), axis=AX.X))(xt0, sb),
               reads=[("arf", xt0, xt0 + 1024)], writes=[("stat", sb, sb + 1)])
            em("vector", (lambda ss: lambda e: e.tensor_scalar(
                stat[:, ss + 1:ss + 2], stat[:, ss:ss + 1], -1.0 / D, None, ALU.mult))(sb),
               reads=[("stat", sb, sb + 1)], writes=[("stat", sb + 1, sb + 2)])
            em("vector", (lambda o, ss: lambda e: e.tensor_scalar(
                arfs(XC, 1024), arfs(o, 1024), stat[:, ss + 1:ss + 2], None, ALU.add))(xt0, sb),
               reads=[("arf", xt0, xt0 + 1024), ("stat", sb + 1, sb + 2)],
               writes=[("arf", XC, XC + 1024)])
            em("scalar", (lambda ss: lambda e: e.activation(
                arfs(SQ, 1024), arfs(XC, 1024), AF.Square,
                accum_out=stat[:, ss + 2:ss + 3]))(sb),
               reads=[("arf", XC, XC + 1024)],
               writes=[("arf", SQ, SQ + 1024), ("stat", sb + 2, sb + 3)])
            em("vector", (lambda ss: lambda e: e.tensor_scalar(
                stat[:, ss + 3:ss + 4], stat[:, ss + 2:ss + 3], 1.0 / D, EPS,
                ALU.mult, ALU.add))(sb),
               reads=[("stat", sb + 2, sb + 3)], writes=[("stat", sb + 3, sb + 4)])
            em("scalar", (lambda ss: lambda e: e.activation(
                stat[:, ss + 4:ss + 5], stat[:, ss + 3:ss + 4], AF.Sqrt))(sb),
               reads=[("stat", sb + 3, sb + 4)], writes=[("stat", sb + 4, sb + 5)])
            em("vector", (lambda ss: lambda e: e.reciprocal(
                stat[:, ss + 5:ss + 6], stat[:, ss + 4:ss + 5]))(sb),
               reads=[("stat", sb + 4, sb + 5)], writes=[("stat", sb + 5, sb + 6)])
            em("vector", (lambda ss: lambda e: e.tensor_scalar(
                arfs(XC, 1024), arfs(XC, 1024), stat[:, ss + 5:ss + 6], None, ALU.mult))(sb),
               reads=[("arf", XC, XC + 1024), ("stat", sb + 5, sb + 6)],
               writes=[("arf", XC, XC + 1024)])
            em("tensor", [(lambda kkc: lambda e: e.transpose(
                pp[:, 2048 + kkc * 128:2048 + (kkc + 1) * 128],
                arf[:, XC + kkc * 128:XC + (kkc + 1) * 128], i128f[:]))(kc)
                for kc in range(KC)],
               reads=[("arf", XC, XC + 1024), ("i128f", 0, 128)],
               writes=[("pp", 2048, 3072)])
            em("scalar", (lambda bb: lambda e: e.copy(
                xn[:].rearrange("p (k t) -> p k t", k=KC)[:, :, bb * 128:(bb + 1) * 128],
                pp[:, 2048:3072].rearrange("p (k t) -> p k t", k=KC)))(b),
               reads=[("pp", 2048, 3072)],
               writes=[("xn", kc * T + b * 128, kc * T + (b + 1) * 128) for kc in range(KC)])
            em("vector", lambda e: e.tensor_copy(arfs(XNF, 1024), pp[:, 2048:3072]),
               reads=[("pp", 2048, 3072)], writes=[("arf", XNF, XNF + 1024)])
            em("tensor", [(lambda kkc, fi, la: lambda e: e.matmul(
                pp[:, 2048:2056], arf[:, XNF + kkc * 128:XNF + (kkc + 1) * 128],
                wgr[:, kkc * 8:(kkc + 1) * 8], start=fi, stop=la))(kc, kc == 0, kc == KC - 1)
                for kc in range(KC)],
               reads=[("arf", XNF, XNF + 1024), ("wgr", 0, 64)],
               writes=[("pp", 2048, 2560)])
            em("tensor", [(lambda o, kkc: lambda e: e.transpose(
                pp[:, 3072 + kkc * 128:3072 + (kkc + 1) * 128],
                arf[:, o + kkc * 128:o + (kkc + 1) * 128], i128f[:]))(xt0, kc)
                for kc in range(KC)],
               reads=[("arf", xt0, xt0 + 1024), ("i128f", 0, 128)],
               writes=[("pp", 3072, 4096)])
            em("vector", (lambda bb: lambda e: e.tensor_copy(
                acc[:].rearrange("p (k t) -> p k t", k=KC)[:, :, bb * 128:(bb + 1) * 128],
                pp[:, 3072:4096].rearrange("p (k t) -> p k t", k=KC)))(b),
               reads=[("pp", 3072, 4096)],
               writes=[("acc", kc * T + b * 128, kc * T + (b + 1) * 128) for kc in range(KC)])
            gread = [("pp", 2048, 2560)]
            em("vector", (lambda ss: lambda e: e.reduce_max(
                stat[:, ss + 6:ss + 7], pp[:, 2048:2056], axis=AX.X))(sb),
               reads=gread, writes=[("stat", sb + 6, sb + 7)])
            em("vector", (lambda ss: lambda e: e.tensor_scalar(
                stat[:, ss + 7:ss + 8], stat[:, ss + 6:ss + 7], -1.0, None, ALU.mult))(sb),
               reads=[("stat", sb + 6, sb + 7)], writes=[("stat", sb + 7, sb + 8)])
            em("vector", (lambda ss: lambda e: e.tensor_scalar(
                stat[:, ss + 8:ss + 16], pp[:, 2048:2056], stat[:, ss + 6:ss + 7], None,
                ALU.is_ge))(sb),
               reads=gread + [("stat", sb + 6, sb + 7)], writes=[("stat", sb + 8, sb + 16)])
            em("vector", (lambda ss: lambda e: e.scalar_tensor_tensor(
                stat[:, ss + 16:ss + 24], stat[:, ss + 8:ss + 16], -1e30, pp[:, 2048:2056],
                ALU.mult, ALU.add))(sb),
               reads=gread + [("stat", sb + 8, sb + 16)], writes=[("stat", sb + 16, sb + 24)])
            em("vector", (lambda ss: lambda e: e.reduce_max(
                stat[:, ss + 24:ss + 25], stat[:, ss + 16:ss + 24], axis=AX.X))(sb),
               reads=[("stat", sb + 16, sb + 24)], writes=[("stat", sb + 24, sb + 25)])
            em("scalar", (lambda ss: lambda e: e.activation(
                stat[:, ss + 25:ss + 26], stat[:, ss + 24:ss + 25], AF.Exp,
                bias=stat[:, ss + 7:ss + 8]))(sb),
               reads=[("stat", sb + 24, sb + 25), ("stat", sb + 7, sb + 8)],
               writes=[("stat", sb + 25, sb + 26)])
            em("vector", (lambda ss: lambda e: e.tensor_scalar(
                stat[:, ss + 26:ss + 27], stat[:, ss + 25:ss + 26], 1.0, None, ALU.add))(sb),
               reads=[("stat", sb + 25, sb + 26)], writes=[("stat", sb + 26, sb + 27)])
            em("vector", (lambda ss: lambda e: e.reciprocal(
                stat[:, ss + 27:ss + 28], stat[:, ss + 26:ss + 27]))(sb),
               reads=[("stat", sb + 26, sb + 27)], writes=[("stat", sb + 27, sb + 28)])
            em("scalar", (lambda ss: lambda e: e.activation(
                stat[:, ss + 16:ss + 24], pp[:, 2048:2056], AF.Exp,
                bias=stat[:, ss + 7:ss + 8]))(sb),
               reads=gread + [("stat", sb + 7, sb + 8)], writes=[("stat", sb + 16, sb + 24)])
            em("vector", (lambda ss: lambda e: e.tensor_scalar(
                stat[:, ss + 8:ss + 16], pp[:, 2048:2056], stat[:, ss + 24:ss + 25], None,
                ALU.is_ge))(sb),
               reads=gread + [("stat", sb + 24, sb + 25)], writes=[("stat", sb + 8, sb + 16)])
            em("vector", (lambda ss: lambda e: e.tensor_tensor(
                stat[:, ss + 8:ss + 16], stat[:, ss + 16:ss + 24],
                stat[:, ss + 8:ss + 16], ALU.mult))(sb),
               reads=[("stat", sb + 8, sb + 24)], writes=[("stat", sb + 8, sb + 16)])
            em("vector", (lambda bb, ss: lambda e: e.tensor_scalar(
                wgate[:, bb * 8:(bb + 1) * 8], stat[:, ss + 8:ss + 16],
                stat[:, ss + 27:ss + 28], None, ALU.mult))(b, sb),
               reads=[("stat", sb + 8, sb + 16), ("stat", sb + 27, sb + 28)],
               writes=[("wgate", b * 8, (b + 1) * 8)])

        def emit_hp_loads(hp):
            for buf, warr, wsem in ((WQB, wq_l, "wqS"), (WQSB, wqs_l, "wqsS"),
                                    (WKB, wk_l, "wkS"), (WKSB, wks_l, "wksS"),
                                    (WVB, wv_r, "wvS")):
                em("gpsimd", (lambda bb, ss, h: lambda e: e.dma_start(
                    pbs(bb, 1024), ss[h]))(buf, warr, hp),
                   writes=[("pb", buf, buf + 1024)], dma=wsem)


        def emit_hp_proj(hp, tq_only=None):
            for tq in (range(NT) if tq_only is None else [tq_only]):
                c0, c1 = tq * 512, (tq + 1) * 512
                for bank, wbuf in ((0, WQB), (1, WQSB), (2, WKB), (3, WKSB)):
                    em("tensor", [(lambda bk, wb, kkc, fi, la, a0, a1: lambda e: e.matmul(
                        pps(BK(bk)), pb[:, wb + kkc * 128:wb + (kkc + 1) * 128],
                        xn[:, kkc * T + a0:kkc * T + a1],
                        start=fi, stop=la))(bank, wbuf, kc, kc == 0, kc == KC - 1, c0, c1)
                        for kc in range(KC)],
                       reads=[("pb", wbuf, wbuf + 1024)] +
                             [("xn", kc * T + c0, kc * T + c1) for kc in range(KC)],
                       writes=[("pp",) + BK(bank)])
                em("vector", (lambda a0, a1: lambda e: e.tensor_tensor(
                    arfs(RT1, 512), pps(BK(0)), csb[:, a0:a1], ALU.mult))(c0, c1),
                   reads=[("pp",) + BK(0), ("csb", c0, c1)],
                   writes=[("arf", RT1, RT1 + 512)])
                em("vector", (lambda a0, a1: lambda e: e.tensor_tensor(
                    arfs(RT2, 512), pps(BK(1)), csb[:, T + a0:T + a1], ALU.mult))(c0, c1),
                   reads=[("pp",) + BK(1), ("csb", T + c0, T + c1)],
                   writes=[("arf", RT2, RT2 + 512)])
                em("vector", (lambda a0, a1: lambda e: e.tensor_tensor(
                    pb[:, QHS + a0:QHS + a1], arfs(RT1, 512), arfs(RT2, 512),
                    ALU.add))(c0, c1),
                   reads=[("arf", RT1, RT1 + 512), ("arf", RT2, RT2 + 512)],
                   writes=[("pb", QHS + c0, QHS + c1)])
                em("vector", (lambda a0, a1: lambda e: e.tensor_tensor(
                    arfs(RT1, 512), pps(BK(2)), csb[:, a0:a1], ALU.mult))(c0, c1),
                   reads=[("pp",) + BK(2), ("csb", c0, c1)],
                   writes=[("arf", RT1, RT1 + 512)])
                em("vector", (lambda a0, a1: lambda e: e.tensor_tensor(
                    arfs(RT2, 512), pps(BK(3)), csb[:, T + a0:T + a1], ALU.mult))(c0, c1),
                   reads=[("pp",) + BK(3), ("csb", T + c0, T + c1)],
                   writes=[("arf", RT2, RT2 + 512)])
                em("vector", (lambda a0, a1: lambda e: e.tensor_tensor(
                    pb[:, KHS + a0:KHS + a1], arfs(RT1, 512), arfs(RT2, 512),
                    ALU.add))(c0, c1),
                   reads=[("arf", RT1, RT1 + 512), ("arf", RT2, RT2 + 512)],
                   writes=[("pb", KHS + c0, KHS + c1)])


        def emit_hp_vprod(hp):
            vb = VH[hp % 2]
            for g4 in range(4):
                bank = 4 + (g4 % 2)
                fns = []
                for j in range(4):
                    blk = g4 * 4 + j
                    for kc in range(KC):
                        fns.append((lambda bk, jj, bb, kkc, fi, la: lambda e: e.matmul(
                            pp[:, BK(bk)[0] + jj * 128:BK(bk)[0] + (jj + 1) * 128],
                            xn[:, kkc * T + bb * 128:kkc * T + (bb + 1) * 128],
                            pb[:, WVB + kkc * 128:WVB + (kkc + 1) * 128],
                            start=fi, stop=la))(bank, j, blk, kc, kc == 0, kc == KC - 1))
                em("tensor", fns,
                   reads=[("pb", WVB, WVB + 1024)] +
                         [("xn", kc * T + g4 * 512, kc * T + (g4 + 1) * 512) for kc in range(KC)],
                   writes=[("pp",) + BK(bank)])
                em("scalar", (lambda vbb, gg, bk: lambda e: e.copy(
                    pb[:, vbb + gg * 512:vbb + (gg + 1) * 512], pps(BK(bk))))(vb, g4, bank),
                   reads=[("pp",) + BK(bank)],
                   writes=[("pb", vb + g4 * 512, vb + (g4 + 1) * 512)])


        def emit_hp_attn(hp):
            vb = VH[hp % 2]
            def emit_dummies(n):
                for _ in range(n):
                    em("tensor", lambda e: e.matmul(
                        pp[:, 3584:4096], i128b[:], csb[:, 0:512],
                        start=True, stop=True),
                       reads=[("i128b", 0, 128), ("csb", 0, 512)],
                       writes=[("pp", 3584, 4096)])

            def s2_stage(q, a, py):
                prb = PRB if py == 0 else PRB_B
                L = (q + 1) * 128
                ngrp = (q + 4) // 4
                emit_dummies(4)
                for g in range(ngrp):
                    bank = 4 + (g % 2)
                    kb0 = g * 4
                    kbn = min(4, q + 1 - kb0)
                    em("tensor", [(lambda bk, jj, kk, pr, pyy: lambda e: e.matmul(
                        pp[:, BK(bk)[0] + jj * 128:BK(bk)[0] + (jj + 1) * 128],
                        pb[:, pr + kk * 128:pr + (kk + 1) * 128],
                        dgb[:, pyy * 128:(pyy + 1) * 128],
                        start=True, stop=True))(bank, j, kb0 + j, prb, py)
                        for j in range(kbn)],
                       reads=[("pb", prb + kb0 * 128, prb + (kb0 + kbn) * 128),
                              ("dgb", py * 128, (py + 1) * 128)],
                       writes=[("pp",) + BK(bank)])
                    em("vector", (lambda gg, bk, n: lambda e: e.tensor_copy(
                        pb[:, ATT + gg * 512:ATT + gg * 512 + n * 128],
                        pp[:, BK(bk)[0]:BK(bk)[0] + n * 128]))(g, bank, kbn),
                       reads=[("pp",) + BK(bank)],
                       writes=[("pb", ATT + kb0 * 128, ATT + (kb0 + kbn) * 128)])
                reg = 3072 + (q % 2) * 128
                emit_dummies(2)
                fns = []
                for kb in range(q + 1):
                    tp = (0, 64) if a == 1 else None
                    fns.append((lambda ax, kk, rr, fi, la, tpp, vbb: lambda e: e.matmul(
                        pp[64 * ax:64 * ax + 64, rr:rr + 128],
                        pb[:, vbb + kk * 128 + 64 * ax:vbb + kk * 128 + 64 * ax + 64],
                        pb[:, ATT + kk * 128:ATT + (kk + 1) * 128],
                        start=fi, stop=la, tile_position=tpp))(
                            a, kb, reg, kb == 0, kb == q, tp, vb))
                em("tensor", fns,
                   reads=[("pb", vb, vb + (q + 1) * 128),
                          ("pb", ATT, ATT + L)],
                   writes=[("pp", reg, reg + 128)])
                if a == 1:
                    em("scalar", (lambda hh, qq, rr: lambda e: e.copy(
                        pb[:, AO + hh * T + qq * 128:AO + hh * T + (qq + 1) * 128],
                        pp[:, rr:rr + 128]))(hp, q, reg),
                       reads=[("pp", reg, reg + 128)],
                       writes=[("pb", AO + hp * T + q * 128, AO + hp * T + (q + 1) * 128)])

            prev = None
            for qb in range(NB):
                for a in range(2):
                    u = qb * 2 + a
                    py = u % 2
                    st0 = 16 * py
                    prb = PRB if py == 0 else PRB_B
                    L = (qb + 1) * 128
                    nch = (L + 511) // 512
                    sb0 = 1024 if (a == 1 and qb < 8) else 0
                    for c in range(nch):
                        w = min(512, L - c * 512)
                        em("tensor", (lambda aa, qq, cc, ww, b0: lambda e: e.matmul(
                            pp[:, b0 + cc * 512:b0 + cc * 512 + ww],
                            pb[64 * aa:64 * aa + 64, QHS + qq * 128:QHS + (qq + 1) * 128],
                            pb[64 * aa:64 * aa + 64, KHS + cc * 512:KHS + cc * 512 + ww],
                            start=True, stop=True))(a, qb, c, w, sb0),
                           reads=[("pb", QHS + qb * 128, QHS + (qb + 1) * 128),
                                  ("pb", KHS + c * 512, KHS + c * 512 + w)],
                           writes=[("pp", sb0 + c * 512, sb0 + c * 512 + w)])
                    em("vector", (lambda LL, b0, s0: lambda e: e.reduce_max(
                        stat[:, s0:s0 + 1], pp[:, b0:b0 + LL], axis=AX.X,
                        negate=True))(L, sb0, st0),
                       reads=[("pp", sb0, sb0 + L)], writes=[("stat", st0, st0 + 1)])
                    em("vector", (lambda s0: lambda e: e.tensor_scalar(
                        stat[:, s0 + 1:s0 + 2], stat[:, s0:s0 + 1], SCALE, None, ALU.mult))(st0),
                       reads=[("stat", st0, st0 + 1)], writes=[("stat", st0 + 1, st0 + 2)])
                    em("vector", (lambda qq, b0, pyy: lambda e: e.tensor_tensor(
                        arfs(DIAG + pyy * 128, 128), pp[:, b0 + qq * 128:b0 + (qq + 1) * 128],
                        trineg[:], ALU.add))(qb, sb0, py),
                       reads=[("pp", sb0 + qb * 128, sb0 + (qb + 1) * 128), ("trineg", 0, 128)],
                       writes=[("arf", DIAG + py * 128, DIAG + py * 128 + 128)])
                    if qb > 0:
                        em("scalar", (lambda qq, b0, s0, pr: lambda e: e.activation(
                            pb[:, pr:pr + qq * 128], pp[:, b0:b0 + qq * 128], AF.Exp,
                            bias=stat[:, s0 + 1:s0 + 2], scale=SCALE,
                            accum_out=stat[:, s0 + 2:s0 + 3]))(qb, sb0, st0, prb),
                           reads=[("pp", sb0, sb0 + qb * 128), ("stat", st0 + 1, st0 + 2)],
                           writes=[("pb", prb, prb + qb * 128), ("stat", st0 + 2, st0 + 3)])
                    em("scalar", (lambda qq, s0, pyy, pr: lambda e: e.activation(
                        pb[:, pr + qq * 128:pr + (qq + 1) * 128],
                        arfs(DIAG + pyy * 128, 128), AF.Exp,
                        bias=stat[:, s0 + 1:s0 + 2], scale=SCALE,
                        accum_out=stat[:, s0 + 3:s0 + 4]))(qb, st0, py, prb),
                       reads=[("arf", DIAG + py * 128, DIAG + py * 128 + 128),
                              ("stat", st0 + 1, st0 + 2)],
                       writes=[("pb", prb + qb * 128, prb + (qb + 1) * 128),
                               ("stat", st0 + 3, st0 + 4)])
                    if prev is not None:
                        s2_stage(*prev)
                    prev = (qb, a, py)
                    if qb > 0:
                        em("vector", (lambda s0: lambda e: e.tensor_tensor(
                            stat[:, s0 + 4:s0 + 5], stat[:, s0 + 2:s0 + 3],
                            stat[:, s0 + 3:s0 + 4], ALU.add))(st0),
                           reads=[("stat", st0 + 2, st0 + 4)], writes=[("stat", st0 + 4, st0 + 5)])
                    else:
                        em("vector", (lambda s0: lambda e: e.tensor_copy(
                            stat[:, s0 + 4:s0 + 5], stat[:, s0 + 3:s0 + 4]))(st0),
                           reads=[("stat", st0 + 3, st0 + 4)], writes=[("stat", st0 + 4, st0 + 5)])
                    em("vector", (lambda s0: lambda e: e.reciprocal(
                        stat[:, s0 + 5:s0 + 6], stat[:, s0 + 4:s0 + 5]))(st0),
                       reads=[("stat", st0 + 4, st0 + 5)], writes=[("stat", st0 + 5, st0 + 6)])
                    em("vector", (lambda s0, pyy: lambda e: e.tensor_scalar(
                        dgb[:, pyy * 128:(pyy + 1) * 128], i128b[:],
                        stat[:, s0 + 5:s0 + 6], None, ALU.mult))(st0, py),
                       reads=[("i128b", 0, 128), ("stat", st0 + 5, st0 + 6)],
                       writes=[("dgb", py * 128, (py + 1) * 128)])
            if prev is not None:
                s2_stage(*prev)
                prev = None


        emit_hp_loads(0)
        for tq4 in range(NT):
            for b in range(tq4 * 4, tq4 * 4 + 4):
                emit_ln_block(b)
            emit_hp_proj(0, tq4)
        for g in range(4):
            em("tensor", [(lambda bb, jj: lambda e: e.transpose(
                pp[0:8, 3584 + jj * 128:3584 + (jj + 1) * 128],
                wgate[:, bb * 8:(bb + 1) * 8], i128f[:]))(g * 4 + j, j)
                for j in range(4)],
               reads=[("wgate", g * 32, (g + 1) * 32), ("i128f", 0, 128)],
               writes=[("pp", 3584, 4096)])
            em("vector", (lambda gg: lambda e: e.tensor_copy(
                wrow[0:8, gg * 512:(gg + 1) * 512], pp[0:8, 3584:4096]))(g),
               reads=[("pp", 3584, 4096)], writes=[("wrow", g * 512, (g + 1) * 512)])
        emit_hp_vprod(0)
        for hp in range(NHP if DO_ATTN else 0):
            emit_hp_attn(hp)
            if hp + 1 < NHP:
                emit_hp_loads(hp + 1)
                emit_hp_proj(hp + 1)
                emit_hp_vprod(hp + 1)

        for mc in range(KC if DO_ATTN else 0):
            em("gpsimd", (lambda m: lambda e: e.dma_start(
                pbs(WOB[m % 2], 1024), wo_l[m]))(mc),
               writes=[("pb", WOB[mc % 2], WOB[mc % 2] + 1024)], dma="woS")
            for t in range(NT):
                bank = 6 + ((mc * NT + t) % 2)
                em("tensor", [(lambda m, tt, bk, kkc, fi, la: lambda e: e.matmul(
                    pps(BK(bk)),
                    pb[:, WOB[m % 2] + kkc * 128:WOB[m % 2] + (kkc + 1) * 128],
                    pb[:, AO + kkc * T + tt * 512:AO + kkc * T + (tt + 1) * 512],
                    start=fi, stop=la))(mc, t, bank, kc, kc == 0, kc == KC - 1)
                    for kc in range(KC)],
                   reads=[("pb", WOB[mc % 2], WOB[mc % 2] + 1024)] +
                         [("pb", AO + kc * T + t * 512, AO + kc * T + (t + 1) * 512)
                          for kc in range(KC)],
                   writes=[("pp",) + BK(bank)])
                em("vector", (lambda m, tt, bk: lambda e: e.tensor_tensor(
                    acc[:, m * T + tt * 512:m * T + (tt + 1) * 512],
                    acc[:, m * T + tt * 512:m * T + (tt + 1) * 512],
                    pps(BK(bk)), ALU.add))(mc, t, bank),
                   reads=[("acc", mc * T + t * 512, mc * T + (t + 1) * 512),
                          ("pp",) + BK(bank)],
                   writes=[("acc", mc * T + t * 512, mc * T + (t + 1) * 512)])

        final_caps = []

        def emit_o_block(b):
            base = (b % 2) * 1024
            ot = OT[b % 2]
            em("tensor", [(lambda bb, kkc, bs: lambda e: e.transpose(
                pp[:, bs + kkc * 128:bs + (kkc + 1) * 128],
                acc[:, kkc * T + bb * 128:kkc * T + (bb + 1) * 128], i128f[:]))(b, kc, base)
                for kc in range(KC)],
               reads=[("acc", kc * T + b * 128, kc * T + (b + 1) * 128) for kc in range(KC)] +
                     [("i128f", 0, 128)],
               writes=[("pp", base, base + 1024)])
            em("vector", (lambda bs, o: lambda e: e.tensor_copy(
                arfs(o, 1024), pp[:, bs:bs + 1024]))(base, ot),
               reads=[("pp", base, base + 1024)],
               writes=[("arf", ot, ot + 1024)])
            final_caps.append(em("sync", (lambda bb, o: lambda e: e.dma_start(
                out[bb * 128:(bb + 1) * 128, :], arfs(o, 1024)))(b, ot),
               reads=[("arf", ot, ot + 1024)], dma="do%d" % (b % 2)))

        o_done = set()
        for e_i in range(E if DO_FFN else 0):
            eb = e_i % 2
            for t in range(NT):
                em("tensor", (lambda ee, tt: lambda e: e.matmul(
                    pps(BK(6)), sel8[:, ee * 128:(ee + 1) * 128],
                    wrow[0:8, tt * 512:(tt + 1) * 512],
                    start=True, stop=True))(e_i, t),
                   reads=[("sel8", e_i * 128, (e_i + 1) * 128),
                          ("wrow", t * 512, (t + 1) * 512)],
                   writes=[("pp",) + BK(6)])
                em("scalar", (lambda ebb, tt: lambda e: e.copy(
                    arfs(WBC[ebb] + tt * 512, 512), pps(BK(6))))(eb, t),
                   reads=[("pp",) + BK(6)],
                   writes=[("arf", WBC[eb] + t * 512, WBC[eb] + (t + 1) * 512)])

            for half in range(2):
                for c in range(half * HHALF, (half + 1) * HHALF):
                    cl = c - half * HHALF
                    cb = c % 2
                    b1 = 0 if cb == 0 else 2
                    b2 = 1 if cb == 0 else 3
                    em("gpsimd", (lambda ee, cc: lambda e: e.dma_start(
                        pbs(W1B[cc % 2], 1024), w1_l[ee, cc]))(e_i, c),
                       writes=[("pb", W1B[cb], W1B[cb] + 1024)], dma="w1%d" % cb)
                    em("gpsimd", (lambda ee, cc: lambda e: e.dma_start(
                        pbs(W2B[cc % 2], 1024), w2_l[ee, cc]))(e_i, c),
                       writes=[("pb", W2B[cb], W2B[cb] + 1024)], dma="w2%d" % cb)
                    for t in range(NT):
                        tb = t % 2
                        em("tensor", [(lambda cc, tt, bk, kkc, fi, la: lambda e: e.matmul(
                            pps(BK(bk)),
                            pb[:, W1B[cc % 2] + kkc * 128:W1B[cc % 2] + (kkc + 1) * 128],
                            xn[:, kkc * T + tt * 512:kkc * T + (tt + 1) * 512],
                            start=fi, stop=la))(c, t, b1, kc, kc == 0, kc == KC - 1)
                            for kc in range(KC)],
                           reads=[("pb", W1B[cb], W1B[cb] + 1024)] +
                                 [("xn", kc * T + t * 512, kc * T + (t + 1) * 512)
                                  for kc in range(KC)],
                           writes=[("pp",) + BK(b1)])
                        em("tensor", [(lambda cc, tt, bk, kkc, fi, la: lambda e: e.matmul(
                            pps(BK(bk)),
                            pb[:, W2B[cc % 2] + kkc * 128:W2B[cc % 2] + (kkc + 1) * 128],
                            xn[:, kkc * T + tt * 512:kkc * T + (tt + 1) * 512],
                            start=fi, stop=la))(c, t, b2, kc, kc == 0, kc == KC - 1)
                            for kc in range(KC)],
                           reads=[("pb", W2B[cb], W2B[cb] + 1024)] +
                                 [("xn", kc * T + t * 512, kc * T + (t + 1) * 512)
                                  for kc in range(KC)],
                           writes=[("pp",) + BK(b2)])
                        em("scalar", (lambda bk, tbb: lambda e: e.activation(
                            arfs(SIL[tbb], 512), pps(BK(bk)), AF.Silu))(b1, tb),
                           reads=[("pp",) + BK(b1)],
                           writes=[("arf", SIL[tb], SIL[tb] + 512)])
                        em("vector", (lambda bk, tbb, ebb, tt: lambda e: e.tensor_tensor(
                            arfs(H2S[tbb], 512), pps(BK(bk)),
                            arfs(WBC[ebb] + tt * 512, 512), ALU.mult))(b2, tb, eb, t),
                           reads=[("pp",) + BK(b2),
                                  ("arf", WBC[eb] + t * 512, WBC[eb] + (t + 1) * 512)],
                           writes=[("arf", H2S[tb], H2S[tb] + 512)])
                        em("vector", (lambda cll, tt, tbb: lambda e: e.tensor_tensor(
                            pb[:, HIDB + cll * T + tt * 512:HIDB + cll * T + (tt + 1) * 512],
                            arfs(SIL[tbb], 512), arfs(H2S[tbb], 512), ALU.mult))(cl, t, tb),
                           reads=[("arf", SIL[tb], SIL[tb] + 512),
                                  ("arf", H2S[tb], H2S[tb] + 512)],
                           writes=[("pb", HIDB + cl * T + t * 512,
                                    HIDB + cl * T + (t + 1) * 512)])
                for m in range(KC):
                    mb = m % 2
                    em("gpsimd", (lambda ee, hf, mm: lambda e: e.dma_start(
                        pbs(W3B[mm % 2], HHALF * 128),
                        w3_l[ee, mm, :, hf * HHALF * 128:(hf + 1) * HHALF * 128]))(e_i, half, m),
                       writes=[("pb", W3B[mb], W3B[mb] + HHALF * 128)], dma="w3%d" % mb)
                    for t in range(NT):
                        bank = 4 + ((m * NT + t) % 2)
                        em("tensor", [(lambda mm, tt, bk, cll, fi, la: lambda e: e.matmul(
                            pps(BK(bk)),
                            pb[:, W3B[mm % 2] + cll * 128:W3B[mm % 2] + (cll + 1) * 128],
                            pb[:, HIDB + cll * T + tt * 512:HIDB + cll * T + (tt + 1) * 512],
                            start=fi, stop=la))(m, t, bank, cl, cl == 0, cl == HHALF - 1)
                            for cl in range(HHALF)],
                           reads=[("pb", W3B[mb], W3B[mb] + HHALF * 128)] +
                                 [("pb", HIDB + cl * T + t * 512,
                                   HIDB + cl * T + (t + 1) * 512) for cl in range(HHALF)],
                           writes=[("pp",) + BK(bank)])
                        em("vector", (lambda mm, tt, bk: lambda e: e.tensor_tensor(
                            acc[:, mm * T + tt * 512:mm * T + (tt + 1) * 512],
                            acc[:, mm * T + tt * 512:mm * T + (tt + 1) * 512],
                            pps(BK(bk)), ALU.add))(m, t, bank),
                           reads=[("acc", m * T + t * 512, m * T + (t + 1) * 512),
                                  ("pp",) + BK(bank)],
                           writes=[("acc", m * T + t * 512, m * T + (t + 1) * 512)])
                        if (e_i == E - 1 and half == 1 and m == KC - 1):
                            for bb in range(t * 4, t * 4 + 4):
                                if bb not in o_done:
                                    o_done.add(bb)
                                    emit_o_block(bb)

        o_rest = [b for b in range(NB) if b not in o_done]
        for b in o_rest:
            base = (b % 2) * 1024
            ot = OT[b % 2]
            em("tensor", [(lambda bb, kkc, bs: lambda e: e.transpose(
                pp[:, bs + kkc * 128:bs + (kkc + 1) * 128],
                acc[:, kkc * T + bb * 128:kkc * T + (bb + 1) * 128], i128f[:]))(b, kc, base)
                for kc in range(KC)],
               reads=[("acc", kc * T + b * 128, kc * T + (b + 1) * 128) for kc in range(KC)] +
                     [("i128f", 0, 128)],
               writes=[("pp", base, base + 1024)])
            em("vector", (lambda bs, o: lambda e: e.tensor_copy(
                arfs(o, 1024), pp[:, bs:bs + 1024]))(base, ot),
               reads=[("pp", base, base + 1024)],
               writes=[("arf", ot, ot + 1024)])
            final_caps.append(em("sync", (lambda bb, o: lambda e: e.dma_start(
                out[bb * 128:(bb + 1) * 128, :], arfs(o, 1024)))(b, ot),
               reads=[("arf", ot, ot + 1024)], dma="do%d" % (b % 2)))

        def play(e, name):
            for (fns, waits, sem, inc_k) in B.ops[name]:
                for (s, v) in waits:
                    e.wait_ge(sems[s], v)
                r = None
                for fn in fns:
                    r = fn(e)
                r.then_inc(sems[sem], inc_k)
            if name == "sync":
                e.wait_ge(sems["do0"], B.cnt["do0"])
                e.wait_ge(sems["do1"], B.cnt["do1"])

        @block.sync
        def _(e):
            play(e, "sync")

        @block.gpsimd
        def _(e):
            play(e, "gpsimd")

        @block.tensor
        def _(e):
            play(e, "tensor")

        @block.vector
        def _(e):
            play(e, "vector")

        @block.scalar
        def _(e):
            play(e, "scalar")

    return nc


def _lhsT_tiles(w):
    return np.ascontiguousarray(
        w.reshape(8, 128, 8, 128).transpose(0, 3, 2, 1).reshape(8, 128, 1024))


def _swap_rows(w):
    return np.ascontiguousarray(w.reshape(-1, 2, w.shape[-1])[:, ::-1, :].reshape(w.shape))


def kernel(L0_attn_x, L0_attn_Wq, L0_attn_Wk, L0_attn_Wv, L0_attn_Wo,
           L0_ffn_Wg, L0_ffn_W1, L0_ffn_W2, L0_ffn_W3):
    x = np.ascontiguousarray(np.asarray(L0_attn_x, dtype=np.float32))
    Wq = np.asarray(L0_attn_Wq, dtype=np.float32)
    Wk = np.asarray(L0_attn_Wk, dtype=np.float32)
    Wv = np.asarray(L0_attn_Wv, dtype=np.float32)
    Wo = np.asarray(L0_attn_Wo, dtype=np.float32)
    Wg = np.asarray(L0_ffn_Wg, dtype=np.float32)
    W1 = np.asarray(L0_ffn_W1, dtype=np.float32)
    W2 = np.asarray(L0_ffn_W2, dtype=np.float32)
    W3 = np.asarray(L0_ffn_W3, dtype=np.float32)

    wq_l = _lhsT_tiles(Wq)
    wqs_l = _lhsT_tiles(_swap_rows(Wq))
    wk_l = _lhsT_tiles(Wk)
    wks_l = _lhsT_tiles(_swap_rows(Wk))
    wv_r = _lhsT_tiles(Wv)
    wo_l = _lhsT_tiles(Wo)
    wg_r = np.ascontiguousarray(Wg.reshape(8, 8, 128).transpose(2, 1, 0).reshape(128, 64))

    HPAD = HCH * 128
    W1p = np.zeros((E, HPAD, D), np.float32)
    W1p[:, :2730] = W1
    W2p = np.zeros((E, HPAD, D), np.float32)
    W2p[:, :2730] = W2
    W3p = np.zeros((E, D, HPAD), np.float32)
    W3p[:, :, :2730] = W3
    w1_l = np.ascontiguousarray(
        W1p.reshape(E, HCH, 128, 8, 128).transpose(0, 1, 4, 3, 2).reshape(E, HCH, 128, 1024))
    w2_l = np.ascontiguousarray(
        W2p.reshape(E, HCH, 128, 8, 128).transpose(0, 1, 4, 3, 2).reshape(E, HCH, 128, 1024))
    w3_l = np.ascontiguousarray(
        W3p.reshape(E, 8, 128, HCH, 128).transpose(0, 1, 4, 3, 2).reshape(E, 8, 128, HCH * 128))

    inv_freq = 1.0 / (10000.0 ** (np.arange(0, 64, 2, dtype=np.float64) / 64.0))
    tt = np.arange(T, dtype=np.float64)
    fr = np.outer(tt, inv_freq)
    cosv = np.cos(fr)
    sinv = np.sin(fr)
    p = np.arange(128)
    fi = (p % 64) // 2
    ct = cosv[:, fi].T.astype(np.float32)
    sgn = np.where(p % 2 == 0, -1.0, 1.0)
    st = (sinv[:, fi].T * sgn[:, None]).astype(np.float32)

    i128 = np.eye(128, dtype=np.float32)
    qi = np.arange(128)[:, None]
    ki = np.arange(128)[None, :]
    trineg = np.where(ki <= qi, 0.0, -1e30).astype(np.float32)
    sel8 = np.zeros((8, 1024), np.float32)
    for e_ in range(8):
        sel8[e_, e_ * 128:(e_ + 1) * 128] = 1.0

    shared = {
        "wq_l": wq_l, "wqs_l": wqs_l, "wk_l": wk_l, "wks_l": wks_l,
        "wv_r": wv_r, "wo_l": wo_l, "wg_r": wg_r,
        "w1_l": w1_l, "w2_l": w2_l, "w3_l": w3_l,
        "ct": ct, "st": st, "i128": i128, "trineg": trineg, "sel8": sel8,
    }
    in_maps = [dict(shared, x=np.ascontiguousarray(x[b])) for b in range(8)]
    nc = build_nc()
    res = run_bass_kernel_spmd(nc, in_maps, core_ids=list(range(8)))
    return np.stack([np.asarray(r["out"]) for r in res.results], axis=0)


# version 17
# speedup vs baseline: 1.0748x; 1.0229x over previous
import contextlib
import numpy as np
import concourse.bass as bass
import concourse.mybir as mybir
from concourse.bass_utils import run_bass_kernel_spmd

F32 = mybir.dt.float32
BF16 = mybir.dt.bfloat16
AF = mybir.ActivationFunctionType
ALU = mybir.AluOpType
AX = mybir.AxisListType

T = 2048
D = 1024
KC = 8
NB = 16
NT = 4
NHP = 8
E = 8
HCH = 22
HHALF = 11
EPS = 1e-5
SCALE = 0.125

DO_ATTN = True
DO_FFN = True

ENGS = ("sync", "gpsimd", "tensor", "vector", "scalar")
ENG_SEM = {"tensor": "tPE", "vector": "tVE", "scalar": "tSC",
           "sync": "tSY", "gpsimd": "tGP"}

AO = 0
QHS = 16384
KHS = 18432
VH = (20480, 22528)
PRB = 24576
ATT = 26624
WQB, WQSB, WKB, WKSB = 28672, 29696, 30720, 31744
WVB = 32768
WOB = (32768, 32768)
HIDB = 0
W1B = (28672, 29696)
W2B = (30720, 31744)
W3B = (24576, 25984)
POOLB_N = 35840
PRB_B = 33792

XT = (0, 1024)
XC = 2048
SQ = 3072
XNF = 4096
DIAG = 2048
RT1, RT2 = 0, 512
WBC = (0, 2048)
SIL = (4096, 4608)
H2S = (5120, 5632)
OT = (0, 1024)
ARF_N = 6144

GRAN = {"pp": 512, "stat": 8}


def BK(b, off=0, w=512):
    return (b * 512 + off, b * 512 + off + w)


class Builder:

    def __init__(self):
        self.ops = {e: [] for e in ENGS}
        import collections
        self.cnt = collections.defaultdict(int)
        self.wr = {}
        self.rd = {}
        self.waited = {e: {} for e in ENGS}

    def _slots(self, rng):
        name, lo, hi = rng
        g = GRAN.get(name, 128)
        return [(name, s) for s in range(lo // g, (hi + g - 1) // g)]

    def emit(self, eng, fns, reads=(), writes=(), dma=None, extra=()):
        if not isinstance(fns, (list, tuple)):
            fns = [fns]
        waits = {}

        def need(cap):
            if cap is None:
                return
            s, v = cap
            if v <= 0:
                return
            waits[s] = max(waits.get(s, 0), v)

        rslots = []
        for rng in reads:
            for sl in self._slots(rng):
                rslots.append(sl)
                need(self.wr.get(sl))
                if sl[0] == "pp":
                    for cap, eng2 in self.rd.get(sl, ()):
                        if eng2 != eng:
                            need(cap)
        wslots = []
        for rng in writes:
            for sl in self._slots(rng):
                wslots.append(sl)
                need(self.wr.get(sl))
                for cap, eng2 in self.rd.get(sl, ()):
                    need(cap)
        for cap in extra:
            need(cap)

        wd = self.waited[eng]
        final = []
        for s, v in sorted(waits.items()):
            if wd.get(s, 0) >= v:
                continue
            wd[s] = v
            final.append((s, v))

        sem = dma if dma is not None else ENG_SEM[eng]
        inc_k = 16 if dma is not None else 1
        self.cnt[sem] += inc_k
        cap = (sem, self.cnt[sem])

        for sl in rslots:
            self.rd.setdefault(sl, []).append((cap, eng))
        for sl in wslots:
            self.wr[sl] = cap
            self.rd[sl] = []

        self.ops[eng].append((fns, tuple(final), sem, inc_k))
        return cap


def build_nc():
    nc = bass.Bass("TRN2", target_bir_lowering=False, debug=False)
    dt = nc.dram_tensor
    x_in = dt("x", [T, D], F32, kind="ExternalInput").ap()
    wq_l = dt("wq_l", [NHP, 128, 1024], F32, kind="ExternalInput").ap()
    wqs_l = dt("wqs_l", [NHP, 128, 1024], F32, kind="ExternalInput").ap()
    wk_l = dt("wk_l", [NHP, 128, 1024], F32, kind="ExternalInput").ap()
    wks_l = dt("wks_l", [NHP, 128, 1024], F32, kind="ExternalInput").ap()
    wv_r = dt("wv_r", [NHP, 128, 1024], F32, kind="ExternalInput").ap()
    wo_l = dt("wo_l", [KC, 128, 1024], F32, kind="ExternalInput").ap()
    wg_r = dt("wg_r", [128, 64], F32, kind="ExternalInput").ap()
    w1_l = dt("w1_l", [E, HCH, 128, 1024], F32, kind="ExternalInput").ap()
    w2_l = dt("w2_l", [E, HCH, 128, 1024], F32, kind="ExternalInput").ap()
    w3_l = dt("w3_l", [E, KC, 128, HCH * 128], F32, kind="ExternalInput").ap()
    ct_in = dt("ct", [128, T], F32, kind="ExternalInput").ap()
    st_in = dt("st", [128, T], F32, kind="ExternalInput").ap()
    i128_in = dt("i128", [128, 128], F32, kind="ExternalInput").ap()
    trineg_in = dt("trineg", [128, 128], F32, kind="ExternalInput").ap()
    sel8_in = dt("sel8", [8, 1024], F32, kind="ExternalInput").ap()
    out = dt("out", [T, D], F32, kind="ExternalOutput").ap()

    with contextlib.ExitStack() as ctx:
        en = ctx.enter_context
        acc = en(nc.sbuf_tensor("acc", [128, KC * T], F32))
        xn = en(nc.sbuf_tensor("xn", [128, KC * T], BF16))
        pb = en(nc.sbuf_tensor("poolb", [128, POOLB_N], BF16))
        arf = en(nc.sbuf_tensor("arf", [128, ARF_N], F32))
        csb = en(nc.sbuf_tensor("csb", [128, 2 * T], BF16))
        wgate = en(nc.sbuf_tensor("wgate", [128, 128], F32))
        wrow = en(nc.sbuf_tensor("wrow", [8, T], BF16))
        i128f = en(nc.sbuf_tensor("i128f", [128, 128], F32))
        i128b = en(nc.sbuf_tensor("i128b", [128, 128], BF16))
        trineg = en(nc.sbuf_tensor("trineg_s", [128, 128], F32))
        sel8 = en(nc.sbuf_tensor("sel8_s", [8, 1024], BF16))
        wgr = en(nc.sbuf_tensor("wgr", [128, 64], F32))
        dgb = en(nc.sbuf_tensor("dgb", [128, 256], BF16))
        stat = en(nc.sbuf_tensor("stat", [128, 64], F32))
        pp = en(nc.psum_tensor("pp", [128, 4096], F32))
        sem_names = ("tPE", "tVE", "tSC", "tSY", "tGP",
                     "cI128F", "cTRI", "cWGR", "cCT", "cST", "cI128B", "cSEL",
                     "dx0", "dx1", "wqS", "wqsS", "wkS", "wksS", "wvS", "woS",
                     "w10", "w11", "w20", "w21", "w30", "w31", "do0", "do1")
        sems = {n: en(nc.semaphore(n)) for n in sem_names}
        block = en(nc.Block())

        B = Builder()
        em = B.emit

        def arfs(off, w):
            return arf[:, off:off + w]

        def pbs(off, w):
            return pb[:, off:off + w]

        def pps(lo_hi):
            lo, hi = lo_hi
            return pp[:, lo:hi]

        em("sync", lambda e: e.dma_start(i128f[:], i128_in),
           writes=[("i128f", 0, 128)], dma="cI128F")
        em("sync", lambda e: e.dma_start(trineg[:], trineg_in),
           writes=[("trineg", 0, 128)], dma="cTRI")
        em("sync", lambda e: e.dma_start(wgr[:], wg_r),
           writes=[("wgr", 0, 64)], dma="cWGR")
        em("gpsimd", lambda e: e.dma_start(csb[:, 0:T], ct_in),
           writes=[("csb", 0, T)], dma="cCT")
        em("gpsimd", lambda e: e.dma_start(csb[:, T:2 * T], st_in),
           writes=[("csb", T, 2 * T)], dma="cST")
        em("gpsimd", lambda e: e.dma_start(i128b[:], i128_in),
           writes=[("i128b", 0, 128)], dma="cI128B")
        em("gpsimd", lambda e: e.dma_start(sel8[:], sel8_in),
           writes=[("sel8", 0, 1024)], dma="cSEL")

        def emit_ln_block(b):
            bf = b % 2
            xt0 = XT[bf]
            sb = bf * 32
            em("sync", (lambda bb, o: lambda e: e.dma_start(
                arfs(o, 1024), x_in[bb * 128:(bb + 1) * 128, :]))(b, xt0),
               writes=[("arf", xt0, xt0 + 1024)], dma="dx%d" % bf)
            em("vector", (lambda o, ss: lambda e: e.reduce_sum(
                stat[:, ss:ss + 1], arfs(o, 1024), axis=AX.X))(xt0, sb),
               reads=[("arf", xt0, xt0 + 1024)], writes=[("stat", sb, sb + 1)])
            em("vector", (lambda ss: lambda e: e.tensor_scalar(
                stat[:, ss + 1:ss + 2], stat[:, ss:ss + 1], -1.0 / D, None, ALU.mult))(sb),
               reads=[("stat", sb, sb + 1)], writes=[("stat", sb + 1, sb + 2)])
            em("vector", (lambda o, ss: lambda e: e.tensor_scalar(
                arfs(XC, 1024), arfs(o, 1024), stat[:, ss + 1:ss + 2], None, ALU.add))(xt0, sb),
               reads=[("arf", xt0, xt0 + 1024), ("stat", sb + 1, sb + 2)],
               writes=[("arf", XC, XC + 1024)])
            em("scalar", (lambda ss: lambda e: e.activation(
                arfs(SQ, 1024), arfs(XC, 1024), AF.Square,
                accum_out=stat[:, ss + 2:ss + 3]))(sb),
               reads=[("arf", XC, XC + 1024)],
               writes=[("arf", SQ, SQ + 1024), ("stat", sb + 2, sb + 3)])
            em("vector", (lambda ss: lambda e: e.tensor_scalar(
                stat[:, ss + 3:ss + 4], stat[:, ss + 2:ss + 3], 1.0 / D, EPS,
                ALU.mult, ALU.add))(sb),
               reads=[("stat", sb + 2, sb + 3)], writes=[("stat", sb + 3, sb + 4)])
            em("scalar", (lambda ss: lambda e: e.activation(
                stat[:, ss + 4:ss + 5], stat[:, ss + 3:ss + 4], AF.Sqrt))(sb),
               reads=[("stat", sb + 3, sb + 4)], writes=[("stat", sb + 4, sb + 5)])
            em("vector", (lambda ss: lambda e: e.reciprocal(
                stat[:, ss + 5:ss + 6], stat[:, ss + 4:ss + 5]))(sb),
               reads=[("stat", sb + 4, sb + 5)], writes=[("stat", sb + 5, sb + 6)])
            em("vector", (lambda ss: lambda e: e.tensor_scalar(
                arfs(XC, 1024), arfs(XC, 1024), stat[:, ss + 5:ss + 6], None, ALU.mult))(sb),
               reads=[("arf", XC, XC + 1024), ("stat", sb + 5, sb + 6)],
               writes=[("arf", XC, XC + 1024)])
            em("tensor", [(lambda kkc: lambda e: e.transpose(
                pp[:, 2048 + kkc * 128:2048 + (kkc + 1) * 128],
                arf[:, XC + kkc * 128:XC + (kkc + 1) * 128], i128f[:]))(kc)
                for kc in range(KC)],
               reads=[("arf", XC, XC + 1024), ("i128f", 0, 128)],
               writes=[("pp", 2048, 3072)])
            em("scalar", (lambda bb: lambda e: e.copy(
                xn[:].rearrange("p (k t) -> p k t", k=KC)[:, :, bb * 128:(bb + 1) * 128],
                pp[:, 2048:3072].rearrange("p (k t) -> p k t", k=KC)))(b),
               reads=[("pp", 2048, 3072)],
               writes=[("xn", kc * T + b * 128, kc * T + (b + 1) * 128) for kc in range(KC)])
            em("vector", lambda e: e.tensor_copy(arfs(XNF, 1024), pp[:, 2048:3072]),
               reads=[("pp", 2048, 3072)], writes=[("arf", XNF, XNF + 1024)])
            em("tensor", [(lambda kkc, fi, la: lambda e: e.matmul(
                pp[:, 2048:2056], arf[:, XNF + kkc * 128:XNF + (kkc + 1) * 128],
                wgr[:, kkc * 8:(kkc + 1) * 8], start=fi, stop=la))(kc, kc == 0, kc == KC - 1)
                for kc in range(KC)],
               reads=[("arf", XNF, XNF + 1024), ("wgr", 0, 64)],
               writes=[("pp", 2048, 2560)])
            em("tensor", [(lambda o, kkc: lambda e: e.transpose(
                pp[:, 3072 + kkc * 128:3072 + (kkc + 1) * 128],
                arf[:, o + kkc * 128:o + (kkc + 1) * 128], i128f[:]))(xt0, kc)
                for kc in range(KC)],
               reads=[("arf", xt0, xt0 + 1024), ("i128f", 0, 128)],
               writes=[("pp", 3072, 4096)])
            em("vector", (lambda bb: lambda e: e.tensor_copy(
                acc[:].rearrange("p (k t) -> p k t", k=KC)[:, :, bb * 128:(bb + 1) * 128],
                pp[:, 3072:4096].rearrange("p (k t) -> p k t", k=KC)))(b),
               reads=[("pp", 3072, 4096)],
               writes=[("acc", kc * T + b * 128, kc * T + (b + 1) * 128) for kc in range(KC)])
            gread = [("pp", 2048, 2560)]
            em("vector", (lambda ss: lambda e: e.reduce_max(
                stat[:, ss + 6:ss + 7], pp[:, 2048:2056], axis=AX.X))(sb),
               reads=gread, writes=[("stat", sb + 6, sb + 7)])
            em("vector", (lambda ss: lambda e: e.tensor_scalar(
                stat[:, ss + 7:ss + 8], stat[:, ss + 6:ss + 7], -1.0, None, ALU.mult))(sb),
               reads=[("stat", sb + 6, sb + 7)], writes=[("stat", sb + 7, sb + 8)])
            em("vector", (lambda ss: lambda e: e.tensor_scalar(
                stat[:, ss + 8:ss + 16], pp[:, 2048:2056], stat[:, ss + 6:ss + 7], None,
                ALU.is_ge))(sb),
               reads=gread + [("stat", sb + 6, sb + 7)], writes=[("stat", sb + 8, sb + 16)])
            em("vector", (lambda ss: lambda e: e.scalar_tensor_tensor(
                stat[:, ss + 16:ss + 24], stat[:, ss + 8:ss + 16], -1e30, pp[:, 2048:2056],
                ALU.mult, ALU.add))(sb),
               reads=gread + [("stat", sb + 8, sb + 16)], writes=[("stat", sb + 16, sb + 24)])
            em("vector", (lambda ss: lambda e: e.reduce_max(
                stat[:, ss + 24:ss + 25], stat[:, ss + 16:ss + 24], axis=AX.X))(sb),
               reads=[("stat", sb + 16, sb + 24)], writes=[("stat", sb + 24, sb + 25)])
            em("scalar", (lambda ss: lambda e: e.activation(
                stat[:, ss + 25:ss + 26], stat[:, ss + 24:ss + 25], AF.Exp,
                bias=stat[:, ss + 7:ss + 8]))(sb),
               reads=[("stat", sb + 24, sb + 25), ("stat", sb + 7, sb + 8)],
               writes=[("stat", sb + 25, sb + 26)])
            em("vector", (lambda ss: lambda e: e.tensor_scalar(
                stat[:, ss + 26:ss + 27], stat[:, ss + 25:ss + 26], 1.0, None, ALU.add))(sb),
               reads=[("stat", sb + 25, sb + 26)], writes=[("stat", sb + 26, sb + 27)])
            em("vector", (lambda ss: lambda e: e.reciprocal(
                stat[:, ss + 27:ss + 28], stat[:, ss + 26:ss + 27]))(sb),
               reads=[("stat", sb + 26, sb + 27)], writes=[("stat", sb + 27, sb + 28)])
            em("scalar", (lambda ss: lambda e: e.activation(
                stat[:, ss + 16:ss + 24], pp[:, 2048:2056], AF.Exp,
                bias=stat[:, ss + 7:ss + 8]))(sb),
               reads=gread + [("stat", sb + 7, sb + 8)], writes=[("stat", sb + 16, sb + 24)])
            em("vector", (lambda ss: lambda e: e.tensor_scalar(
                stat[:, ss + 8:ss + 16], pp[:, 2048:2056], stat[:, ss + 24:ss + 25], None,
                ALU.is_ge))(sb),
               reads=gread + [("stat", sb + 24, sb + 25)], writes=[("stat", sb + 8, sb + 16)])
            em("vector", (lambda ss: lambda e: e.tensor_tensor(
                stat[:, ss + 8:ss + 16], stat[:, ss + 16:ss + 24],
                stat[:, ss + 8:ss + 16], ALU.mult))(sb),
               reads=[("stat", sb + 8, sb + 24)], writes=[("stat", sb + 8, sb + 16)])
            em("vector", (lambda bb, ss: lambda e: e.tensor_scalar(
                wgate[:, bb * 8:(bb + 1) * 8], stat[:, ss + 8:ss + 16],
                stat[:, ss + 27:ss + 28], None, ALU.mult))(b, sb),
               reads=[("stat", sb + 8, sb + 16), ("stat", sb + 27, sb + 28)],
               writes=[("wgate", b * 8, (b + 1) * 8)])

        def emit_hp_loads(hp):
            for buf, warr, wsem in ((WQB, wq_l, "wqS"), (WQSB, wqs_l, "wqsS"),
                                    (WKB, wk_l, "wkS"), (WKSB, wks_l, "wksS"),
                                    (WVB, wv_r, "wvS")):
                em("gpsimd", (lambda bb, ss, h: lambda e: e.dma_start(
                    pbs(bb, 1024), ss[h]))(buf, warr, hp),
                   writes=[("pb", buf, buf + 1024)], dma=wsem)


        def emit_hp_proj(hp, tq_only=None):
            for tq in (range(NT) if tq_only is None else [tq_only]):
                c0, c1 = tq * 512, (tq + 1) * 512
                for bank, wbuf in ((0, WQB), (1, WQSB), (2, WKB), (3, WKSB)):
                    em("tensor", [(lambda bk, wb, kkc, fi, la, a0, a1: lambda e: e.matmul(
                        pps(BK(bk)), pb[:, wb + kkc * 128:wb + (kkc + 1) * 128],
                        xn[:, kkc * T + a0:kkc * T + a1],
                        start=fi, stop=la))(bank, wbuf, kc, kc == 0, kc == KC - 1, c0, c1)
                        for kc in range(KC)],
                       reads=[("pb", wbuf, wbuf + 1024)] +
                             [("xn", kc * T + c0, kc * T + c1) for kc in range(KC)],
                       writes=[("pp",) + BK(bank)])
                em("vector", (lambda a0, a1: lambda e: e.scalar_tensor_tensor(
                    arfs(RT1, 512), pps(BK(0)), SCALE, csb[:, a0:a1],
                    ALU.mult, ALU.mult))(c0, c1),
                   reads=[("pp",) + BK(0), ("csb", c0, c1)],
                   writes=[("arf", RT1, RT1 + 512)])
                em("vector", (lambda a0, a1: lambda e: e.scalar_tensor_tensor(
                    arfs(RT2, 512), pps(BK(1)), SCALE, csb[:, T + a0:T + a1],
                    ALU.mult, ALU.mult))(c0, c1),
                   reads=[("pp",) + BK(1), ("csb", T + c0, T + c1)],
                   writes=[("arf", RT2, RT2 + 512)])
                em("vector", (lambda a0, a1: lambda e: e.tensor_tensor(
                    pb[:, QHS + a0:QHS + a1], arfs(RT1, 512), arfs(RT2, 512),
                    ALU.add))(c0, c1),
                   reads=[("arf", RT1, RT1 + 512), ("arf", RT2, RT2 + 512)],
                   writes=[("pb", QHS + c0, QHS + c1)])
                em("vector", (lambda a0, a1: lambda e: e.tensor_tensor(
                    arfs(RT1, 512), pps(BK(2)), csb[:, a0:a1], ALU.mult))(c0, c1),
                   reads=[("pp",) + BK(2), ("csb", c0, c1)],
                   writes=[("arf", RT1, RT1 + 512)])
                em("vector", (lambda a0, a1: lambda e: e.tensor_tensor(
                    arfs(RT2, 512), pps(BK(3)), csb[:, T + a0:T + a1], ALU.mult))(c0, c1),
                   reads=[("pp",) + BK(3), ("csb", T + c0, T + c1)],
                   writes=[("arf", RT2, RT2 + 512)])
                em("vector", (lambda a0, a1: lambda e: e.tensor_tensor(
                    pb[:, KHS + a0:KHS + a1], arfs(RT1, 512), arfs(RT2, 512),
                    ALU.add))(c0, c1),
                   reads=[("arf", RT1, RT1 + 512), ("arf", RT2, RT2 + 512)],
                   writes=[("pb", KHS + c0, KHS + c1)])


        def emit_hp_vprod(hp):
            vb = VH[hp % 2]
            for g4 in range(4):
                bank = 4 + (g4 % 2)
                fns = []
                for j in range(4):
                    blk = g4 * 4 + j
                    for kc in range(KC):
                        fns.append((lambda bk, jj, bb, kkc, fi, la: lambda e: e.matmul(
                            pp[:, BK(bk)[0] + jj * 128:BK(bk)[0] + (jj + 1) * 128],
                            xn[:, kkc * T + bb * 128:kkc * T + (bb + 1) * 128],
                            pb[:, WVB + kkc * 128:WVB + (kkc + 1) * 128],
                            start=fi, stop=la))(bank, j, blk, kc, kc == 0, kc == KC - 1))
                em("tensor", fns,
                   reads=[("pb", WVB, WVB + 1024)] +
                         [("xn", kc * T + g4 * 512, kc * T + (g4 + 1) * 512) for kc in range(KC)],
                   writes=[("pp",) + BK(bank)])
                em("scalar", (lambda vbb, gg, bk: lambda e: e.copy(
                    pb[:, vbb + gg * 512:vbb + (gg + 1) * 512], pps(BK(bk))))(vb, g4, bank),
                   reads=[("pp",) + BK(bank)],
                   writes=[("pb", vb + g4 * 512, vb + (g4 + 1) * 512)])


        def emit_hp_attn(hp):
            vb = VH[hp % 2]
            def emit_dummies(n):
                for _ in range(n):
                    em("tensor", lambda e: e.matmul(
                        pp[:, 3584:4096], i128b[:], csb[:, 0:512],
                        start=True, stop=True),
                       reads=[("i128b", 0, 128), ("csb", 0, 512)],
                       writes=[("pp", 3584, 4096)])

            def s2_stage(q, a, py):
                prb = PRB if py == 0 else PRB_B
                L = (q + 1) * 128
                ngrp = (q + 4) // 4
                emit_dummies(4)
                for g in range(ngrp):
                    bank = 4 + (g % 2)
                    kb0 = g * 4
                    kbn = min(4, q + 1 - kb0)
                    em("tensor", [(lambda bk, jj, kk, pr, pyy: lambda e: e.matmul(
                        pp[:, BK(bk)[0] + jj * 128:BK(bk)[0] + (jj + 1) * 128],
                        pb[:, pr + kk * 128:pr + (kk + 1) * 128],
                        dgb[:, pyy * 128:(pyy + 1) * 128],
                        start=True, stop=True))(bank, j, kb0 + j, prb, py)
                        for j in range(kbn)],
                       reads=[("pb", prb + kb0 * 128, prb + (kb0 + kbn) * 128),
                              ("dgb", py * 128, (py + 1) * 128)],
                       writes=[("pp",) + BK(bank)])
                    if g % 2 == 0:
                        em("vector", (lambda gg, bk, n: lambda e: e.tensor_copy(
                            pb[:, ATT + gg * 512:ATT + gg * 512 + n * 128],
                            pp[:, BK(bk)[0]:BK(bk)[0] + n * 128]))(g, bank, kbn),
                           reads=[("pp",) + BK(bank)],
                           writes=[("pb", ATT + kb0 * 128, ATT + (kb0 + kbn) * 128)])
                    else:
                        em("scalar", (lambda gg, bk, n: lambda e: e.copy(
                            pb[:, ATT + gg * 512:ATT + gg * 512 + n * 128],
                            pp[:, BK(bk)[0]:BK(bk)[0] + n * 128]))(g, bank, kbn),
                           reads=[("pp",) + BK(bank)],
                           writes=[("pb", ATT + kb0 * 128, ATT + (kb0 + kbn) * 128)])
                reg = 3072 + (q % 2) * 128
                emit_dummies(2)
                fns = []
                for kb in range(q + 1):
                    tp = (0, 64) if a == 1 else None
                    fns.append((lambda ax, kk, rr, fi, la, tpp, vbb: lambda e: e.matmul(
                        pp[64 * ax:64 * ax + 64, rr:rr + 128],
                        pb[:, vbb + kk * 128 + 64 * ax:vbb + kk * 128 + 64 * ax + 64],
                        pb[:, ATT + kk * 128:ATT + (kk + 1) * 128],
                        start=fi, stop=la, tile_position=tpp))(
                            a, kb, reg, kb == 0, kb == q, tp, vb))
                em("tensor", fns,
                   reads=[("pb", vb, vb + (q + 1) * 128),
                          ("pb", ATT, ATT + L)],
                   writes=[("pp", reg, reg + 128)])
                if a == 1:
                    em("scalar", (lambda hh, qq, rr: lambda e: e.copy(
                        pb[:, AO + hh * T + qq * 128:AO + hh * T + (qq + 1) * 128],
                        pp[:, rr:rr + 128]))(hp, q, reg),
                       reads=[("pp", reg, reg + 128)],
                       writes=[("pb", AO + hp * T + q * 128, AO + hp * T + (q + 1) * 128)])

            prev = None
            for qb in range(NB):
                for a in range(2):
                    u = qb * 2 + a
                    py = u % 2
                    st0 = 16 * py
                    prb = PRB if py == 0 else PRB_B
                    L = (qb + 1) * 128
                    nch = (L + 511) // 512
                    sb0 = 1024 if (a == 1 and qb < 8) else 0
                    for c in range(nch):
                        w = min(512, L - c * 512)
                        em("tensor", (lambda aa, qq, cc, ww, b0: lambda e: e.matmul(
                            pp[:, b0 + cc * 512:b0 + cc * 512 + ww],
                            pb[64 * aa:64 * aa + 64, QHS + qq * 128:QHS + (qq + 1) * 128],
                            pb[64 * aa:64 * aa + 64, KHS + cc * 512:KHS + cc * 512 + ww],
                            start=True, stop=True))(a, qb, c, w, sb0),
                           reads=[("pb", QHS + qb * 128, QHS + (qb + 1) * 128),
                                  ("pb", KHS + c * 512, KHS + c * 512 + w)],
                           writes=[("pp", sb0 + c * 512, sb0 + c * 512 + w)])
                    em("vector", (lambda LL, b0, s0: lambda e: e.reduce_max(
                        stat[:, s0:s0 + 1], pp[:, b0:b0 + LL], axis=AX.X,
                        negate=True))(L, sb0, st0),
                       reads=[("pp", sb0, sb0 + L)], writes=[("stat", st0, st0 + 1)])
                    em("vector", (lambda qq, b0: lambda e: e.tensor_tensor(
                        pp[:, b0 + qq * 128:b0 + (qq + 1) * 128],
                        pp[:, b0 + qq * 128:b0 + (qq + 1) * 128],
                        trineg[:], ALU.add))(qb, sb0),
                       reads=[("pp", sb0 + qb * 128, sb0 + (qb + 1) * 128), ("trineg", 0, 128)],
                       writes=[("pp", sb0 + qb * 128, sb0 + (qb + 1) * 128)])
                    em("scalar", (lambda qq, b0, s0, pr: lambda e: e.activation(
                        pb[:, pr:pr + (qq + 1) * 128], pp[:, b0:b0 + (qq + 1) * 128], AF.Exp,
                        bias=stat[:, s0:s0 + 1], scale=1.0,
                        accum_out=stat[:, s0 + 2:s0 + 3]))(qb, sb0, st0, prb),
                       reads=[("pp", sb0, sb0 + (qb + 1) * 128), ("stat", st0, st0 + 1)],
                       writes=[("pb", prb, prb + (qb + 1) * 128), ("stat", st0 + 2, st0 + 3)])
                    if prev is not None:
                        s2_stage(*prev)
                    prev = (qb, a, py)
                    em("vector", (lambda s0: lambda e: e.reciprocal(
                        stat[:, s0 + 5:s0 + 6], stat[:, s0 + 2:s0 + 3]))(st0),
                       reads=[("stat", st0 + 2, st0 + 3)], writes=[("stat", st0 + 5, st0 + 6)])
                    em("vector", (lambda s0, pyy: lambda e: e.tensor_scalar(
                        dgb[:, pyy * 128:(pyy + 1) * 128], i128b[:],
                        stat[:, s0 + 5:s0 + 6], None, ALU.mult))(st0, py),
                       reads=[("i128b", 0, 128), ("stat", st0 + 5, st0 + 6)],
                       writes=[("dgb", py * 128, (py + 1) * 128)])
            if prev is not None:
                s2_stage(*prev)
                prev = None


        emit_hp_loads(0)
        for tq4 in range(NT):
            for b in range(tq4 * 4, tq4 * 4 + 4):
                emit_ln_block(b)
            emit_hp_proj(0, tq4)
        for g in range(4):
            em("tensor", [(lambda bb, jj: lambda e: e.transpose(
                pp[0:8, 3584 + jj * 128:3584 + (jj + 1) * 128],
                wgate[:, bb * 8:(bb + 1) * 8], i128f[:]))(g * 4 + j, j)
                for j in range(4)],
               reads=[("wgate", g * 32, (g + 1) * 32), ("i128f", 0, 128)],
               writes=[("pp", 3584, 4096)])
            em("vector", (lambda gg: lambda e: e.tensor_copy(
                wrow[0:8, gg * 512:(gg + 1) * 512], pp[0:8, 3584:4096]))(g),
               reads=[("pp", 3584, 4096)], writes=[("wrow", g * 512, (g + 1) * 512)])
        emit_hp_vprod(0)
        for hp in range(NHP if DO_ATTN else 0):
            emit_hp_attn(hp)
            if hp + 1 < NHP:
                emit_hp_loads(hp + 1)
                emit_hp_proj(hp + 1)
                emit_hp_vprod(hp + 1)

        for mc in range(KC if DO_ATTN else 0):
            em("gpsimd", (lambda m: lambda e: e.dma_start(
                pbs(WOB[m % 2], 1024), wo_l[m]))(mc),
               writes=[("pb", WOB[mc % 2], WOB[mc % 2] + 1024)], dma="woS")
            for t in range(NT):
                bank = 6 + ((mc * NT + t) % 2)
                em("tensor", [(lambda m, tt, bk, kkc, fi, la: lambda e: e.matmul(
                    pps(BK(bk)),
                    pb[:, WOB[m % 2] + kkc * 128:WOB[m % 2] + (kkc + 1) * 128],
                    pb[:, AO + kkc * T + tt * 512:AO + kkc * T + (tt + 1) * 512],
                    start=fi, stop=la))(mc, t, bank, kc, kc == 0, kc == KC - 1)
                    for kc in range(KC)],
                   reads=[("pb", WOB[mc % 2], WOB[mc % 2] + 1024)] +
                         [("pb", AO + kc * T + t * 512, AO + kc * T + (t + 1) * 512)
                          for kc in range(KC)],
                   writes=[("pp",) + BK(bank)])
                em("vector", (lambda m, tt, bk: lambda e: e.tensor_tensor(
                    acc[:, m * T + tt * 512:m * T + (tt + 1) * 512],
                    acc[:, m * T + tt * 512:m * T + (tt + 1) * 512],
                    pps(BK(bk)), ALU.add))(mc, t, bank),
                   reads=[("acc", mc * T + t * 512, mc * T + (t + 1) * 512),
                          ("pp",) + BK(bank)],
                   writes=[("acc", mc * T + t * 512, mc * T + (t + 1) * 512)])

        final_caps = []

        def emit_o_block(b):
            base = (b % 2) * 1024
            ot = OT[b % 2]
            em("tensor", [(lambda bb, kkc, bs: lambda e: e.transpose(
                pp[:, bs + kkc * 128:bs + (kkc + 1) * 128],
                acc[:, kkc * T + bb * 128:kkc * T + (bb + 1) * 128], i128f[:]))(b, kc, base)
                for kc in range(KC)],
               reads=[("acc", kc * T + b * 128, kc * T + (b + 1) * 128) for kc in range(KC)] +
                     [("i128f", 0, 128)],
               writes=[("pp", base, base + 1024)])
            em("vector", (lambda bs, o: lambda e: e.tensor_copy(
                arfs(o, 1024), pp[:, bs:bs + 1024]))(base, ot),
               reads=[("pp", base, base + 1024)],
               writes=[("arf", ot, ot + 1024)])
            final_caps.append(em("sync", (lambda bb, o: lambda e: e.dma_start(
                out[bb * 128:(bb + 1) * 128, :], arfs(o, 1024)))(b, ot),
               reads=[("arf", ot, ot + 1024)], dma="do%d" % (b % 2)))

        o_done = set()
        for e_i in range(E if DO_FFN else 0):
            eb = e_i % 2
            for t in range(NT):
                em("tensor", (lambda ee, tt: lambda e: e.matmul(
                    pps(BK(6)), sel8[:, ee * 128:(ee + 1) * 128],
                    wrow[0:8, tt * 512:(tt + 1) * 512],
                    start=True, stop=True))(e_i, t),
                   reads=[("sel8", e_i * 128, (e_i + 1) * 128),
                          ("wrow", t * 512, (t + 1) * 512)],
                   writes=[("pp",) + BK(6)])
                em("scalar", (lambda ebb, tt: lambda e: e.copy(
                    arfs(WBC[ebb] + tt * 512, 512), pps(BK(6))))(eb, t),
                   reads=[("pp",) + BK(6)],
                   writes=[("arf", WBC[eb] + t * 512, WBC[eb] + (t + 1) * 512)])

            for half in range(2):
                for c in range(half * HHALF, (half + 1) * HHALF):
                    cl = c - half * HHALF
                    cb = c % 2
                    b1 = 0 if cb == 0 else 2
                    b2 = 1 if cb == 0 else 3
                    em("gpsimd", (lambda ee, cc: lambda e: e.dma_start(
                        pbs(W1B[cc % 2], 1024), w1_l[ee, cc]))(e_i, c),
                       writes=[("pb", W1B[cb], W1B[cb] + 1024)], dma="w1%d" % cb)
                    em("gpsimd", (lambda ee, cc: lambda e: e.dma_start(
                        pbs(W2B[cc % 2], 1024), w2_l[ee, cc]))(e_i, c),
                       writes=[("pb", W2B[cb], W2B[cb] + 1024)], dma="w2%d" % cb)
                    for t in range(NT):
                        tb = t % 2
                        em("tensor", [(lambda cc, tt, bk, kkc, fi, la: lambda e: e.matmul(
                            pps(BK(bk)),
                            pb[:, W1B[cc % 2] + kkc * 128:W1B[cc % 2] + (kkc + 1) * 128],
                            xn[:, kkc * T + tt * 512:kkc * T + (tt + 1) * 512],
                            start=fi, stop=la))(c, t, b1, kc, kc == 0, kc == KC - 1)
                            for kc in range(KC)],
                           reads=[("pb", W1B[cb], W1B[cb] + 1024)] +
                                 [("xn", kc * T + t * 512, kc * T + (t + 1) * 512)
                                  for kc in range(KC)],
                           writes=[("pp",) + BK(b1)])
                        em("tensor", [(lambda cc, tt, bk, kkc, fi, la: lambda e: e.matmul(
                            pps(BK(bk)),
                            pb[:, W2B[cc % 2] + kkc * 128:W2B[cc % 2] + (kkc + 1) * 128],
                            xn[:, kkc * T + tt * 512:kkc * T + (tt + 1) * 512],
                            start=fi, stop=la))(c, t, b2, kc, kc == 0, kc == KC - 1)
                            for kc in range(KC)],
                           reads=[("pb", W2B[cb], W2B[cb] + 1024)] +
                                 [("xn", kc * T + t * 512, kc * T + (t + 1) * 512)
                                  for kc in range(KC)],
                           writes=[("pp",) + BK(b2)])
                        em("scalar", (lambda bk, tbb: lambda e: e.activation(
                            arfs(SIL[tbb], 512), pps(BK(bk)), AF.Silu))(b1, tb),
                           reads=[("pp",) + BK(b1)],
                           writes=[("arf", SIL[tb], SIL[tb] + 512)])
                        em("vector", (lambda bk, tbb, ebb, tt: lambda e: e.tensor_tensor(
                            arfs(H2S[tbb], 512), pps(BK(bk)),
                            arfs(WBC[ebb] + tt * 512, 512), ALU.mult))(b2, tb, eb, t),
                           reads=[("pp",) + BK(b2),
                                  ("arf", WBC[eb] + t * 512, WBC[eb] + (t + 1) * 512)],
                           writes=[("arf", H2S[tb], H2S[tb] + 512)])
                        em("vector", (lambda cll, tt, tbb: lambda e: e.tensor_tensor(
                            pb[:, HIDB + cll * T + tt * 512:HIDB + cll * T + (tt + 1) * 512],
                            arfs(SIL[tbb], 512), arfs(H2S[tbb], 512), ALU.mult))(cl, t, tb),
                           reads=[("arf", SIL[tb], SIL[tb] + 512),
                                  ("arf", H2S[tb], H2S[tb] + 512)],
                           writes=[("pb", HIDB + cl * T + t * 512,
                                    HIDB + cl * T + (t + 1) * 512)])
                for m in range(KC):
                    mb = m % 2
                    em("gpsimd", (lambda ee, hf, mm: lambda e: e.dma_start(
                        pbs(W3B[mm % 2], HHALF * 128),
                        w3_l[ee, mm, :, hf * HHALF * 128:(hf + 1) * HHALF * 128]))(e_i, half, m),
                       writes=[("pb", W3B[mb], W3B[mb] + HHALF * 128)], dma="w3%d" % mb)
                    for t in range(NT):
                        bank = 4 + ((m * NT + t) % 2)
                        em("tensor", [(lambda mm, tt, bk, cll, fi, la: lambda e: e.matmul(
                            pps(BK(bk)),
                            pb[:, W3B[mm % 2] + cll * 128:W3B[mm % 2] + (cll + 1) * 128],
                            pb[:, HIDB + cll * T + tt * 512:HIDB + cll * T + (tt + 1) * 512],
                            start=fi, stop=la))(m, t, bank, cl, cl == 0, cl == HHALF - 1)
                            for cl in range(HHALF)],
                           reads=[("pb", W3B[mb], W3B[mb] + HHALF * 128)] +
                                 [("pb", HIDB + cl * T + t * 512,
                                   HIDB + cl * T + (t + 1) * 512) for cl in range(HHALF)],
                           writes=[("pp",) + BK(bank)])
                        em("vector", (lambda mm, tt, bk: lambda e: e.tensor_tensor(
                            acc[:, mm * T + tt * 512:mm * T + (tt + 1) * 512],
                            acc[:, mm * T + tt * 512:mm * T + (tt + 1) * 512],
                            pps(BK(bk)), ALU.add))(m, t, bank),
                           reads=[("acc", m * T + t * 512, m * T + (t + 1) * 512),
                                  ("pp",) + BK(bank)],
                           writes=[("acc", m * T + t * 512, m * T + (t + 1) * 512)])
                        if (e_i == E - 1 and half == 1 and m == KC - 1):
                            for bb in range(t * 4, t * 4 + 4):
                                if bb not in o_done:
                                    o_done.add(bb)
                                    emit_o_block(bb)

        o_rest = [b for b in range(NB) if b not in o_done]
        for b in o_rest:
            base = (b % 2) * 1024
            ot = OT[b % 2]
            em("tensor", [(lambda bb, kkc, bs: lambda e: e.transpose(
                pp[:, bs + kkc * 128:bs + (kkc + 1) * 128],
                acc[:, kkc * T + bb * 128:kkc * T + (bb + 1) * 128], i128f[:]))(b, kc, base)
                for kc in range(KC)],
               reads=[("acc", kc * T + b * 128, kc * T + (b + 1) * 128) for kc in range(KC)] +
                     [("i128f", 0, 128)],
               writes=[("pp", base, base + 1024)])
            em("vector", (lambda bs, o: lambda e: e.tensor_copy(
                arfs(o, 1024), pp[:, bs:bs + 1024]))(base, ot),
               reads=[("pp", base, base + 1024)],
               writes=[("arf", ot, ot + 1024)])
            final_caps.append(em("sync", (lambda bb, o: lambda e: e.dma_start(
                out[bb * 128:(bb + 1) * 128, :], arfs(o, 1024)))(b, ot),
               reads=[("arf", ot, ot + 1024)], dma="do%d" % (b % 2)))

        def play(e, name):
            for (fns, waits, sem, inc_k) in B.ops[name]:
                for (s, v) in waits:
                    e.wait_ge(sems[s], v)
                r = None
                for fn in fns:
                    r = fn(e)
                r.then_inc(sems[sem], inc_k)
            if name == "sync":
                e.wait_ge(sems["do0"], B.cnt["do0"])
                e.wait_ge(sems["do1"], B.cnt["do1"])

        @block.sync
        def _(e):
            play(e, "sync")

        @block.gpsimd
        def _(e):
            play(e, "gpsimd")

        @block.tensor
        def _(e):
            play(e, "tensor")

        @block.vector
        def _(e):
            play(e, "vector")

        @block.scalar
        def _(e):
            play(e, "scalar")

    return nc


def _lhsT_tiles(w):
    return np.ascontiguousarray(
        w.reshape(8, 128, 8, 128).transpose(0, 3, 2, 1).reshape(8, 128, 1024))


def _swap_rows(w):
    return np.ascontiguousarray(w.reshape(-1, 2, w.shape[-1])[:, ::-1, :].reshape(w.shape))


def kernel(L0_attn_x, L0_attn_Wq, L0_attn_Wk, L0_attn_Wv, L0_attn_Wo,
           L0_ffn_Wg, L0_ffn_W1, L0_ffn_W2, L0_ffn_W3):
    x = np.ascontiguousarray(np.asarray(L0_attn_x, dtype=np.float32))
    Wq = np.asarray(L0_attn_Wq, dtype=np.float32)
    Wk = np.asarray(L0_attn_Wk, dtype=np.float32)
    Wv = np.asarray(L0_attn_Wv, dtype=np.float32)
    Wo = np.asarray(L0_attn_Wo, dtype=np.float32)
    Wg = np.asarray(L0_ffn_Wg, dtype=np.float32)
    W1 = np.asarray(L0_ffn_W1, dtype=np.float32)
    W2 = np.asarray(L0_ffn_W2, dtype=np.float32)
    W3 = np.asarray(L0_ffn_W3, dtype=np.float32)

    wq_l = _lhsT_tiles(Wq)
    wqs_l = _lhsT_tiles(_swap_rows(Wq))
    wk_l = _lhsT_tiles(Wk)
    wks_l = _lhsT_tiles(_swap_rows(Wk))
    wv_r = _lhsT_tiles(Wv)
    wo_l = _lhsT_tiles(Wo)
    wg_r = np.ascontiguousarray(Wg.reshape(8, 8, 128).transpose(2, 1, 0).reshape(128, 64))

    HPAD = HCH * 128
    W1p = np.zeros((E, HPAD, D), np.float32)
    W1p[:, :2730] = W1
    W2p = np.zeros((E, HPAD, D), np.float32)
    W2p[:, :2730] = W2
    W3p = np.zeros((E, D, HPAD), np.float32)
    W3p[:, :, :2730] = W3
    w1_l = np.ascontiguousarray(
        W1p.reshape(E, HCH, 128, 8, 128).transpose(0, 1, 4, 3, 2).reshape(E, HCH, 128, 1024))
    w2_l = np.ascontiguousarray(
        W2p.reshape(E, HCH, 128, 8, 128).transpose(0, 1, 4, 3, 2).reshape(E, HCH, 128, 1024))
    w3_l = np.ascontiguousarray(
        W3p.reshape(E, 8, 128, HCH, 128).transpose(0, 1, 4, 3, 2).reshape(E, 8, 128, HCH * 128))

    inv_freq = 1.0 / (10000.0 ** (np.arange(0, 64, 2, dtype=np.float64) / 64.0))
    tt = np.arange(T, dtype=np.float64)
    fr = np.outer(tt, inv_freq)
    cosv = np.cos(fr)
    sinv = np.sin(fr)
    p = np.arange(128)
    fi = (p % 64) // 2
    ct = cosv[:, fi].T.astype(np.float32)
    sgn = np.where(p % 2 == 0, -1.0, 1.0)
    st = (sinv[:, fi].T * sgn[:, None]).astype(np.float32)

    i128 = np.eye(128, dtype=np.float32)
    qi = np.arange(128)[:, None]
    ki = np.arange(128)[None, :]
    trineg = np.where(ki <= qi, 0.0, -1e30).astype(np.float32)
    sel8 = np.zeros((8, 1024), np.float32)
    for e_ in range(8):
        sel8[e_, e_ * 128:(e_ + 1) * 128] = 1.0

    shared = {
        "wq_l": wq_l, "wqs_l": wqs_l, "wk_l": wk_l, "wks_l": wks_l,
        "wv_r": wv_r, "wo_l": wo_l, "wg_r": wg_r,
        "w1_l": w1_l, "w2_l": w2_l, "w3_l": w3_l,
        "ct": ct, "st": st, "i128": i128, "trineg": trineg, "sel8": sel8,
    }
    in_maps = [dict(shared, x=np.ascontiguousarray(x[b])) for b in range(8)]
    nc = build_nc()
    res = run_bass_kernel_spmd(nc, in_maps, core_ids=list(range(8)))
    return np.stack([np.asarray(r["out"]) for r in res.results], axis=0)


# version 18 (final)
# speedup vs baseline: 1.1109x; 1.0336x over previous
import contextlib
import numpy as np
import concourse.bass as bass
import concourse.mybir as mybir
from concourse.bass_utils import run_bass_kernel_spmd

F32 = mybir.dt.float32
BF16 = mybir.dt.bfloat16
AF = mybir.ActivationFunctionType
ALU = mybir.AluOpType
AX = mybir.AxisListType

T = 2048
D = 1024
KC = 8
NB = 16
NT = 4
NHP = 8
E = 8
HCH = 22
HHALF = 11
EPS = 1e-5
SCALE = 0.125

DO_ATTN = True
DO_FFN = True

ENGS = ("sync", "gpsimd", "tensor", "vector", "scalar")
ENG_SEM = {"tensor": "tPE", "vector": "tVE", "scalar": "tSC",
           "sync": "tSY", "gpsimd": "tGP"}

AO = 0
QHS = 16384
KHS = 18432
VH = (20480, 22528)
PRB = 24576
ATT = 26624
WQB, WQSB, WKB, WKSB = 28672, 29696, 30720, 31744
WVB = 32768
WOB = (32768, 32768)
HIDB = 0
W1B = (28672, 29696)
W2B = (30720, 31744)
W3B = (24576, 25984)
POOLB_N = 35840
PRB_B = 33792

XT = (0, 1024)
XC = 2048
SQ = 3072
XNF = 4096
DIAG = 2048
RT1, RT2 = 0, 512
WBC = (0, 2048)
SIL = (4096, 4608)
H2S = (5120, 5632)
OT = (0, 1024)
ARF_N = 6144

GRAN = {"pp": 512, "stat": 8}


def BK(b, off=0, w=512):
    return (b * 512 + off, b * 512 + off + w)


class Builder:

    def __init__(self):
        self.ops = {e: [] for e in ENGS}
        import collections
        self.cnt = collections.defaultdict(int)
        self.wr = {}
        self.rd = {}
        self.waited = {e: {} for e in ENGS}

    def _slots(self, rng):
        name, lo, hi = rng
        g = GRAN.get(name, 128)
        return [(name, s) for s in range(lo // g, (hi + g - 1) // g)]

    def emit(self, eng, fns, reads=(), writes=(), dma=None, extra=()):
        if not isinstance(fns, (list, tuple)):
            fns = [fns]
        waits = {}

        def need(cap):
            if cap is None:
                return
            s, v = cap
            if v <= 0:
                return
            waits[s] = max(waits.get(s, 0), v)

        rslots = []
        for rng in reads:
            for sl in self._slots(rng):
                rslots.append(sl)
                need(self.wr.get(sl))
                if sl[0] == "pp":
                    for cap, eng2 in self.rd.get(sl, ()):
                        if eng2 != eng:
                            need(cap)
        wslots = []
        for rng in writes:
            for sl in self._slots(rng):
                wslots.append(sl)
                need(self.wr.get(sl))
                for cap, eng2 in self.rd.get(sl, ()):
                    need(cap)
        for cap in extra:
            need(cap)

        wd = self.waited[eng]
        final = []
        for s, v in sorted(waits.items()):
            if wd.get(s, 0) >= v:
                continue
            wd[s] = v
            final.append((s, v))

        sem = dma if dma is not None else ENG_SEM[eng]
        inc_k = 16 if dma is not None else 1
        self.cnt[sem] += inc_k
        cap = (sem, self.cnt[sem])

        for sl in rslots:
            self.rd.setdefault(sl, []).append((cap, eng))
        for sl in wslots:
            self.wr[sl] = cap
            self.rd[sl] = []

        self.ops[eng].append((fns, tuple(final), sem, inc_k))
        return cap


def build_nc():
    nc = bass.Bass("TRN2", target_bir_lowering=False, debug=False)
    dt = nc.dram_tensor
    x_in = dt("x", [T, D], F32, kind="ExternalInput").ap()
    wq_l = dt("wq_l", [NHP, 128, 1024], F32, kind="ExternalInput").ap()
    wqs_l = dt("wqs_l", [NHP, 128, 1024], F32, kind="ExternalInput").ap()
    wk_l = dt("wk_l", [NHP, 128, 1024], F32, kind="ExternalInput").ap()
    wks_l = dt("wks_l", [NHP, 128, 1024], F32, kind="ExternalInput").ap()
    wv_r = dt("wv_r", [NHP, 128, 1024], F32, kind="ExternalInput").ap()
    wo_l = dt("wo_l", [KC, 128, 1024], F32, kind="ExternalInput").ap()
    wg_r = dt("wg_r", [128, 64], F32, kind="ExternalInput").ap()
    w1_l = dt("w1_l", [E, HCH, 128, 1024], F32, kind="ExternalInput").ap()
    w2_l = dt("w2_l", [E, HCH, 128, 1024], F32, kind="ExternalInput").ap()
    w3_l = dt("w3_l", [E, KC, 128, HCH * 128], F32, kind="ExternalInput").ap()
    ct_in = dt("ct", [128, T], F32, kind="ExternalInput").ap()
    st_in = dt("st", [128, T], F32, kind="ExternalInput").ap()
    i128_in = dt("i128", [128, 128], F32, kind="ExternalInput").ap()
    trineg_in = dt("trineg", [128, 128], F32, kind="ExternalInput").ap()
    sel8_in = dt("sel8", [8, 1024], F32, kind="ExternalInput").ap()
    out = dt("out", [T, D], F32, kind="ExternalOutput").ap()

    with contextlib.ExitStack() as ctx:
        en = ctx.enter_context
        acc = en(nc.sbuf_tensor("acc", [128, KC * T], F32))
        xn = en(nc.sbuf_tensor("xn", [128, KC * T], BF16))
        pb = en(nc.sbuf_tensor("poolb", [128, POOLB_N], BF16))
        arf = en(nc.sbuf_tensor("arf", [128, ARF_N], F32))
        csb = en(nc.sbuf_tensor("csb", [128, 2 * T], BF16))
        wgate = en(nc.sbuf_tensor("wgate", [128, 128], F32))
        wrow = en(nc.sbuf_tensor("wrow", [8, T], BF16))
        i128f = en(nc.sbuf_tensor("i128f", [128, 128], F32))
        i128b = en(nc.sbuf_tensor("i128b", [128, 128], BF16))
        trineg = en(nc.sbuf_tensor("trineg_s", [128, 128], F32))
        sel8 = en(nc.sbuf_tensor("sel8_s", [8, 1024], BF16))
        wgr = en(nc.sbuf_tensor("wgr", [128, 64], F32))
        dgb = en(nc.sbuf_tensor("dgb", [128, 256], BF16))
        stat = en(nc.sbuf_tensor("stat", [128, 64], F32))
        pp = en(nc.psum_tensor("pp", [128, 4096], F32))
        sem_names = ("tPE", "tVE", "tSC", "tSY", "tGP",
                     "cI128F", "cTRI", "cWGR", "cCT", "cST", "cI128B", "cSEL",
                     "dx0", "dx1", "wqS", "wqsS", "wkS", "wksS", "wvS", "woS",
                     "w10", "w11", "w20", "w21", "w30", "w31", "do0", "do1")
        sems = {n: en(nc.semaphore(n)) for n in sem_names}
        block = en(nc.Block())

        B = Builder()
        em = B.emit

        def arfs(off, w):
            return arf[:, off:off + w]

        def pbs(off, w):
            return pb[:, off:off + w]

        def pps(lo_hi):
            lo, hi = lo_hi
            return pp[:, lo:hi]

        em("sync", lambda e: e.dma_start(i128f[:], i128_in),
           writes=[("i128f", 0, 128)], dma="cI128F")
        em("sync", lambda e: e.dma_start(trineg[:], trineg_in),
           writes=[("trineg", 0, 128)], dma="cTRI")
        em("sync", lambda e: e.dma_start(wgr[:], wg_r),
           writes=[("wgr", 0, 64)], dma="cWGR")
        em("gpsimd", lambda e: e.dma_start(csb[:, 0:T], ct_in),
           writes=[("csb", 0, T)], dma="cCT")
        em("gpsimd", lambda e: e.dma_start(csb[:, T:2 * T], st_in),
           writes=[("csb", T, 2 * T)], dma="cST")
        em("gpsimd", lambda e: e.dma_start(i128b[:], i128_in),
           writes=[("i128b", 0, 128)], dma="cI128B")
        em("gpsimd", lambda e: e.dma_start(sel8[:], sel8_in),
           writes=[("sel8", 0, 1024)], dma="cSEL")

        def emit_ln_block(b):
            bf = b % 2
            xt0 = XT[bf]
            sb = bf * 32
            em("sync", (lambda bb, o: lambda e: e.dma_start(
                arfs(o, 1024), x_in[bb * 128:(bb + 1) * 128, :]))(b, xt0),
               writes=[("arf", xt0, xt0 + 1024)], dma="dx%d" % bf)
            em("vector", (lambda o, ss: lambda e: e.reduce_sum(
                stat[:, ss:ss + 1], arfs(o, 1024), axis=AX.X))(xt0, sb),
               reads=[("arf", xt0, xt0 + 1024)], writes=[("stat", sb, sb + 1)])
            em("vector", (lambda ss: lambda e: e.tensor_scalar(
                stat[:, ss + 1:ss + 2], stat[:, ss:ss + 1], -1.0 / D, None, ALU.mult))(sb),
               reads=[("stat", sb, sb + 1)], writes=[("stat", sb + 1, sb + 2)])
            em("vector", (lambda o, ss: lambda e: e.tensor_scalar(
                arfs(XC, 1024), arfs(o, 1024), stat[:, ss + 1:ss + 2], None, ALU.add))(xt0, sb),
               reads=[("arf", xt0, xt0 + 1024), ("stat", sb + 1, sb + 2)],
               writes=[("arf", XC, XC + 1024)])
            em("scalar", (lambda ss: lambda e: e.activation(
                arfs(SQ, 1024), arfs(XC, 1024), AF.Square,
                accum_out=stat[:, ss + 2:ss + 3]))(sb),
               reads=[("arf", XC, XC + 1024)],
               writes=[("arf", SQ, SQ + 1024), ("stat", sb + 2, sb + 3)])
            em("vector", (lambda ss: lambda e: e.tensor_scalar(
                stat[:, ss + 3:ss + 4], stat[:, ss + 2:ss + 3], 1.0 / D, EPS,
                ALU.mult, ALU.add))(sb),
               reads=[("stat", sb + 2, sb + 3)], writes=[("stat", sb + 3, sb + 4)])
            em("scalar", (lambda ss: lambda e: e.activation(
                stat[:, ss + 4:ss + 5], stat[:, ss + 3:ss + 4], AF.Sqrt))(sb),
               reads=[("stat", sb + 3, sb + 4)], writes=[("stat", sb + 4, sb + 5)])
            em("vector", (lambda ss: lambda e: e.reciprocal(
                stat[:, ss + 5:ss + 6], stat[:, ss + 4:ss + 5]))(sb),
               reads=[("stat", sb + 4, sb + 5)], writes=[("stat", sb + 5, sb + 6)])
            em("vector", (lambda ss: lambda e: e.tensor_scalar(
                arfs(XC, 1024), arfs(XC, 1024), stat[:, ss + 5:ss + 6], None, ALU.mult))(sb),
               reads=[("arf", XC, XC + 1024), ("stat", sb + 5, sb + 6)],
               writes=[("arf", XC, XC + 1024)])
            em("tensor", [(lambda kkc: lambda e: e.transpose(
                pp[:, 2048 + kkc * 128:2048 + (kkc + 1) * 128],
                arf[:, XC + kkc * 128:XC + (kkc + 1) * 128], i128f[:]))(kc)
                for kc in range(KC)],
               reads=[("arf", XC, XC + 1024), ("i128f", 0, 128)],
               writes=[("pp", 2048, 3072)])
            em("scalar", (lambda bb: lambda e: e.copy(
                xn[:].rearrange("p (k t) -> p k t", k=KC)[:, :, bb * 128:(bb + 1) * 128],
                pp[:, 2048:3072].rearrange("p (k t) -> p k t", k=KC)))(b),
               reads=[("pp", 2048, 3072)],
               writes=[("xn", kc * T + b * 128, kc * T + (b + 1) * 128) for kc in range(KC)])
            em("vector", lambda e: e.tensor_copy(arfs(XNF, 1024), pp[:, 2048:3072]),
               reads=[("pp", 2048, 3072)], writes=[("arf", XNF, XNF + 1024)])
            em("tensor", [(lambda kkc, fi, la: lambda e: e.matmul(
                pp[:, 2048:2056], arf[:, XNF + kkc * 128:XNF + (kkc + 1) * 128],
                wgr[:, kkc * 8:(kkc + 1) * 8], start=fi, stop=la))(kc, kc == 0, kc == KC - 1)
                for kc in range(KC)],
               reads=[("arf", XNF, XNF + 1024), ("wgr", 0, 64)],
               writes=[("pp", 2048, 2560)])
            em("tensor", [(lambda o, kkc: lambda e: e.transpose(
                pp[:, 3072 + kkc * 128:3072 + (kkc + 1) * 128],
                arf[:, o + kkc * 128:o + (kkc + 1) * 128], i128f[:]))(xt0, kc)
                for kc in range(KC)],
               reads=[("arf", xt0, xt0 + 1024), ("i128f", 0, 128)],
               writes=[("pp", 3072, 4096)])
            em("vector", (lambda bb: lambda e: e.tensor_copy(
                acc[:].rearrange("p (k t) -> p k t", k=KC)[:, :, bb * 128:(bb + 1) * 128],
                pp[:, 3072:4096].rearrange("p (k t) -> p k t", k=KC)))(b),
               reads=[("pp", 3072, 4096)],
               writes=[("acc", kc * T + b * 128, kc * T + (b + 1) * 128) for kc in range(KC)])
            gread = [("pp", 2048, 2560)]
            em("vector", (lambda ss: lambda e: e.reduce_max(
                stat[:, ss + 6:ss + 7], pp[:, 2048:2056], axis=AX.X))(sb),
               reads=gread, writes=[("stat", sb + 6, sb + 7)])
            em("vector", (lambda ss: lambda e: e.tensor_scalar(
                stat[:, ss + 7:ss + 8], stat[:, ss + 6:ss + 7], -1.0, None, ALU.mult))(sb),
               reads=[("stat", sb + 6, sb + 7)], writes=[("stat", sb + 7, sb + 8)])
            em("vector", (lambda ss: lambda e: e.tensor_scalar(
                stat[:, ss + 8:ss + 16], pp[:, 2048:2056], stat[:, ss + 6:ss + 7], None,
                ALU.is_ge))(sb),
               reads=gread + [("stat", sb + 6, sb + 7)], writes=[("stat", sb + 8, sb + 16)])
            em("vector", (lambda ss: lambda e: e.scalar_tensor_tensor(
                stat[:, ss + 16:ss + 24], stat[:, ss + 8:ss + 16], -1e30, pp[:, 2048:2056],
                ALU.mult, ALU.add))(sb),
               reads=gread + [("stat", sb + 8, sb + 16)], writes=[("stat", sb + 16, sb + 24)])
            em("vector", (lambda ss: lambda e: e.reduce_max(
                stat[:, ss + 24:ss + 25], stat[:, ss + 16:ss + 24], axis=AX.X))(sb),
               reads=[("stat", sb + 16, sb + 24)], writes=[("stat", sb + 24, sb + 25)])
            em("scalar", (lambda ss: lambda e: e.activation(
                stat[:, ss + 25:ss + 26], stat[:, ss + 24:ss + 25], AF.Exp,
                bias=stat[:, ss + 7:ss + 8]))(sb),
               reads=[("stat", sb + 24, sb + 25), ("stat", sb + 7, sb + 8)],
               writes=[("stat", sb + 25, sb + 26)])
            em("vector", (lambda ss: lambda e: e.tensor_scalar(
                stat[:, ss + 26:ss + 27], stat[:, ss + 25:ss + 26], 1.0, None, ALU.add))(sb),
               reads=[("stat", sb + 25, sb + 26)], writes=[("stat", sb + 26, sb + 27)])
            em("vector", (lambda ss: lambda e: e.reciprocal(
                stat[:, ss + 27:ss + 28], stat[:, ss + 26:ss + 27]))(sb),
               reads=[("stat", sb + 26, sb + 27)], writes=[("stat", sb + 27, sb + 28)])
            em("scalar", (lambda ss: lambda e: e.activation(
                stat[:, ss + 16:ss + 24], pp[:, 2048:2056], AF.Exp,
                bias=stat[:, ss + 7:ss + 8]))(sb),
               reads=gread + [("stat", sb + 7, sb + 8)], writes=[("stat", sb + 16, sb + 24)])
            em("vector", (lambda ss: lambda e: e.tensor_scalar(
                stat[:, ss + 8:ss + 16], pp[:, 2048:2056], stat[:, ss + 24:ss + 25], None,
                ALU.is_ge))(sb),
               reads=gread + [("stat", sb + 24, sb + 25)], writes=[("stat", sb + 8, sb + 16)])
            em("vector", (lambda ss: lambda e: e.tensor_tensor(
                stat[:, ss + 8:ss + 16], stat[:, ss + 16:ss + 24],
                stat[:, ss + 8:ss + 16], ALU.mult))(sb),
               reads=[("stat", sb + 8, sb + 24)], writes=[("stat", sb + 8, sb + 16)])
            em("vector", (lambda bb, ss: lambda e: e.tensor_scalar(
                wgate[:, bb * 8:(bb + 1) * 8], stat[:, ss + 8:ss + 16],
                stat[:, ss + 27:ss + 28], None, ALU.mult))(b, sb),
               reads=[("stat", sb + 8, sb + 16), ("stat", sb + 27, sb + 28)],
               writes=[("wgate", b * 8, (b + 1) * 8)])

        def emit_hp_loads(hp):
            for buf, warr, wsem in ((WQB, wq_l, "wqS"), (WQSB, wqs_l, "wqsS"),
                                    (WKB, wk_l, "wkS"), (WKSB, wks_l, "wksS"),
                                    (WVB, wv_r, "wvS")):
                em("gpsimd", (lambda bb, ss, h: lambda e: e.dma_start(
                    pbs(bb, 1024), ss[h]))(buf, warr, hp),
                   writes=[("pb", buf, buf + 1024)], dma=wsem)


        def emit_hp_proj(hp, tq_only=None):
            for tq in (range(NT) if tq_only is None else [tq_only]):
                c0, c1 = tq * 512, (tq + 1) * 512
                for bank, wbuf in ((0, WQB), (1, WQSB), (2, WKB), (3, WKSB)):
                    em("tensor", [(lambda bk, wb, kkc, fi, la, a0, a1: lambda e: e.matmul(
                        pps(BK(bk)), pb[:, wb + kkc * 128:wb + (kkc + 1) * 128],
                        xn[:, kkc * T + a0:kkc * T + a1],
                        start=fi, stop=la))(bank, wbuf, kc, kc == 0, kc == KC - 1, c0, c1)
                        for kc in range(KC)],
                       reads=[("pb", wbuf, wbuf + 1024)] +
                             [("xn", kc * T + c0, kc * T + c1) for kc in range(KC)],
                       writes=[("pp",) + BK(bank)])
                em("vector", (lambda a0, a1: lambda e: e.scalar_tensor_tensor(
                    arfs(RT1, 512), pps(BK(0)), SCALE, csb[:, a0:a1],
                    ALU.mult, ALU.mult))(c0, c1),
                   reads=[("pp",) + BK(0), ("csb", c0, c1)],
                   writes=[("arf", RT1, RT1 + 512)])
                em("vector", (lambda a0, a1: lambda e: e.scalar_tensor_tensor(
                    arfs(RT2, 512), pps(BK(1)), SCALE, csb[:, T + a0:T + a1],
                    ALU.mult, ALU.mult))(c0, c1),
                   reads=[("pp",) + BK(1), ("csb", T + c0, T + c1)],
                   writes=[("arf", RT2, RT2 + 512)])
                em("vector", (lambda a0, a1: lambda e: e.tensor_tensor(
                    pb[:, QHS + a0:QHS + a1], arfs(RT1, 512), arfs(RT2, 512),
                    ALU.add))(c0, c1),
                   reads=[("arf", RT1, RT1 + 512), ("arf", RT2, RT2 + 512)],
                   writes=[("pb", QHS + c0, QHS + c1)])
                em("vector", (lambda a0, a1: lambda e: e.tensor_tensor(
                    arfs(RT1, 512), pps(BK(2)), csb[:, a0:a1], ALU.mult))(c0, c1),
                   reads=[("pp",) + BK(2), ("csb", c0, c1)],
                   writes=[("arf", RT1, RT1 + 512)])
                em("vector", (lambda a0, a1: lambda e: e.tensor_tensor(
                    arfs(RT2, 512), pps(BK(3)), csb[:, T + a0:T + a1], ALU.mult))(c0, c1),
                   reads=[("pp",) + BK(3), ("csb", T + c0, T + c1)],
                   writes=[("arf", RT2, RT2 + 512)])
                em("vector", (lambda a0, a1: lambda e: e.tensor_tensor(
                    pb[:, KHS + a0:KHS + a1], arfs(RT1, 512), arfs(RT2, 512),
                    ALU.add))(c0, c1),
                   reads=[("arf", RT1, RT1 + 512), ("arf", RT2, RT2 + 512)],
                   writes=[("pb", KHS + c0, KHS + c1)])


        def emit_hp_vprod(hp):
            vb = VH[hp % 2]
            for g4 in range(4):
                bank = 4 + (g4 % 2)
                fns = []
                for j in range(4):
                    blk = g4 * 4 + j
                    for kc in range(KC):
                        fns.append((lambda bk, jj, bb, kkc, fi, la: lambda e: e.matmul(
                            pp[:, BK(bk)[0] + jj * 128:BK(bk)[0] + (jj + 1) * 128],
                            xn[:, kkc * T + bb * 128:kkc * T + (bb + 1) * 128],
                            pb[:, WVB + kkc * 128:WVB + (kkc + 1) * 128],
                            start=fi, stop=la))(bank, j, blk, kc, kc == 0, kc == KC - 1))
                em("tensor", fns,
                   reads=[("pb", WVB, WVB + 1024)] +
                         [("xn", kc * T + g4 * 512, kc * T + (g4 + 1) * 512) for kc in range(KC)],
                   writes=[("pp",) + BK(bank)])
                em("scalar", (lambda vbb, gg, bk: lambda e: e.copy(
                    pb[:, vbb + gg * 512:vbb + (gg + 1) * 512], pps(BK(bk))))(vb, g4, bank),
                   reads=[("pp",) + BK(bank)],
                   writes=[("pb", vb + g4 * 512, vb + (g4 + 1) * 512)])


        def emit_hp_attn(hp):
            vb = VH[hp % 2]
            def emit_dummies(n):
                for _ in range(n):
                    em("tensor", lambda e: e.matmul(
                        pp[:, 3584:4096], i128b[:], csb[:, 0:512],
                        start=True, stop=True),
                       reads=[("i128b", 0, 128), ("csb", 0, 512)],
                       writes=[("pp", 3584, 4096)])

            def s2_stage(q, a, py):
                prb = PRB if py == 0 else PRB_B
                L = (q + 1) * 128
                ngrp = (q + 4) // 4
                emit_dummies(4)
                for g in range(ngrp):
                    bank = 4 + (g % 2)
                    kb0 = g * 4
                    kbn = min(4, q + 1 - kb0)
                    em("tensor", [(lambda bk, jj, kk, pr, pyy: lambda e: e.matmul(
                        pp[:, BK(bk)[0] + jj * 128:BK(bk)[0] + (jj + 1) * 128],
                        pb[:, pr + kk * 128:pr + (kk + 1) * 128],
                        dgb[:, pyy * 128:(pyy + 1) * 128],
                        start=True, stop=True))(bank, j, kb0 + j, prb, py)
                        for j in range(kbn)],
                       reads=[("pb", prb + kb0 * 128, prb + (kb0 + kbn) * 128),
                              ("dgb", py * 128, (py + 1) * 128)],
                       writes=[("pp",) + BK(bank)])
                    em("vector", (lambda gg, bk, n: lambda e: e.tensor_copy(
                        pb[:, ATT + gg * 512:ATT + gg * 512 + n * 128],
                        pp[:, BK(bk)[0]:BK(bk)[0] + n * 128]))(g, bank, kbn),
                       reads=[("pp",) + BK(bank)],
                       writes=[("pb", ATT + kb0 * 128, ATT + (kb0 + kbn) * 128)])
                reg = 3072 + (q % 2) * 128
                emit_dummies(2)
                fns = []
                for kb in range(q + 1):
                    tp = (0, 64) if a == 1 else None
                    fns.append((lambda ax, kk, rr, fi, la, tpp, vbb: lambda e: e.matmul(
                        pp[64 * ax:64 * ax + 64, rr:rr + 128],
                        pb[:, vbb + kk * 128 + 64 * ax:vbb + kk * 128 + 64 * ax + 64],
                        pb[:, ATT + kk * 128:ATT + (kk + 1) * 128],
                        start=fi, stop=la, tile_position=tpp))(
                            a, kb, reg, kb == 0, kb == q, tp, vb))
                em("tensor", fns,
                   reads=[("pb", vb, vb + (q + 1) * 128),
                          ("pb", ATT, ATT + L)],
                   writes=[("pp", reg, reg + 128)])
                if a == 1:
                    em("scalar", (lambda hh, qq, rr: lambda e: e.copy(
                        pb[:, AO + hh * T + qq * 128:AO + hh * T + (qq + 1) * 128],
                        pp[:, rr:rr + 128]))(hp, q, reg),
                       reads=[("pp", reg, reg + 128)],
                       writes=[("pb", AO + hp * T + q * 128, AO + hp * T + (q + 1) * 128)])

            prev = None
            for qb in range(NB):
                for a in range(2):
                    u = qb * 2 + a
                    py = u % 2
                    st0 = 16 * py
                    prb = PRB if py == 0 else PRB_B
                    L = (qb + 1) * 128
                    nch = (L + 511) // 512
                    sb0 = 1024 if (a == 1 and qb < 8) else 0
                    for c in range(nch):
                        w = min(512, L - c * 512)
                        em("tensor", (lambda aa, qq, cc, ww, b0: lambda e: e.matmul(
                            pp[:, b0 + cc * 512:b0 + cc * 512 + ww],
                            pb[64 * aa:64 * aa + 64, QHS + qq * 128:QHS + (qq + 1) * 128],
                            pb[64 * aa:64 * aa + 64, KHS + cc * 512:KHS + cc * 512 + ww],
                            start=True, stop=True))(a, qb, c, w, sb0),
                           reads=[("pb", QHS + qb * 128, QHS + (qb + 1) * 128),
                                  ("pb", KHS + c * 512, KHS + c * 512 + w)],
                           writes=[("pp", sb0 + c * 512, sb0 + c * 512 + w)])
                    em("vector", (lambda LL, b0, s0: lambda e: e.reduce_max(
                        stat[:, s0:s0 + 1], pp[:, b0:b0 + LL], axis=AX.X,
                        negate=True))(L, sb0, st0),
                       reads=[("pp", sb0, sb0 + L)], writes=[("stat", st0, st0 + 1)])
                    em("vector", (lambda qq, b0: lambda e: e.tensor_tensor(
                        pp[:, b0 + qq * 128:b0 + (qq + 1) * 128],
                        pp[:, b0 + qq * 128:b0 + (qq + 1) * 128],
                        trineg[:], ALU.add))(qb, sb0),
                       reads=[("pp", sb0 + qb * 128, sb0 + (qb + 1) * 128), ("trineg", 0, 128)],
                       writes=[("pp", sb0 + qb * 128, sb0 + (qb + 1) * 128)])
                    em("scalar", (lambda qq, b0, s0, pr: lambda e: e.activation(
                        pb[:, pr:pr + (qq + 1) * 128], pp[:, b0:b0 + (qq + 1) * 128], AF.Exp,
                        bias=stat[:, s0:s0 + 1], scale=1.0,
                        accum_out=stat[:, s0 + 2:s0 + 3]))(qb, sb0, st0, prb),
                       reads=[("pp", sb0, sb0 + (qb + 1) * 128), ("stat", st0, st0 + 1)],
                       writes=[("pb", prb, prb + (qb + 1) * 128), ("stat", st0 + 2, st0 + 3)])
                    if prev is not None:
                        s2_stage(*prev)
                    prev = (qb, a, py)
                    em("vector", (lambda s0: lambda e: e.reciprocal(
                        stat[:, s0 + 5:s0 + 6], stat[:, s0 + 2:s0 + 3]))(st0),
                       reads=[("stat", st0 + 2, st0 + 3)], writes=[("stat", st0 + 5, st0 + 6)])
                    em("vector", (lambda s0, pyy: lambda e: e.tensor_scalar(
                        dgb[:, pyy * 128:(pyy + 1) * 128], i128b[:],
                        stat[:, s0 + 5:s0 + 6], None, ALU.mult))(st0, py),
                       reads=[("i128b", 0, 128), ("stat", st0 + 5, st0 + 6)],
                       writes=[("dgb", py * 128, (py + 1) * 128)])
            if prev is not None:
                s2_stage(*prev)
                prev = None


        emit_hp_loads(0)
        for tq4 in range(NT):
            for b in range(tq4 * 4, tq4 * 4 + 4):
                emit_ln_block(b)
            emit_hp_proj(0, tq4)
        for g in range(4):
            em("tensor", [(lambda bb, jj: lambda e: e.transpose(
                pp[0:8, 3584 + jj * 128:3584 + (jj + 1) * 128],
                wgate[:, bb * 8:(bb + 1) * 8], i128f[:]))(g * 4 + j, j)
                for j in range(4)],
               reads=[("wgate", g * 32, (g + 1) * 32), ("i128f", 0, 128)],
               writes=[("pp", 3584, 4096)])
            em("vector", (lambda gg: lambda e: e.tensor_copy(
                wrow[0:8, gg * 512:(gg + 1) * 512], pp[0:8, 3584:4096]))(g),
               reads=[("pp", 3584, 4096)], writes=[("wrow", g * 512, (g + 1) * 512)])
        emit_hp_vprod(0)
        for hp in range(NHP if DO_ATTN else 0):
            emit_hp_attn(hp)
            if hp + 1 < NHP:
                emit_hp_loads(hp + 1)
                emit_hp_proj(hp + 1)
                emit_hp_vprod(hp + 1)

        for mc in range(KC if DO_ATTN else 0):
            em("gpsimd", (lambda m: lambda e: e.dma_start(
                pbs(WOB[m % 2], 1024), wo_l[m]))(mc),
               writes=[("pb", WOB[mc % 2], WOB[mc % 2] + 1024)], dma="woS")
            for t in range(NT):
                bank = 6 + ((mc * NT + t) % 2)
                em("tensor", [(lambda m, tt, bk, kkc, fi, la: lambda e: e.matmul(
                    pps(BK(bk)),
                    pb[:, WOB[m % 2] + kkc * 128:WOB[m % 2] + (kkc + 1) * 128],
                    pb[:, AO + kkc * T + tt * 512:AO + kkc * T + (tt + 1) * 512],
                    start=fi, stop=la))(mc, t, bank, kc, kc == 0, kc == KC - 1)
                    for kc in range(KC)],
                   reads=[("pb", WOB[mc % 2], WOB[mc % 2] + 1024)] +
                         [("pb", AO + kc * T + t * 512, AO + kc * T + (t + 1) * 512)
                          for kc in range(KC)],
                   writes=[("pp",) + BK(bank)])
                em("vector", (lambda m, tt, bk: lambda e: e.tensor_tensor(
                    acc[:, m * T + tt * 512:m * T + (tt + 1) * 512],
                    acc[:, m * T + tt * 512:m * T + (tt + 1) * 512],
                    pps(BK(bk)), ALU.add))(mc, t, bank),
                   reads=[("acc", mc * T + t * 512, mc * T + (t + 1) * 512),
                          ("pp",) + BK(bank)],
                   writes=[("acc", mc * T + t * 512, mc * T + (t + 1) * 512)])

        final_caps = []

        def emit_o_block(b):
            base = (b % 2) * 1024
            ot = OT[b % 2]
            em("tensor", [(lambda bb, kkc, bs: lambda e: e.transpose(
                pp[:, bs + kkc * 128:bs + (kkc + 1) * 128],
                acc[:, kkc * T + bb * 128:kkc * T + (bb + 1) * 128], i128f[:]))(b, kc, base)
                for kc in range(KC)],
               reads=[("acc", kc * T + b * 128, kc * T + (b + 1) * 128) for kc in range(KC)] +
                     [("i128f", 0, 128)],
               writes=[("pp", base, base + 1024)])
            em("vector", (lambda bs, o: lambda e: e.tensor_copy(
                arfs(o, 1024), pp[:, bs:bs + 1024]))(base, ot),
               reads=[("pp", base, base + 1024)],
               writes=[("arf", ot, ot + 1024)])
            final_caps.append(em("sync", (lambda bb, o: lambda e: e.dma_start(
                out[bb * 128:(bb + 1) * 128, :], arfs(o, 1024)))(b, ot),
               reads=[("arf", ot, ot + 1024)], dma="do%d" % (b % 2)))

        o_done = set()
        for e_i in range(E if DO_FFN else 0):
            eb = e_i % 2
            for t in range(NT):
                em("tensor", (lambda ee, tt: lambda e: e.matmul(
                    pps(BK(6)), sel8[:, ee * 128:(ee + 1) * 128],
                    wrow[0:8, tt * 512:(tt + 1) * 512],
                    start=True, stop=True))(e_i, t),
                   reads=[("sel8", e_i * 128, (e_i + 1) * 128),
                          ("wrow", t * 512, (t + 1) * 512)],
                   writes=[("pp",) + BK(6)])
                em("scalar", (lambda ebb, tt: lambda e: e.copy(
                    arfs(WBC[ebb] + tt * 512, 512), pps(BK(6))))(eb, t),
                   reads=[("pp",) + BK(6)],
                   writes=[("arf", WBC[eb] + t * 512, WBC[eb] + (t + 1) * 512)])

            for half in range(2):
                for c in range(half * HHALF, (half + 1) * HHALF):
                    cl = c - half * HHALF
                    cb = c % 2
                    b1 = 0 if cb == 0 else 2
                    b2 = 1 if cb == 0 else 3
                    em("gpsimd", (lambda ee, cc: lambda e: e.dma_start(
                        pbs(W1B[cc % 2], 1024), w1_l[ee, cc]))(e_i, c),
                       writes=[("pb", W1B[cb], W1B[cb] + 1024)], dma="w1%d" % cb)
                    em("gpsimd", (lambda ee, cc: lambda e: e.dma_start(
                        pbs(W2B[cc % 2], 1024), w2_l[ee, cc]))(e_i, c),
                       writes=[("pb", W2B[cb], W2B[cb] + 1024)], dma="w2%d" % cb)
                    for t in range(NT):
                        tb = t % 2
                        em("tensor", [(lambda cc, tt, bk, kkc, fi, la: lambda e: e.matmul(
                            pps(BK(bk)),
                            pb[:, W1B[cc % 2] + kkc * 128:W1B[cc % 2] + (kkc + 1) * 128],
                            xn[:, kkc * T + tt * 512:kkc * T + (tt + 1) * 512],
                            start=fi, stop=la))(c, t, b1, kc, kc == 0, kc == KC - 1)
                            for kc in range(KC)],
                           reads=[("pb", W1B[cb], W1B[cb] + 1024)] +
                                 [("xn", kc * T + t * 512, kc * T + (t + 1) * 512)
                                  for kc in range(KC)],
                           writes=[("pp",) + BK(b1)])
                        em("tensor", [(lambda cc, tt, bk, kkc, fi, la: lambda e: e.matmul(
                            pps(BK(bk)),
                            pb[:, W2B[cc % 2] + kkc * 128:W2B[cc % 2] + (kkc + 1) * 128],
                            xn[:, kkc * T + tt * 512:kkc * T + (tt + 1) * 512],
                            start=fi, stop=la))(c, t, b2, kc, kc == 0, kc == KC - 1)
                            for kc in range(KC)],
                           reads=[("pb", W2B[cb], W2B[cb] + 1024)] +
                                 [("xn", kc * T + t * 512, kc * T + (t + 1) * 512)
                                  for kc in range(KC)],
                           writes=[("pp",) + BK(b2)])
                        em("scalar", (lambda bk, tbb: lambda e: e.activation(
                            arfs(SIL[tbb], 512), pps(BK(bk)), AF.Silu))(b1, tb),
                           reads=[("pp",) + BK(b1)],
                           writes=[("arf", SIL[tb], SIL[tb] + 512)])
                        em("vector", (lambda bk, tbb, ebb, tt: lambda e: e.tensor_tensor(
                            arfs(H2S[tbb], 512), pps(BK(bk)),
                            arfs(WBC[ebb] + tt * 512, 512), ALU.mult))(b2, tb, eb, t),
                           reads=[("pp",) + BK(b2),
                                  ("arf", WBC[eb] + t * 512, WBC[eb] + (t + 1) * 512)],
                           writes=[("arf", H2S[tb], H2S[tb] + 512)])
                        em("vector", (lambda cll, tt, tbb: lambda e: e.tensor_tensor(
                            pb[:, HIDB + cll * T + tt * 512:HIDB + cll * T + (tt + 1) * 512],
                            arfs(SIL[tbb], 512), arfs(H2S[tbb], 512), ALU.mult))(cl, t, tb),
                           reads=[("arf", SIL[tb], SIL[tb] + 512),
                                  ("arf", H2S[tb], H2S[tb] + 512)],
                           writes=[("pb", HIDB + cl * T + t * 512,
                                    HIDB + cl * T + (t + 1) * 512)])
                for m in range(KC):
                    mb = m % 2
                    em("gpsimd", (lambda ee, hf, mm: lambda e: e.dma_start(
                        pbs(W3B[mm % 2], HHALF * 128),
                        w3_l[ee, mm, :, hf * HHALF * 128:(hf + 1) * HHALF * 128]))(e_i, half, m),
                       writes=[("pb", W3B[mb], W3B[mb] + HHALF * 128)], dma="w3%d" % mb)
                    for t in range(NT):
                        bank = 4 + ((m * NT + t) % 2)
                        em("tensor", [(lambda mm, tt, bk, cll, fi, la: lambda e: e.matmul(
                            pps(BK(bk)),
                            pb[:, W3B[mm % 2] + cll * 128:W3B[mm % 2] + (cll + 1) * 128],
                            pb[:, HIDB + cll * T + tt * 512:HIDB + cll * T + (tt + 1) * 512],
                            start=fi, stop=la))(m, t, bank, cl, cl == 0, cl == HHALF - 1)
                            for cl in range(HHALF)],
                           reads=[("pb", W3B[mb], W3B[mb] + HHALF * 128)] +
                                 [("pb", HIDB + cl * T + t * 512,
                                   HIDB + cl * T + (t + 1) * 512) for cl in range(HHALF)],
                           writes=[("pp",) + BK(bank)])
                        em("vector", (lambda mm, tt, bk: lambda e: e.tensor_tensor(
                            acc[:, mm * T + tt * 512:mm * T + (tt + 1) * 512],
                            acc[:, mm * T + tt * 512:mm * T + (tt + 1) * 512],
                            pps(BK(bk)), ALU.add))(m, t, bank),
                           reads=[("acc", m * T + t * 512, m * T + (t + 1) * 512),
                                  ("pp",) + BK(bank)],
                           writes=[("acc", m * T + t * 512, m * T + (t + 1) * 512)])
                        if (e_i == E - 1 and half == 1 and m == KC - 1):
                            for bb in range(t * 4, t * 4 + 4):
                                if bb not in o_done:
                                    o_done.add(bb)
                                    emit_o_block(bb)

        o_rest = [b for b in range(NB) if b not in o_done]
        for b in o_rest:
            base = (b % 2) * 1024
            ot = OT[b % 2]
            em("tensor", [(lambda bb, kkc, bs: lambda e: e.transpose(
                pp[:, bs + kkc * 128:bs + (kkc + 1) * 128],
                acc[:, kkc * T + bb * 128:kkc * T + (bb + 1) * 128], i128f[:]))(b, kc, base)
                for kc in range(KC)],
               reads=[("acc", kc * T + b * 128, kc * T + (b + 1) * 128) for kc in range(KC)] +
                     [("i128f", 0, 128)],
               writes=[("pp", base, base + 1024)])
            em("vector", (lambda bs, o: lambda e: e.tensor_copy(
                arfs(o, 1024), pp[:, bs:bs + 1024]))(base, ot),
               reads=[("pp", base, base + 1024)],
               writes=[("arf", ot, ot + 1024)])
            final_caps.append(em("sync", (lambda bb, o: lambda e: e.dma_start(
                out[bb * 128:(bb + 1) * 128, :], arfs(o, 1024)))(b, ot),
               reads=[("arf", ot, ot + 1024)], dma="do%d" % (b % 2)))

        def play(e, name):
            for (fns, waits, sem, inc_k) in B.ops[name]:
                for (s, v) in waits:
                    e.wait_ge(sems[s], v)
                r = None
                for fn in fns:
                    r = fn(e)
                r.then_inc(sems[sem], inc_k)
            if name == "sync":
                e.wait_ge(sems["do0"], B.cnt["do0"])
                e.wait_ge(sems["do1"], B.cnt["do1"])

        @block.sync
        def _(e):
            play(e, "sync")

        @block.gpsimd
        def _(e):
            play(e, "gpsimd")

        @block.tensor
        def _(e):
            play(e, "tensor")

        @block.vector
        def _(e):
            play(e, "vector")

        @block.scalar
        def _(e):
            play(e, "scalar")

    return nc


def _lhsT_tiles(w):
    return np.ascontiguousarray(
        w.reshape(8, 128, 8, 128).transpose(0, 3, 2, 1).reshape(8, 128, 1024))


def _swap_rows(w):
    return np.ascontiguousarray(w.reshape(-1, 2, w.shape[-1])[:, ::-1, :].reshape(w.shape))


def kernel(L0_attn_x, L0_attn_Wq, L0_attn_Wk, L0_attn_Wv, L0_attn_Wo,
           L0_ffn_Wg, L0_ffn_W1, L0_ffn_W2, L0_ffn_W3):
    x = np.ascontiguousarray(np.asarray(L0_attn_x, dtype=np.float32))
    Wq = np.asarray(L0_attn_Wq, dtype=np.float32)
    Wk = np.asarray(L0_attn_Wk, dtype=np.float32)
    Wv = np.asarray(L0_attn_Wv, dtype=np.float32)
    Wo = np.asarray(L0_attn_Wo, dtype=np.float32)
    Wg = np.asarray(L0_ffn_Wg, dtype=np.float32)
    W1 = np.asarray(L0_ffn_W1, dtype=np.float32)
    W2 = np.asarray(L0_ffn_W2, dtype=np.float32)
    W3 = np.asarray(L0_ffn_W3, dtype=np.float32)

    wq_l = _lhsT_tiles(Wq)
    wqs_l = _lhsT_tiles(_swap_rows(Wq))
    wk_l = _lhsT_tiles(Wk)
    wks_l = _lhsT_tiles(_swap_rows(Wk))
    wv_r = _lhsT_tiles(Wv)
    wo_l = _lhsT_tiles(Wo)
    wg_r = np.ascontiguousarray(Wg.reshape(8, 8, 128).transpose(2, 1, 0).reshape(128, 64))

    HPAD = HCH * 128
    W1p = np.zeros((E, HPAD, D), np.float32)
    W1p[:, :2730] = W1
    W2p = np.zeros((E, HPAD, D), np.float32)
    W2p[:, :2730] = W2
    W3p = np.zeros((E, D, HPAD), np.float32)
    W3p[:, :, :2730] = W3
    w1_l = np.ascontiguousarray(
        W1p.reshape(E, HCH, 128, 8, 128).transpose(0, 1, 4, 3, 2).reshape(E, HCH, 128, 1024))
    w2_l = np.ascontiguousarray(
        W2p.reshape(E, HCH, 128, 8, 128).transpose(0, 1, 4, 3, 2).reshape(E, HCH, 128, 1024))
    w3_l = np.ascontiguousarray(
        W3p.reshape(E, 8, 128, HCH, 128).transpose(0, 1, 4, 3, 2).reshape(E, 8, 128, HCH * 128))

    inv_freq = 1.0 / (10000.0 ** (np.arange(0, 64, 2, dtype=np.float64) / 64.0))
    tt = np.arange(T, dtype=np.float64)
    fr = np.outer(tt, inv_freq)
    cosv = np.cos(fr)
    sinv = np.sin(fr)
    p = np.arange(128)
    fi = (p % 64) // 2
    ct = cosv[:, fi].T.astype(np.float32)
    sgn = np.where(p % 2 == 0, -1.0, 1.0)
    st = (sinv[:, fi].T * sgn[:, None]).astype(np.float32)

    i128 = np.eye(128, dtype=np.float32)
    qi = np.arange(128)[:, None]
    ki = np.arange(128)[None, :]
    trineg = np.where(ki <= qi, 0.0, -1e30).astype(np.float32)
    sel8 = np.zeros((8, 1024), np.float32)
    for e_ in range(8):
        sel8[e_, e_ * 128:(e_ + 1) * 128] = 1.0

    shared = {
        "wq_l": wq_l, "wqs_l": wqs_l, "wk_l": wk_l, "wks_l": wks_l,
        "wv_r": wv_r, "wo_l": wo_l, "wg_r": wg_r,
        "w1_l": w1_l, "w2_l": w2_l, "w3_l": w3_l,
        "ct": ct, "st": st, "i128": i128, "trineg": trineg, "sel8": sel8,
    }
    in_maps = [dict(shared, x=np.ascontiguousarray(x[b])) for b in range(8)]
    nc = build_nc()
    res = run_bass_kernel_spmd(nc, in_maps, core_ids=list(range(8)))
    return np.stack([np.asarray(r["out"]) for r in res.results], axis=0)
